# version 21
# baseline (speedup 1.0000x reference)
import os
import numpy as np
import concourse.bass as bass
import concourse.mybir as mybir
from concourse.bass_utils import run_bass_kernel_spmd

F32 = mybir.dt.float32
F32R = mybir.dt.float32r
BF16 = mybir.dt.bfloat16
OP = mybir.AluOpType
AF = mybir.ActivationFunctionType
AX = mybir.AxisListType.X

SIM_FP64 = os.environ.get("BASS_SIM_FP64") == "1"
MAGIC = 12582912.0

L, D, H, NW, INNER = 2048, 1024, 16, 3, 4096
NT, ND, NI = L // 128, D // 128, INNER // 128
EPS = 1e-5
NEGBIG = -1.0e30

TAP = "final"


class Buf:
    __slots__ = ("name", "writer", "readers", "is_psum")
    def __init__(self, name):
        self.name, self.writer, self.readers = name, None, []
        self.is_psum = name in ("pA1L", "pA1R", "pA2L", "pC", "pBL", "pBR", "pPVL", "pPVR")


class MultiBuf:
    __slots__ = ("parts",)
    def __init__(self, parts):
        self.parts = list(parts)


def _expand(bs):
    out = []
    for b in bs:
        if isinstance(b, MultiBuf):
            out.extend(b.parts)
        else:
            out.append(b)
    return out


class Prog:
    ENGINES = ("tensor", "vector", "scalar", "gpsimd", "qsync", "qgp")

    def __init__(self):
        self.steps = {e: [] for e in self.ENGINES}
        self.count = {e: 0 for e in self.ENGINES}
        self.waited = {}
        self.chains = {}
        self.seq = 0

    def _chain(self, c):
        if c not in self.chains:
            self.chains[c] = {"total": 0, "open": False, "last_closed": 0}
        return self.chains[c]

    def _dep_value(self, dep):
        kind, key, v = dep
        if kind == "e":
            return (key, v)
        ch = self._chain(key)
        if v > ch["last_closed"]:
            ch["last_closed"] = ch["total"]
            ch["open"] = False
        return ("c:" + key, 16 * ch["last_closed"])

    def step(self, eng, fns, reads=(), writes=()):
        if not isinstance(fns, (list, tuple)):
            fns = [fns]
        for fn in fns:
            self._step1(eng, fn, reads, writes)

    def _step1(self, eng, fn, reads, writes):
        reads, writes = _expand(reads), _expand(writes)
        deps = {}
        def add(dep, waw_psum=False):
            if waw_psum and dep[0] == "e" and dep[1] == eng:
                return
            k, v = self._dep_value(dep)
            if v > deps.get(k, 0):
                deps[k] = v
        for b in reads:
            if b.writer:
                add(b.writer)
        for b in writes:
            if b.writer:
                add(b.writer, waw_psum=b.is_psum)
            for r in b.readers:
                add(r)
        waits = []
        for k, v in sorted(deps.items()):
            prev = self.waited.get((eng, k), 0)
            if v > prev:
                waits.append((k, v))
                self.waited[(eng, k)] = v
        self.count[eng] += 1
        after = ("e", eng, self.count[eng])
        for b in reads:
            b.readers.append(after)
        for b in writes:
            b.writer = after
            b.readers = []
        self.seq += 1
        self.steps[eng].append((waits, [fn], None, self.seq))

    def dma(self, queue, chain, fn, reads=(), writes=()):
        reads, writes = _expand(reads), _expand(writes)
        ch = self._chain(chain)
        deps = {}
        def add(dep):
            k, v = self._dep_value(dep)
            if v > deps.get(k, 0):
                deps[k] = v
        for b in reads:
            if b.writer:
                add(b.writer)
        for b in writes:
            if b.writer:
                add(b.writer)
            for r in b.readers:
                add(r)
        if not ch["open"]:
            if ch["last_closed"] > 0:
                k = "c:" + chain
                if ch["last_closed"] * 16 > deps.get(k, 0):
                    deps[k] = ch["last_closed"] * 16
            ch["open"] = True
        waits = []
        for k, v in sorted(deps.items()):
            prev = self.waited.get((queue, k), 0)
            if v > prev:
                waits.append((k, v))
                self.waited[(queue, k)] = v
        ch["total"] += 1
        self.count[queue] += 1
        after = ("d", chain, ch["total"])
        for b in reads:
            b.readers.append(after)
        for b in writes:
            b.writer = after
            b.readers = []
        self.seq += 1
        self.steps[queue].append((waits, [fn], chain, self.seq))


def build_nc():
    nc = bass.Bass("TRN2", target_bir_lowering=False)
    pr = Prog()
    bufs = {}
    def B(name):
        if name not in bufs:
            bufs[name] = Buf(name)
        return bufs[name]

    dr = {}
    def din(name, shape, dtype=F32):
        dr[name] = nc.dram_tensor(name, list(shape), dtype, kind="ExternalInput").ap()
    din("x", (L, D)); din("c", (1, D))
    din("Wq", (NW, D, D)); din("bq", (NW, D)); din("Wk", (NW, D, D)); din("bk", (NW, D))
    din("Wv", (NW, D, D)); din("bv", (NW, D)); din("Wo", (NW, D, D)); din("bo", (NW, D))
    din("lng", (NW, D)); din("lnb", (NW, D))
    din("Wout", (D, NW * D)); din("bout", (1, D)); din("ln2g", (1, D)); din("ln2b", (1, D))
    din("wg", (INNER, D)); din("wu", (INNER, D)); din("wd", (D, INNER))
    din("wmod", (D, 6 * D))
    din("d_ident_b", (128, 128), BF16); din("d_lmask", (128, 128), BF16); din("d_umask", (128, 128), BF16)
    yout = nc.dram_tensor("y", [L, D], F32, kind="ExternalOutput").ap()
    z_hbm = nc.dram_tensor("z_scr", [D, L], F32, kind="Internal").ap()
    wgT_hbm = nc.dram_tensor("wgT_scr", [D, INNER], BF16, kind="Internal").ap()
    wuT_hbm = nc.dram_tensor("wuT_scr", [D, INNER], BF16, kind="Internal").ap()
    wdT_hbm = nc.dram_tensor("wdT_scr", [INNER, D], BF16, kind="Internal").ap()

    sb = {}
    def S(name, shape, dtype=F32, offset=None):
        if offset is None:
            t = nc.alloc_sbuf_tensor(name, list(shape), dtype)
        else:
            t = nc.alloc_sbuf_tensor_at(name, list(shape), dtype, offset=offset)
        sb[name] = t
        return t

    S("ident_b", (128, 128), BF16); S("lmask", (128, 128), BF16); S("umask", (128, 128), BF16)
    S("ones_col_b", (128, 1), BF16)
    S("ones_row_r", (1, 128), F32R)
    S("onesrow_f", (1, 128), F32)
    S("eps_col", (128, 1), F32); S("eps6_col", (128, 1), F32)
    S("c_col", (128, 8), F32); S("csil_f", (128, 8), F32); S("csil_r", (128, 8), F32R)
    S("mod_col", (128, 48), F32)
    S("modx_col", (128, 48), F32)
    S("bias_col", (128, 32), F32)
    S("bk_stage", (128, 8), F32)
    S("bv_col_b", (128, 8), BF16)
    S("lnw_col", (128, 16), F32)
    S("gcol_b", (128, 16), BF16)
    S("ln2_col", (128, 16), F32)
    S("sm", (128, 64), F32)
    S("zp", (128, 64), F32)
    S("mfin", (128, 16), F32)
    S("aug_f", (128, 16), F32)
    S("aughl_b", (128, 32), BF16)
    S("auglo_f", (128, 16), F32)
    S("mrow_b", (32, 128), BF16)
    S("r16a", (16, 128), F32); S("r16b", (16, 128), F32); S("r16c", (16, 128), F32)
    S("r16r", (16, 128), F32R); S("r16r2", (16, 128), F32R)
    S("srow_f", (128, 2048), F32)
    S("srow_b", (128, 2048), BF16)
    S("Abc_r", (2, 128), F32R)
    S("Awgwb_r", (2, 1024), F32R)
    S("Br_r", (2, 2048), F32R)
    S("Bmu_r", (2, 2048), F32R)
    S("S1_b", (16, 2048), BF16)
    S("WoBv_b", (16, 1024), BF16)
    S("stats_scrap", (128, 1024), F32)
    S("scrap_b", (128, 1024), BF16)
    S("qscl", (128, 16), F32)
    S("ones_col_f", (128, 1), F32)
    S("swcols", (128, 8), F32)
    S("swr", (1, 4), F32R)
    S("onesrow_b", (1, 128), BF16)
    S("S1f", (16, 2048), F32)

    BASE = nc.SBUF_PARTITION_SIZE_BYTES - nc.SBUF_PARTITION_SIZE_BYTES
    BASE = ((nc._sbuf_addr_for_side("left") + 63) // 64) * 64 + 64
    off = [BASE]
    def R(name, shape, dtype, n=1, at=None):
        if at is not None:
            off[0] = at
        ts = []
        per = int(np.prod(shape[1:])) * mybir.dt.size(dtype)
        for i in range(n):
            ts.append(S(f"{name}{i}" if n > 1 else name, shape, dtype, offset=off[0]))
            off[0] += (per + 31) // 32 * 32
        return ts

    h1 = R("h1_", (128, L), BF16, ND)
    Ofm = R("Ofm", (128, L), BF16, ND)
    qkv_base = off[0]
    Qt = R("Qt", (128, L), BF16, 4)
    Kt = R("Kt", (128, L), BF16, 4)
    qkv_end = off[0]
    yfm = R("yfm", (128, L), BF16, ND, at=qkv_base)
    off[0] = qkv_end
    vp_base = off[0]
    Vp = R("Vp", (128, NT * 130), BF16, 2)
    stg_f = R("stg_f", (128, 1024), F32, 2)
    stgb_base = off[0]
    stg_b = R("stg_b", (128, 1024), BF16, 2)
    stg_r = R("stg_r", (128, 512), F32R, 2, at=stgb_base)
    off[0] = stgb_base + 4 * 1024
    wt_base = off[0]
    wtq = R("wtq", (128, 1024), BF16, 2)
    wtk = R("wtk", (128, 1024), BF16, 2)
    wtv = R("wtv", (128, 1024), BF16, 2)
    wot_base = off[0]
    wot = R("wot", (128, 1024), BF16, ND)
    ptb = R("ptb", (128, 1024), BF16, 2)
    attn_end = off[0]
    assert attn_end <= 228 * 1024, attn_end

    h2qfm = R("h2qfm", (128, L), BF16, ND, at=BASE)
    prod = R("prod", (128, INNER), BF16, 4, at=BASE + 32 * 1024)
    wdT = R("wdT", (128, 1024), BF16, 16, at=qkv_base)
    wgTs = R("wgTs", (128, 512), BF16, 8, at=wot_base)
    wuTs = R("wuTs", (128, 512), BF16, 8)
    x1t = R("x1t", (128, 1024), F32, 1, at=vp_base)[0]
    x2t = R("x2t", (128, 1024), F32)[0]
    assert off[0] <= vp_base + 8 * 1024 + 512
    gbt = R("gbt", (128, 1024), F32, 1, at=wt_base)[0]
    cb1t = R("cb1t", (128, 1024), F32)[0]
    sbt = R("sbt", (128, 1024), F32)[0]
    assert off[0] <= wot_base

    pA1 = nc.alloc_psum_tensor("pA1", [128, 1024], F32)
    pA2 = nc.alloc_psum_tensor("pA2", [128, 1024], F32)
    pB = nc.alloc_psum_tensor("pB", [128, 1024], F32)
    pPV = nc.alloc_psum_tensor("pPV", [65, 1024], F32)
    pBb = pB[:].bitcast(BF16)
    pA2b = pA2[:].bitcast(BF16)

    b_pA1L, b_pA1R, b_pA2L, b_pC = B("pA1L"), B("pA1R"), B("pA2L"), B("pC")
    b_pBL, b_pBR = B("pBL"), B("pBR")
    b_pPVL, b_pPVR = B("pPVL"), B("pPVR")
    b_pA = [MultiBuf([b_pA1L, b_pA1R]), MultiBuf([b_pA2L, b_pC])]
    b_pB = MultiBuf([b_pBL, b_pBR])
    b_pPV = MultiBuf([b_pPVL, b_pPVR])
    b_const = B("consts")
    b_stg = [B("stg_f0"), B("stg_f1")]
    b_stgb = [B("stg_b0"), B("stg_b1")]
    b_stgr = b_stgb
    b_h1 = [B(f"h1_{j}") for j in range(ND)]
    b_O = [B(f"O_{j}") for j in range(ND)]
    b_Qt = [B(f"Qt{i}") for i in range(4)]
    b_Kt = [B(f"Kt{i}") for i in range(4)]
    b_Vp = [B("Vp0"), B("Vp1")]
    b_yfm = [B(f"yfm{j}") for j in range(ND)]
    b_wt = {m: [B(f"wt{m}0"), B(f"wt{m}1")] for m in ("q", "k", "v")}
    b_wot = [B(f"wot{j}") for j in range(ND)]
    b_ptb = [B("ptb0"), B("ptb1")]
    b_zp, b_mfin = B("zp"), B("mfin")
    b_bias, b_lnw = B("bias_col"), B("lnw")
    b_S1, b_WoBv = B("S1"), B("WoBv")
    b_rowf = [B(f"rowf{i}") for i in range(4)]
    b_rowb = [B(f"rowb{i}") for i in range(4)]
    b_Abc, b_Awgwb, b_Br, b_Bmu = B("Abc"), B("Awgwb"), B("Br"), B("Bmu")
    b_r16 = {k: B("r16" + k) for k in "abc"}
    b_r16r = [B("r16r"), B("r16r2")]
    b_modcol, b_modx = B("mod_col"), B("modx")
    b_zhbm = [B(f"zhbm{j}") for j in range(ND)]
    b_yout = [B(f"yout{i}") for i in range(NT)]
    b_scrap, b_scrapb = B("scrap"), B("scrapb")
    b_ln1 = B("ln1")
    b_ln1r = [B("ln1"), B("ln1B")]
    b_qscl = B("qscl")
    b_h2qfm = [B(f"h2qfm{j}") for j in range(ND)]
    b_prod = [B(f"prod{i}") for i in range(4)]
    b_wdT = [B(f"wdT{k}") for k in range(16)]
    b_wgTs = [B(f"wgTs{i}") for i in range(8)]
    b_wuTs = [B(f"wuTs{i}") for i in range(8)]
    b_x1t = [B("x1t0"), B("x1t1")]
    VP_AL = [b_Vp[0], b_Vp[1]]
    class _Misc:
        def __getitem__(self, k):
            return B("misc_" + k)
    b_misc = _Misc()

    def yfm_writes(j):
        return [b_yfm[j], (b_Qt[j] if j < 4 else b_Kt[j - 4])]
    def h2qfm_writes(j):
        return [b_h2qfm[j], b_h1[j]]
    def prod_writes(i):
        return [b_prod[i], b_O[2 * i], b_O[2 * i + 1]]
    def wdT_writes(k):
        extra = [b_Qt[k // 2] if k < 8 else b_Kt[(k - 8) // 2],
                 b_yfm[k // 2]]
        return [b_wdT[k]] + extra

    def rowslot_f(i):
        return sb["srow_f"][32 * i:32 * i + 1, :]
    def rowslot_b(i):
        return sb["srow_b"][32 * i:32 * i + 1, :]

    CHAIN_MAP = (("stg_f0", "stgf0"), ("stg_f1", "stgf1"), ("stg_b0", "stgb0"),
                 ("stg_b1", "stgb1"), ("Qt", "qtrow"), ("yfm", "yfmz"), ("wdT", "wdt"),
                 ("wgTs", "wgu"), ("wuTs", "wgu"), ("misc_x1t", "x1t"), ("yout", "yout"),
                 ("zhbm", "zq"), ("r16", "r16"), ("Abc", "rows"), ("Awgwb", "rows"),
                 ("Br", "rows"), ("Bmu", "rows"), ("S1", "s1"), ("WoBv", "wobv"),
                 ("misc_wgT_hbm", "whbm"), ("misc_wuT_hbm", "whbm"), ("misc_wdT_hbm", "whbm"))

    def _chain_for(writes):
        nm = writes[0].name if writes else "misc"
        for pref, ch in CHAIN_MAP:
            if nm.startswith(pref):
                return ch
        return "const"

    def DMA(dst, src, reads=(), writes=(), chain=None, queue="qsync"):
        pr.dma(queue, chain or _chain_for(list(writes)),
               lambda e: e.dma_start(out=dst, in_=src, allow_slow_non_contiguous=True),
               reads=reads, writes=writes)

    def GDMA(dst, src, chain, reads=(), writes=()):
        pr.dma("qgp", chain,
               lambda e: e.dma_start(out=dst, in_=src, accum_op=OP.add,
                                     allow_slow_non_contiguous=True),
               reads=reads, writes=writes)

    def CDMA(dst, src, chain, reads=(), writes=()):
        pr.dma("qgp", chain,
               lambda e: e.dma_start(out=dst, in_=src, allow_slow_non_contiguous=True),
               reads=reads, writes=writes)

    def TRANS(out_ap, in_ap, rd, wr, start=True, stop=True):
        pr.step("tensor", lambda e: e.matmul(out_ap, in_ap, sb["ident_b"][:],
                                             is_transpose=True, start=start, stop=stop),
                reads=list(rd) + [b_const], writes=wr)

    def round_to_int(eng, out_ap, in_ap, scale_ap, rd, wr, tmp_ap, tmp2_ap,
                     tmp_bufs=None, tmp2_bufs=None):
        bt = [B("rtmp")] + (tmp_bufs or [])
        bt2 = [B("rtmp2")] + (tmp2_bufs or [])
        if not SIM_FP64:
            pr.step(eng, lambda e: e.tensor_scalar(tmp_ap, in_ap, scale_ap, MAGIC, OP.mult, OP.add),
                    reads=rd, writes=bt)
            pr.step(eng, lambda e: e.tensor_scalar(out_ap, tmp_ap, MAGIC, None, OP.subtract),
                    reads=bt, writes=wr)
        else:
            pr.step(eng, lambda e: e.tensor_scalar(tmp_ap, in_ap, scale_ap, 0.5, OP.mult, OP.add),
                    reads=rd, writes=bt)
            pr.step(eng, lambda e: e.tensor_single_scalar(tmp2_ap, tmp_ap, 1.0, OP.mod),
                    reads=bt, writes=bt2)
            pr.step(eng, lambda e: e.tensor_sub(out_ap, tmp_ap, tmp2_ap),
                    reads=bt + bt2, writes=wr)

    DMA(sb["ident_b"][:], dr["d_ident_b"], writes=[b_const])
    DMA(sb["lmask"][:], dr["d_lmask"], writes=[b_const])
    DMA(sb["umask"][:], dr["d_umask"], writes=[b_const])
    DMA(sb["c_col"][:], dr["c"].rearrange("a (j p) -> (a p) j", p=128), writes=[b_misc["c_col"]])
    pr.step("vector", [lambda e: e.memset(sb["ones_col_b"][:], 1.0),
                       lambda e: e.memset(sb["ones_col_f"][:], 1.0),
                       lambda e: e.memset(sb["eps_col"][:], EPS),
                       lambda e: e.memset(sb["eps6_col"][:], 1e-6),
                       lambda e: e.memset(sb["onesrow_f"][:], 1.0)],
            writes=[b_const, b_misc["onesrow_f"]])
    pr.step("vector", lambda e: e.tensor_copy(sb["ones_row_r"][:], sb["onesrow_f"][:]),
            reads=[b_misc["onesrow_f"]], writes=[b_const])
    pr.step("vector", [lambda e: e.memset(sb["srow_f"][0:2, 0:2048], 0.0),
                       lambda e: e.memset(sb["srow_f"][0:1, 0:2048], 1.0)],
            writes=[b_rowf[0]])
    pr.step("vector", [lambda e: e.tensor_copy(sb["Abc_r"][:], sb["srow_f"][0:2, 0:128]),
                       lambda e: e.tensor_copy(sb["Br_r"][:], sb["srow_f"][0:2, 0:2048])],
            reads=[b_rowf[0]], writes=[b_Abc, b_Br])
    pr.step("vector", lambda e: e.memset(sb["srow_f"][0:2, 0:2048], 1.0),
            reads=[b_Abc, b_Br], writes=[b_rowf[0]])
    pr.step("vector", lambda e: e.tensor_copy(sb["Bmu_r"][:], sb["srow_f"][0:2, 0:2048]),
            reads=[b_rowf[0]], writes=[b_Bmu])
    for i4 in range(4):
        pr.step("vector", lambda e, i4=i4: e.memset(sb[f"Kt{i4}"][64:66, :], 1.0),
                writes=[b_Kt[i4]])
    pr.step("vector", lambda e: e.memset(sb["onesrow_b"][:], 1.0), writes=[b_misc["onesrowb"]])
    selrow = sb["srow_b"][64:80, 0:1024]
    pr.step("vector", lambda e: e.memset(selrow, 0.0), writes=[b_misc["selrow"]])
    for oc in range(ND):
        DMA(sb["srow_b"][64 + 2 * oc:64 + 2 * oc + 1, 128 * oc:128 * oc + 64],
            sb["onesrow_b"][:, 0:64],
            reads=[b_misc["onesrowb"]], writes=[b_misc["selrow"]])
        DMA(sb["srow_b"][64 + 2 * oc + 1:64 + 2 * oc + 2, 128 * oc + 64:128 * oc + 128],
            sb["onesrow_b"][:, 64:128], reads=[b_misc["onesrowb"]], writes=[b_misc["selrow"]])
    b_S1f = B("S1f")
    for s in range(2):
        pr.step("vector", [
            lambda e, s=s: e.memset(sb[f"Vp{s}"][:].rearrange("p (t u) -> p t u", u=130)[:, :, 64:65], 1.0),
            lambda e, s=s: e.memset(sb[f"Vp{s}"][:].rearrange("p (t u) -> p t u", u=130)[:, :, 129:130], 1.0),
        ], writes=[b_Vp[s]])

    pr.step("scalar", lambda e: e.activation(sb["csil_f"][:], sb["c_col"][:], AF.Silu),
            reads=[b_misc["c_col"]], writes=[b_misc["csil_f"]])
    pr.step("vector", lambda e: e.tensor_copy(sb["csil_r"][:], sb["csil_f"][:]),
            reads=[b_misc["csil_f"]], writes=[b_misc["csil"]])

    si = 0
    for g in range(6):
        for hf in range(2):
            for j in range(8):
                s = si % 2; si += 1
                DMA(sb[f"stg_f{s}"][:, 0:512],
                    dr["wmod"][128 * j:128 * (j + 1), 1024 * g + 512 * hf:1024 * g + 512 * hf + 512],
                    reads=[b_stg[s]], writes=[b_stg[s]],
                    queue=("qgp" if s == 1 else "qsync"))
                pr.step("vector", lambda e, s=s: e.tensor_copy(sb[f"stg_r{s}"][:], sb[f"stg_f{s}"][:, 0:512]),
                        reads=[b_stg[s]], writes=[b_stgr[s]])
                pr.step("tensor", lambda e, s=s, j=j: e.matmul(
                            pA1[0:1, 0:512], sb["csil_r"][:, j:j + 1], sb[f"stg_r{s}"][:],
                            start=(j == 0), stop=(j == 7)),
                        reads=[b_stgr[s], b_misc["csil"]], writes=[b_pA[0]])
            pr.step("scalar", lambda e, hf=hf: e.copy(rowslot_f(0)[:, 512 * hf:512 * (hf + 1)],
                                                      pA1[0:1, 0:512]),
                    reads=[b_pA[0]], writes=[b_misc["modrow"]])
        for j in range(8):
            DMA(sb["mod_col"][:, 8 * g + j:8 * g + j + 1],
                sb["srow_f"][0:1, 128 * j:128 * (j + 1)],
                reads=[b_misc["modrow"]], writes=[b_modcol])

    pr.step("vector", [
        lambda e: e.tensor_scalar(sb["modx_col"][:, 0:8], sb["mod_col"][:, 8:16], 1.0, None, OP.add),
        lambda e: e.tensor_scalar(sb["modx_col"][:, 8:16], sb["mod_col"][:, 32:40], 1.0, None, OP.add),
    ], reads=[b_modcol], writes=[b_modx])
    for (s0, d0) in ((16, 16), (40, 24)):
        pr.step("vector", [
            lambda e, s0=s0: e.tensor_scalar(sb["sm"][:, 16:24], sb["mod_col"][:, s0:s0 + 8], 0.0, 2.0, OP.is_ge, OP.mult),
            lambda e: e.tensor_scalar(sb["sm"][:, 16:24], sb["sm"][:, 16:24], 1.0, None, OP.subtract),
            lambda e, s0=s0: e.tensor_scalar(sb["sm"][:, 24:32], sb["mod_col"][:, s0:s0 + 8], -1.0, None, OP.mult),
            lambda e, s0=s0: e.tensor_max(sb["sm"][:, 24:32], sb["sm"][:, 24:32], sb["mod_col"][:, s0:s0 + 8]),
            lambda e: e.tensor_scalar(sb["sm"][:, 24:32], sb["sm"][:, 24:32], 0.25, None, OP.max),
            lambda e, d0=d0: e.tensor_mul(sb["modx_col"][:, d0:d0 + 8], sb["sm"][:, 16:24], sb["sm"][:, 24:32]),
        ], reads=[b_modcol, b_misc["sm_mod"]], writes=[b_modx, b_misc["sm_mod"]])
    DMA(sb["ln2_col"][:, 0:8], dr["ln2g"].rearrange("a (j p) -> (a p) j", p=128), writes=[b_misc["ln2col"]])
    DMA(sb["ln2_col"][:, 8:16], dr["ln2b"].rearrange("a (j p) -> (a p) j", p=128),
        reads=[b_misc["ln2col"]], writes=[b_misc["ln2col"]])
    pr.step("vector", [
        lambda e: e.tensor_mul(sb["modx_col"][:, 32:40], sb["modx_col"][:, 16:24], sb["ln2_col"][:, 0:8]),
        lambda e: e.tensor_mul(sb["modx_col"][:, 40:48], sb["modx_col"][:, 16:24], sb["ln2_col"][:, 8:16]),
    ], reads=[b_misc["ln2col"], b_modx], writes=[b_modx])

    def emit_ln1(startfill=None):
      for i in range(NT):
        s = i % 2
        bs = 32 + 20 * (i % 2)
        bl = b_ln1r[i % 2]
        if startfill is not None:
            for _ in range(3):
                try:
                    next(startfill)
                except StopIteration:
                    startfill = None
                    break
        DMA(sb[f"stg_f{s}"][:], dr["x"][128 * i:128 * (i + 1), :], reads=[b_stg[s]], writes=[b_stg[s]])
        pr.step("vector", lambda e, s=s, bs=bs: e.reduce_sum(sb["sm"][:, bs:bs + 1], sb[f"stg_f{s}"][:], axis=AX),
                reads=[b_stg[s]], writes=[bl])
        pr.step("scalar", lambda e, s=s, bs=bs, i=i: e.activation(
                    sb["stats_scrap"][:, 512 * (i % 2):512 * (i % 2) + 512],
                    sb[f"stg_f{s}"][:, 0:512], AF.Square, accum_out=sb["sm"][:, bs + 1:bs + 2]),
                reads=[b_stg[s]], writes=[b_scrap, bl])
        pr.step("scalar", lambda e, s=s, bs=bs, i=i: e.activation(
                    sb["stats_scrap"][:, 512 * (i % 2):512 * (i % 2) + 512],
                    sb[f"stg_f{s}"][:, 512:1024], AF.Square, accum_out=sb["sm"][:, bs + 8:bs + 9]),
                reads=[b_stg[s]], writes=[b_scrap, bl])
        pr.step("vector", [
            lambda e, bs=bs: e.tensor_add(sb["sm"][:, bs + 1:bs + 2], sb["sm"][:, bs + 1:bs + 2], sb["sm"][:, bs + 8:bs + 9]),
            lambda e, bs=bs: e.tensor_scalar(sb["sm"][:, bs + 2:bs + 3], sb["sm"][:, bs:bs + 1], 1.0 / D, None, OP.mult),
            lambda e, bs=bs: e.tensor_scalar(sb["sm"][:, bs + 3:bs + 4], sb["sm"][:, bs + 1:bs + 2], 1.0 / D, None, OP.mult),
            lambda e, bs=bs: e.tensor_mul(sb["sm"][:, bs + 4:bs + 5], sb["sm"][:, bs + 2:bs + 3], sb["sm"][:, bs + 2:bs + 3]),
            lambda e, bs=bs: e.tensor_sub(sb["sm"][:, bs + 3:bs + 4], sb["sm"][:, bs + 3:bs + 4], sb["sm"][:, bs + 4:bs + 5]),
        ], reads=[bl], writes=[bl])
        pr.step("scalar", lambda e, bs=bs: e.activation(sb["sm"][:, bs + 5:bs + 6], sb["sm"][:, bs + 3:bs + 4], AF.Sqrt,
                                                 bias=sb["eps_col"][:]),
                reads=[bl, b_const], writes=[bl])
        pr.step("vector", [
            lambda e, bs=bs: e.reciprocal(sb["sm"][:, bs + 6:bs + 7], sb["sm"][:, bs + 5:bs + 6]),
            lambda e, bs=bs: e.tensor_mul(sb["sm"][:, bs + 7:bs + 8], sb["sm"][:, bs + 2:bs + 3], sb["sm"][:, bs + 6:bs + 7]),
            lambda e, bs=bs: e.tensor_scalar(sb["sm"][:, bs + 7:bs + 8], sb["sm"][:, bs + 7:bs + 8], -1.0, None, OP.mult),
        ], reads=[bl], writes=[bl])
        pr.step("scalar", lambda e, s=s, bs=bs: e.activation(sb[f"stg_b{s}"][:], sb[f"stg_f{s}"][:], AF.Identity,
                                                      scale=sb["sm"][:, bs + 6:bs + 7], bias=sb["sm"][:, bs + 7:bs + 8]),
                reads=[b_stg[s], bl], writes=[b_stgb[s]])
        for j in range(ND):
            TRANS(pBb[:, 128 * j:128 * (j + 1)], sb[f"stg_b{s}"][:, 128 * j:128 * (j + 1)],
                  [b_stgb[s]], [b_pB])
        for j in range(ND):
            pr.step("scalar", lambda e, i=i, j=j: e.activation(
                        sb[f"h1_{j}"][:, 128 * i:128 * (i + 1)], pBb[:, 128 * j:128 * (j + 1)], AF.Identity,
                        scale=sb["modx_col"][:, j:j + 1], bias=sb["mod_col"][:, j:j + 1]),
                    reads=[b_pB, b_modx, b_modcol], writes=[b_h1[j]])

    P1REG = ((pA1, 0, b_pA1L), (pA1, 512, b_pA1R), (pA2, 0, b_pA2L))

    def gen_wprep(w, g):
        s = g % 2
        for m, wnm in (("q", "Wq"), ("k", "Wk"), ("v", "Wv")):
            CDMA(sb[f"wt{m}{s}"][:], dr[wnm][w, 128 * g:128 * (g + 1), :], "wcast",
                 reads=[b_wt[m][s]], writes=[b_wt[m][s]])
            yield
            for j in range(ND):
                TRANS(pA2b[:, 1024 + 128 * j:1024 + 128 * (j + 1)],
                      sb[f"wt{m}{s}"][:, 128 * j:128 * (j + 1)], [b_wt[m][s]], [b_pC])
            pr.step("scalar", lambda e, m=m, s=s: e.copy(sb[f"wt{m}{s}"][:], pA2b[:, 1024:2048]),
                    reads=[b_pC, b_wt[m][s]], writes=[b_wt[m][s]])
            yield

    def gen_proj(w, g, part="all", vstage="stg"):
        s = g % 2
        for m, bc in ((("q", 0), ("k", 8)) if part in ("all", "qk") else ()):
            tiles = Qt if m == "q" else Kt
            btile = b_Qt if m == "q" else b_Kt
            yoff = 0 if m == "q" else 4
            for tc in range(4):
                for j in range(ND):
                    pr.step("tensor", lambda e, m=m, s=s, j=j, tc=tc: e.matmul(
                                pA2[:, 512:1024], sb[f"wt{m}{s}"][:, 128 * j:128 * (j + 1)],
                                sb[f"h1_{j}"][:, 512 * tc:512 * (tc + 1)],
                                start=(j == 0), stop=(j == ND - 1)),
                            reads=[b_wt[m][s], b_h1[j]], writes=[b_pC])
                for hh in range(2):
                    dst = tiles[2 * s + hh]
                    pr.step("scalar", lambda e, dst=dst, tc=tc, hh=hh, g=g, bc=bc: e.activation(
                                dst[0:64, 512 * tc:512 * (tc + 1)], pA2[64 * hh:64 * (hh + 1), 512:1024],
                                AF.Identity, bias=sb["bias_col"][64 * hh:64 * (hh + 1), bc + g:bc + g + 1]),
                            reads=[b_pC, b_bias],
                            writes=[btile[2 * s + hh], b_yfm[yoff + 2 * s + hh]])
                yield
        if part == "qk":
            return
        if vstage == "stg":
            vst = lambda tc: (sb[f"stg_b{tc % 2}"], b_stgb[tc % 2], 512 * (tc // 2))
        else:
            vst = lambda tc: (sb["ptb1"], b_ptb[1], 512 * (tc % 2))
        for tc in range(4):
            for j in range(ND):
                pr.step("tensor", lambda e, s=s, j=j, tc=tc: e.matmul(
                            pA2[:, 512:1024], sb[f"wtv{s}"][:, 128 * j:128 * (j + 1)],
                            sb[f"h1_{j}"][:, 512 * tc:512 * (tc + 1)],
                            start=(j == 0), stop=(j == ND - 1)),
                        reads=[b_wt["v"][s], b_h1[j]], writes=[b_pC])
            tile, btile, hc = vst(tc)
            pr.step("scalar", lambda e, tile=tile, hc=hc: e.copy(
                        tile[:, hc:hc + 512], pA2[:, 512:1024]),
                    reads=[b_pC], writes=[btile])
            yield
            for t4 in range(4):
                TRANS(pA2b[:, 1024 + 128 * t4:1024 + 128 * (t4 + 1)],
                      tile[:, hc + 128 * t4:hc + 128 * (t4 + 1)],
                      [btile], [b_pC])
            for t4 in range(4):
                i = 4 * tc + t4
                vdst = sb[f"Vp{s}"][:, 130 * i:130 * i + 130].rearrange(
                    "p (h c) -> p h c", c=65)[:, :, 0:64]
                pr.step("scalar", lambda e, vdst=vdst, t4=t4: e.copy(
                            vdst, pA2b[:, 1024 + 128 * t4:1024 + 128 * (t4 + 1)].rearrange(
                                "p (h c) -> p h c", h=2)),
                        reads=[b_pC], writes=[b_Vp[s]])
            yield

    def gen_p2pv(w, g, hh):
        s = g % 2
        h = 2 * g + hh
        q_t, k_t = Qt[2 * s + hh], Kt[2 * s + hh]
        bq_, bk_ = b_Qt[2 * s + hh], b_Kt[2 * s + hh]
        vp, bvp = sb[f"Vp{s}"], b_Vp[s]
        oc, ro = h // 2, 64 * (h % 2)
        bpv = (b_pPVL, b_pPVR)
        prev = None
        pend_evac = None
        pc = 0

        def emit_pv(piece):
            qq_, c_, qs_, wd_, pts_ = piece
            off = 512 * (qq_ % 2) + (qs_ - 512 * qq_)
            pr.step("tensor", lambda e, c_=c_, off=off, wd_=wd_, pts_=pts_, qq_=qq_: e.matmul(
                        pPV[0:65, off:off + wd_],
                        vp[:, 130 * c_ + 65 * hh:130 * c_ + 65 * hh + 65],
                        ptb[pts_][:, 0:wd_],
                        start=(c_ == 0), stop=(c_ == 4 * qq_ + 3)),
                    reads=[b_ptb[pts_], bvp], writes=[bpv[qq_ % 2]])

        def emit_evac(qq_):
            off = 512 * (qq_ % 2)
            pr.step("scalar", lambda e, qq_=qq_, off=off: e.copy(
                        sb[f"Ofm{oc}"][ro:ro + 64, 512 * qq_:512 * (qq_ + 1)], pPV[0:64, off:off + 512]),
                    reads=[bpv[qq_ % 2]], writes=[b_O[oc]])
            rs = h % 4
            pr.step("scalar", lambda e, qq_=qq_, off=off, rs=rs: e.copy(
                        rowslot_f(rs)[:, 512 * qq_:512 * (qq_ + 1)], pPV[64:65, off:off + 512]),
                    reads=[bpv[qq_ % 2]], writes=[b_rowf[rs]])
            DMA(sb["S1f"][h:h + 1, 512 * qq_:512 * (qq_ + 1)], rowslot_f(rs)[:, 512 * qq_:512 * (qq_ + 1)],
                reads=[b_rowf[rs]], writes=[b_S1f])

        for qq in range(4):
            for c in range(4 * qq + 4):
                qs = max(512 * qq, 128 * c)
                width = 512 * (qq + 1) - qs
                pbh = pc % 2
                bpb = b_pBL if pbh == 0 else b_pBR
                pts = pc % 2
                diag = c >= 4 * qq
                pr.step("tensor", lambda e, c=c, qs=qs, width=width, pbh=pbh, diag=diag,
                                   q_t=q_t, k_t=k_t: e.matmul(
                            pB[:, 512 * pbh:512 * pbh + width], k_t[0:66, 128 * c:128 * (c + 1)],
                            q_t[0:66, qs:qs + width],
                            start=True, stop=not diag),
                        reads=[bq_, bk_], writes=[bpb])
                if diag:
                    pr.step("tensor", lambda e, pbh=pbh: e.matmul(
                                pB[:, 512 * pbh:512 * pbh + 128], sb["ident_b"][:], sb["lmask"][:],
                                start=False, stop=True),
                            reads=[b_const], writes=[bpb])
                pr.step("scalar", lambda e, pbh=pbh, width=width, pts=pts: e.activation(
                            ptb[pts][:, 0:width], pB[:, 512 * pbh:512 * pbh + width], AF.Exp,
                            scale=8.0),
                        reads=[bpb], writes=[b_ptb[pts]])
                if prev is not None:
                    emit_pv(prev)
                if pend_evac is not None:
                    emit_evac(pend_evac)
                    pend_evac = None
                prev = (qq, c, qs, width, pts)
                pc += 1
                yield
            pend_evac = qq
        emit_pv(prev)
        emit_evac(3)
        yield

    def slot(w, h, p2gen, prepgen, terngen=None):
        s = (h // 2) % 2
        hh = h % 2
        q_t, k_t = Qt[2 * s + hh], Kt[2 * s + hh]
        bq_, bk_ = b_Qt[2 * s + hh], b_Kt[2 * s + hh]
        pr.step("vector", lambda e: e.memset(sb["zp"][:], NEGBIG), writes=[b_zp])
        p2n = 0
        blk = 0
        for q in range(NT):
            kend = 128 * (q + 1)
            for b0 in range(0, kend, 512):
                bw = min(512, kend - b0)
                pa, off, bpa = P1REG[blk % 3]
                blk += 1
                has_diag = (b0 + bw == kend)
                pr.step("tensor", lambda e, pa=pa, off=off, q=q, b0=b0, bw=bw,
                                   hd=has_diag, q_t=q_t, k_t=k_t: e.matmul(
                            pa[:, off:off + bw], q_t[0:64, 128 * q:128 * (q + 1)],
                            k_t[0:64, b0:b0 + bw],
                            start=True, stop=not hd),
                        reads=[bq_, bk_], writes=[bpa])
                if has_diag:
                    pr.step("tensor", lambda e, pa=pa, off=off, bw=bw: e.matmul(
                                pa[:, off + bw - 128:off + bw], sb["ident_b"][:], sb["umask"][:],
                                start=False, stop=True),
                            reads=[b_const], writes=[bpa])
                mc = 16 * (b0 // 512) + q
                pr.step("vector", lambda e, pa=pa, off=off, bw=bw, mc=mc: e.reduce_max(
                            sb["zp"][:, mc:mc + 1], pa[:, off:off + bw], axis=AX),
                        reads=[bpa], writes=[b_zp])
            if p2gen is not None:
                tgt = (41 * max(0, q - 1)) // 14
                while p2n < tgt:
                    try:
                        next(p2gen)
                        p2n += 1
                    except StopIteration:
                        p2gen = None
                        break
            if prepgen is not None:
                try:
                    next(prepgen)
                except StopIteration:
                    prepgen = None
            if terngen is not None:
                try:
                    next(terngen)
                except StopIteration:
                    terngen = None
        while p2gen is not None:
            try:
                next(p2gen)
            except StopIteration:
                p2gen = None
        while prepgen is not None:
            try:
                next(prepgen)
            except StopIteration:
                prepgen = None
        pr.step("vector", [
            lambda e: e.tensor_max(sb["mfin"][:], sb["zp"][:, 0:16], sb["zp"][:, 16:32]),
            lambda e: e.tensor_max(sb["mfin"][:], sb["mfin"][:], sb["zp"][:, 32:48]),
            lambda e: e.tensor_max(sb["mfin"][:], sb["mfin"][:], sb["zp"][:, 48:64]),
        ], reads=[b_zp], writes=[b_mfin])
        pr.step("vector", [
            lambda e: e.tensor_scalar(sb["aug_f"][:], sb["mfin"][:], -1.0, None, OP.mult),
            lambda e: e.tensor_copy(sb["aughl_b"][:, 0:16], sb["aug_f"][:]),
        ], reads=[b_mfin], writes=[b_misc["aug"], b_misc["aughl"]])
        pr.step("vector", [
            lambda e: e.tensor_copy(sb["auglo_f"][:], sb["aughl_b"][:, 0:16]),
            lambda e: e.tensor_sub(sb["auglo_f"][:], sb["aug_f"][:], sb["auglo_f"][:]),
            lambda e: e.tensor_copy(sb["aughl_b"][:, 16:32], sb["auglo_f"][:]),
        ], reads=[b_misc["aug"], b_misc["aughl"]], writes=[b_misc["aughl"]])
        TRANS(pA2b[0:32, 1024:1152], sb["aughl_b"][:], [b_misc["aughl"]], [b_pC])
        pr.step("scalar", lambda e: e.copy(sb["mrow_b"][:], pA2b[0:32, 1024:1152]),
                reads=[b_pC], writes=[b_misc["mrow"]])
        DMA(q_t[64:65, :], sb["mrow_b"][0:16, :], reads=[b_misc["mrow"]],
            writes=[bq_, b_yfm[2 * s + hh]])
        DMA(q_t[65:66, :], sb["mrow_b"][16:32, :], reads=[b_misc["mrow"]],
            writes=[bq_, b_yfm[2 * s + hh]])

    b_swcols = B("swcols")
    b_swr = B("swr")
    b_rsum = {nm: B("rsum_" + nm) for nm in ("wg", "wu", "wd")}
    RSCOL = {"wg": 44, "wu": 45, "wd": 46}

    def gen_passA(nm):
        mcol = RSCOL[nm]
        pr.step("vector", lambda e, mcol=mcol: e.memset(sb["sm"][:, mcol:mcol + 1], 0.0),
                writes=[b_rsum[nm]])
        if nm == "wd":
            pieces = [dr["wd"][128 * ch:128 * (ch + 1), 1024 * p:1024 * (p + 1)]
                      for ch in range(8) for p in range(4)]
        else:
            pieces = [dr[nm][128 * ch:128 * (ch + 1), :] for ch in range(32)]
        for src in pieces:
            DMA(sb["stats_scrap"][:], src, reads=[b_scrap], writes=[b_scrap],
                chain=("pAg" if nm == "wg" else None),
                queue=("qgp" if nm == "wg" else "qsync"))
            pr.step("vector", [
                lambda e: e.reduce_sum(sb["sm"][:, 47:48], sb["stats_scrap"][:], axis=AX,
                                       apply_absolute_value=True),
                lambda e, mcol=mcol: e.tensor_add(sb["sm"][:, mcol:mcol + 1],
                                                  sb["sm"][:, mcol:mcol + 1], sb["sm"][:, 47:48]),
            ], reads=[b_scrap, b_rsum[nm], b_misc["rscr"]], writes=[b_rsum[nm], b_misc["rscr"]])
            yield

    def finalize_passA(nm, m):
        mcol = RSCOL[nm]
        pr.step("tensor", lambda e, mcol=mcol: e.matmul(
                    pA1[0:1, 0:1], sb["ones_col_f"][:], sb["sm"][:, mcol:mcol + 1],
                    start=True, stop=True),
                reads=[b_rsum[nm], b_const], writes=[b_pA1L])
        pr.step("scalar", lambda e: e.copy(sb["sm"][0:1, 47:48], pA1[0:1, 0:1]),
                reads=[b_pA1L], writes=[B("meansc")])
        pr.step("vector", [
            lambda e: e.tensor_scalar(sb["sm"][0:1, 47:48], sb["sm"][0:1, 47:48],
                                      1.0 / (2 ** 22), None, OP.mult),
            lambda e: e.tensor_scalar(sb["sm"][0:1, 47:48], sb["sm"][0:1, 47:48],
                                      1e-5, None, OP.max),
            lambda e: e.reciprocal(sb["sm"][0:1, 48:49], sb["sm"][0:1, 47:48]),
            lambda e: e.tensor_copy(sb["swr"][:, 0:1], sb["sm"][0:1, 48:49]),
            lambda e: e.tensor_copy(sb["swr"][:, 1:2], sb["sm"][0:1, 47:48]),
        ], reads=[B("meansc")], writes=[b_swr, B("meansc")])
        pr.step("tensor", lambda e: e.matmul(pA1[:, 0:2], sb["ones_row_r"][:], sb["swr"][:, 0:2],
                                             start=True, stop=True),
                reads=[b_swr, b_const], writes=[b_pA1L])
        pr.step("vector", lambda e, m=m: e.tensor_copy(sb["swcols"][:, 2 * m:2 * m + 2],
                                                       pA1[:, 0:2]),
                reads=[b_pA1L], writes=[b_swcols])

    def _tern_round(scol_ap, rd):
        pr.step("vector", lambda e, scol_ap=scol_ap: e.tensor_scalar(
                    sb["stats_scrap"][:], sb["stats_scrap"][:], scol_ap, MAGIC, OP.mult, OP.add),
                reads=list(rd) + [b_scrap], writes=[b_scrap])
        pr.step("vector", lambda e: e.tensor_scalar(sb["scrap_b"][:], sb["stats_scrap"][:],
                                                    MAGIC, None, OP.subtract),
                reads=[b_scrap], writes=[b_scrapb])
        pr.step("vector", lambda e: e.tensor_scalar(sb["scrap_b"][:], sb["scrap_b"][:],
                                                    1.0, -1.0, OP.min, OP.max),
                reads=[b_scrapb], writes=[b_scrapb])

    def gen_tern_gu(nm, m, hbm, bh):
        for ob in range(8):
            for rc in range(4):
                DMA(sb["stats_scrap"][:], dr[nm][512 * ob + 128 * rc:512 * ob + 128 * (rc + 1), :],
                    reads=[b_scrap], writes=[b_scrap])
                yield
                _tern_round(sb["swcols"][:, 2 * m:2 * m + 1], [b_swcols])
                for j in range(ND):
                    TRANS(pA2b[:, 1024 + 128 * j:1024 + 128 * (j + 1)],
                          sb["scrap_b"][:, 128 * j:128 * (j + 1)], [b_scrapb], [b_pC])
                pr.step("scalar", lambda e: e.copy(sb["scrap_b"][:], pA2b[:, 1024:2048]),
                        reads=[b_pC], writes=[b_scrapb])
                dst = hbm[:, 512 * ob + 128 * rc:512 * ob + 128 * (rc + 1)].rearrange(
                    "(j p) c -> p j c", p=128)
                DMA(dst, sb["scrap_b"][:].rearrange("p (j c) -> p j c", j=8),
                    reads=[b_scrapb], writes=[bh])
                yield

    def gen_tern_d():
        for k in range(NI):
            for r in range(ND):
                DMA(sb["stats_scrap"][:, 128 * r:128 * (r + 1)],
                    dr["wd"][128 * r:128 * (r + 1), 128 * k:128 * (k + 1)],
                    reads=[b_scrap], writes=[b_scrap])
            yield
            _tern_round(sb["swcols"][:, 4:5], [b_swcols])
            for r in range(ND):
                TRANS(pA2b[:, 1024 + 128 * r:1024 + 128 * (r + 1)],
                      sb["scrap_b"][:, 128 * r:128 * (r + 1)], [b_scrapb], [b_pC])
            pr.step("scalar", lambda e: e.copy(sb["scrap_b"][:], pA2b[:, 1024:2048]),
                    reads=[b_pC], writes=[b_scrapb])
            DMA(wdT_hbm[128 * k:128 * (k + 1), :], sb["scrap_b"][:],
                reads=[b_scrapb], writes=[b_misc["wdT_hbm"]])
            yield

    def gen_window_end(w):
        if not SIM_FP64:
            if w == 0:
                finalize_passA("wg", 0)
                finalize_passA("wu", 1)
            elif w == 1:
                finalize_passA("wd", 2)
        pr.step("vector", [
            lambda e: e.tensor_single_scalar(sb["S1f"][:], sb["S1f"][:], 1e-6, OP.add),
            lambda e: e.reciprocal(sb["S1f"][:], sb["S1f"][:]),
        ], reads=[b_S1f], writes=[b_S1f])
        pr.step("vector", lambda e: e.tensor_scalar(sb["S1_b"][:], sb["S1f"][:], -1e-6, 1.0,
                                                    OP.mult, OP.add),
                reads=[b_S1f], writes=[b_S1])
        pr.step("vector", lambda e: e.tensor_copy(sb["srow_b"][0:16, :], sb["S1f"][:]),
                reads=[b_S1f], writes=[b_rowb[0]])
        DMA(sb["ptb0"][0:16, 0:1024], sb["srow_b"][64:80, 0:1024],
            reads=[b_misc["selrow"]], writes=[b_ptb[0]])
        for oc in range(ND):
            for tc in range(4):
                reg, breg = ((pA2[:, 512:1024], b_pC) if tc % 2 == 0
                             else (pB[:, 0:512], b_pBL))
                pr.step("tensor", lambda e, oc=oc, tc=tc, reg=reg: e.matmul(
                            reg, sb["ptb0"][0:16, 128 * oc:128 * (oc + 1)],
                            sb["srow_b"][0:16, 512 * tc:512 * (tc + 1)],
                            start=True, stop=True),
                        reads=[b_ptb[0], b_rowb[0]], writes=[breg])
                pr.step("vector", lambda e, oc=oc, tc=tc, reg=reg: e.tensor_tensor(
                            sb[f"Ofm{oc}"][:, 512 * tc:512 * (tc + 1)],
                            sb[f"Ofm{oc}"][:, 512 * tc:512 * (tc + 1)], reg, OP.mult),
                        reads=[b_O[oc], breg], writes=[b_O[oc]])
            yield
        for r in range(ND):
            st = r % 2
            DMA(sb[f"stg_f{st}"][:], dr["Wo"][w, 128 * r:128 * (r + 1), :],
                reads=[b_stg[st]], writes=[b_stg[st]])
            pr.step("vector", lambda e, st=st: e.tensor_copy(sb[f"stg_b{st}"][:], sb[f"stg_f{st}"][:]),
                    reads=[b_stg[st]], writes=[b_stgb[st]])
            for j in range(ND):
                TRANS(pBb[:, 128 * j:128 * (j + 1)], sb[f"stg_b{st}"][:, 128 * j:128 * (j + 1)],
                      [b_stgb[st]], [b_pB])
            for j in range(ND):
                pr.step("scalar", lambda e, j=j, r=r: e.copy(
                            sb[f"wot{j}"][:, 128 * r:128 * (r + 1)], pBb[:, 128 * j:128 * (j + 1)]),
                        reads=[b_pB], writes=[b_wot[j]])
            yield
        for h in range(H):
            j, ro = h // 2, 64 * (h % 2)
            for half in range(2):
                pr.step("tensor", lambda e, j=j, ro=ro, half=half: e.matmul(
                            pA1[0:1, 512 * half:512 * (half + 1)],
                            sb["bv_col_b"][ro:ro + 64, j:j + 1],
                            sb[f"wot{j}"][ro:ro + 64, 512 * half:512 * (half + 1)],
                            start=True, stop=True),
                        reads=[b_misc["bvcol"], b_wot[j]], writes=[b_pA[0]])
            pr.step("scalar", lambda e: e.copy(rowslot_b(1)[:, 0:1024], pA1[0:1, 0:1024]),
                    reads=[b_pA[0]], writes=[b_rowb[1]])
            DMA(sb["WoBv_b"][h:h + 1, :], rowslot_b(1)[:, 0:1024], reads=[b_rowb[1]], writes=[b_WoBv])
            if h % 4 == 3:
                yield
        for oc in range(ND):
            for tc in range(4):
                for j in range(ND):
                    pr.step("tensor", lambda e, oc=oc, tc=tc, j=j: e.matmul(
                                pB[:, 0:512], sb[f"wot{j}"][:, 128 * oc:128 * (oc + 1)],
                                sb[f"Ofm{j}"][:, 512 * tc:512 * (tc + 1)],
                                start=(j == 0), stop=False),
                            reads=[b_wot[j], b_O[j]], writes=[b_pB])
                pr.step("tensor", lambda e, oc=oc, tc=tc: e.matmul(
                            pB[:, 0:512], sb["WoBv_b"][:, 128 * oc:128 * (oc + 1)],
                            sb["S1_b"][:, 512 * tc:512 * (tc + 1)],
                            start=False, stop=True),
                        reads=[b_WoBv, b_S1], writes=[b_pB])
                pr.step("scalar", lambda e, oc=oc, tc=tc: e.activation(
                            sb[f"yfm{oc}"][:, 512 * tc:512 * (tc + 1)], pB[:, 0:512], AF.Identity,
                            bias=sb["bias_col"][:, 16 + oc:16 + oc + 1]),
                        reads=[b_pB, b_bias], writes=yfm_writes(oc))
            yield
        for tc in range(4):
            for oc in range(ND):
                pr.step("tensor", lambda e, oc=oc, tc=tc: e.matmul(
                            pA1[0:1, 0:512], sb["ones_col_b"][:],
                            sb[f"yfm{oc}"][:, 512 * tc:512 * (tc + 1)],
                            start=(oc == 0), stop=(oc == ND - 1)),
                        reads=[b_yfm[oc], b_const], writes=[b_pA[0]])
            for oc in range(ND):
                sc = oc % 2
                pr.step("scalar", lambda e, oc=oc, tc=tc, sc=sc: e.activation(
                            sb["scrap_b"][:, 512 * sc:512 * (sc + 1)],
                            sb[f"yfm{oc}"][:, 512 * tc:512 * (tc + 1)], AF.Square),
                        reads=[b_yfm[oc]], writes=[b_scrapb])
                pr.step("tensor", lambda e, oc=oc, sc=sc: e.matmul(
                            pA1[0:1, 512:1024], sb["ones_col_b"][:],
                            sb["scrap_b"][:, 512 * sc:512 * (sc + 1)],
                            start=(oc == 0), stop=(oc == ND - 1)),
                        reads=[b_scrapb, b_const], writes=[b_pA[0]])
            pr.step("scalar", [lambda e, tc=tc: e.copy(rowslot_f(0)[:, 512 * tc:512 * (tc + 1)], pA1[0:1, 0:512]),
                               lambda e, tc=tc: e.copy(rowslot_f(1)[:, 512 * tc:512 * (tc + 1)], pA1[0:1, 512:1024])],
                    reads=[b_pA[0]], writes=[b_rowf[0], b_rowf[1]])
            yield
        DMA(sb["r16a"][:], rowslot_f(0), reads=[b_rowf[0]], writes=[b_r16["a"]])
        DMA(sb["r16b"][:], rowslot_f(1), reads=[b_rowf[1]], writes=[b_r16["b"]])
        pr.step("vector", [
            lambda e: e.tensor_scalar(sb["r16a"][:], sb["r16a"][:], 1.0 / D, None, OP.mult),
            lambda e: e.tensor_scalar(sb["r16b"][:], sb["r16b"][:], 1.0 / D, None, OP.mult),
            lambda e: e.tensor_mul(sb["r16c"][:], sb["r16a"][:], sb["r16a"][:]),
            lambda e: e.tensor_sub(sb["r16b"][:], sb["r16b"][:], sb["r16c"][:]),
        ], reads=[b_r16["a"], b_r16["b"]], writes=[b_r16["a"], b_r16["b"], b_r16["c"]])
        yield
        pr.step("scalar", lambda e: e.activation(sb["r16b"][:], sb["r16b"][:], AF.Sqrt,
                                                 bias=sb["eps_col"][0:16, :]),
                reads=[b_r16["b"], b_const], writes=[b_r16["b"]])
        pr.step("vector", [
            lambda e: e.reciprocal(sb["r16b"][:], sb["r16b"][:]),
            lambda e: e.tensor_mul(sb["r16c"][:], sb["r16a"][:], sb["r16b"][:]),
            lambda e: e.tensor_scalar(sb["r16c"][:], sb["r16c"][:], -1.0, None, OP.mult),
            lambda e: e.tensor_copy(sb["r16r"][:], sb["r16b"][:]),
            lambda e: e.tensor_copy(sb["r16r2"][:], sb["r16c"][:]),
        ], reads=[b_r16["b"], b_r16["c"]], writes=[b_r16["b"], b_r16["c"], b_r16r[0], b_r16r[1]])
        DMA(sb["Br_r"][0:1, :], sb["r16r"][:], reads=[b_r16r[0]], writes=[b_Br])
        DMA(sb["Bmu_r"][0:1, :], sb["r16r2"][:], reads=[b_r16r[1]], writes=[b_Bmu])
        for tc in range(4):
            pr.step("tensor", lambda e, tc=tc: e.matmul(
                        pA2[:, 0:512], sb["Abc_r"][:], sb["Br_r"][:, 512 * tc:512 * (tc + 1)],
                        start=True, stop=True),
                    reads=[b_Br, b_Abc], writes=[b_pA[1]])
            for oc in range(ND):
                pr.step("vector", lambda e, oc=oc, tc=tc: e.scalar_tensor_tensor(
                            sb[f"yfm{oc}"][:, 512 * tc:512 * (tc + 1)],
                            sb[f"yfm{oc}"][:, 512 * tc:512 * (tc + 1)],
                            sb["lnw_col"][:, oc:oc + 1], pA2[:, 0:512], OP.mult, OP.mult),
                        reads=[b_yfm[oc], b_lnw, b_pA[1]], writes=yfm_writes(oc))
            yield
        for k in range(ND):
            st = k % 2
            for r in range(ND):
                DMA(sb[f"stg_f{st}"][:, 128 * r:128 * (r + 1)],
                    dr["Wout"][128 * r:128 * (r + 1), 1024 * w + 128 * k:1024 * w + 128 * (k + 1)],
                    reads=[b_stg[st]], writes=[b_stg[st]])
            pr.step("vector", lambda e, st=st: e.tensor_copy(sb[f"stg_b{st}"][:], sb[f"stg_f{st}"][:]),
                    reads=[b_stg[st]], writes=[b_stgb[st]])
            for r in range(ND):
                TRANS(pBb[:, 128 * r:128 * (r + 1)], sb[f"stg_b{st}"][:, 128 * r:128 * (r + 1)],
                      [b_stgb[st]], [b_pB])
            pr.step("scalar", lambda e, k=k: e.copy(sb[f"wot{k}"][:], pBb[:, 0:1024]),
                    reads=[b_pB], writes=[b_wot[k]])
            yield
        for (base, slot) in ((0, 2), (8, 3)):
            for k in range(ND):
                for half in range(2):
                    pr.step("tensor", lambda e, k=k, half=half, base=base: e.matmul(
                                pA1[0:1, 512 * half:512 * (half + 1)],
                                sb["gcol_b"][:, base + k:base + k + 1],
                                sb[f"wot{k}"][:, 512 * half:512 * (half + 1)],
                                start=(k == 0), stop=(k == ND - 1)),
                            reads=[b_misc["gcolb"], b_wot[k]], writes=[b_pA[0]])
            pr.step("scalar", lambda e: e.copy(rowslot_f(2)[:, 0:1024], pA1[0:1, 0:1024]),
                    reads=[b_pA[0]], writes=[b_rowf[2]])
            if slot == 3 and w == 0:
                DMA(rowslot_f(2)[:, 1024:2048], dr["bout"], reads=[b_rowf[2]], writes=[b_rowf[2]])
                pr.step("vector", lambda e: e.tensor_add(rowslot_f(2)[:, 0:1024],
                                                         rowslot_f(2)[:, 0:1024], rowslot_f(2)[:, 1024:2048]),
                        reads=[b_rowf[2]], writes=[b_rowf[2]])
            DMA(sb["r16c"][0:8, :], rowslot_f(2)[:, 0:1024], reads=[b_rowf[2]], writes=[b_r16["c"]])
            pr.step("vector", lambda e: e.tensor_copy(sb["r16r"][0:8, :], sb["r16c"][0:8, :]),
                    reads=[b_r16["c"]], writes=[b_r16r[0]])
            DMA(sb["Awgwb_r"][slot - 2:slot - 1, :], sb["r16r"][0:8, :],
                reads=[b_r16r[0]], writes=[b_Awgwb])
            yield
        for oc in range(ND):
            for tc in range(4):
                for k in range(ND):
                    pr.step("tensor", lambda e, oc=oc, tc=tc, k=k: e.matmul(
                                pB[:, 0:512], sb[f"wot{k}"][:, 128 * oc:128 * (oc + 1)],
                                sb[f"yfm{k}"][:, 512 * tc:512 * (tc + 1)],
                                start=(k == 0), stop=False),
                            reads=[b_wot[k], b_yfm[k]], writes=[b_pB])
                pr.step("tensor", lambda e, oc=oc, tc=tc: e.matmul(
                            pB[:, 0:512], sb["Awgwb_r"][:, 128 * oc:128 * (oc + 1)],
                            sb["Bmu_r"][:, 512 * tc:512 * (tc + 1)],
                            start=False, stop=True),
                        reads=[b_Awgwb, b_Bmu], writes=[b_pB])
                sc = tc % 2
                pr.step("vector", lambda e, sc=sc: e.tensor_copy(
                            sb["stats_scrap"][:, 512 * sc:512 * (sc + 1)], pB[:, 0:512]),
                        reads=[b_pB], writes=[b_scrap])
                if w == 0:
                    DMA(z_hbm[128 * oc:128 * (oc + 1), 512 * tc:512 * (tc + 1)],
                        sb["stats_scrap"][:, 512 * sc:512 * (sc + 1)],
                        reads=[b_scrap], writes=[b_zhbm[oc]])
                else:
                    GDMA(z_hbm[128 * oc:128 * (oc + 1), 512 * tc:512 * (tc + 1)],
                         sb["stats_scrap"][:, 512 * sc:512 * (sc + 1)], "gz",
                         reads=[b_scrap], writes=[b_zhbm[oc]])
            yield

    def window(w, terngen=None, nextfill=None, first=False):
        DMA(sb["bias_col"][:, 0:8], dr["bq"][w].rearrange("(j p) -> p j", p=128),
            reads=[b_bias], writes=[b_bias])
        DMA(sb["bk_stage"][:], dr["bk"][w].rearrange("(j p) -> p j", p=128),
            reads=[b_misc["bkraw"]], writes=[b_misc["bkraw"]])
        pr.step("vector", lambda e: e.tensor_copy(sb["bias_col"][:, 8:16], sb["bk_stage"][:]),
                reads=[b_misc["bkraw"]], writes=[b_bias])
        DMA(sb["bias_col"][:, 16:24], dr["bo"][w].rearrange("(j p) -> p j", p=128),
            reads=[b_bias], writes=[b_bias])
        DMA(sb["bk_stage"][:], dr["bv"][w].rearrange("(j p) -> p j", p=128),
            reads=[b_misc["bkraw"]], writes=[b_misc["bkraw"]])
        pr.step("vector", lambda e: e.tensor_copy(sb["bv_col_b"][:], sb["bk_stage"][:]),
                reads=[b_misc["bkraw"]], writes=[b_misc["bvcol"]])
        DMA(sb["lnw_col"][:, 0:8], dr["lng"][w].rearrange("(j p) -> p j", p=128),
            reads=[b_lnw], writes=[b_lnw])
        DMA(sb["lnw_col"][:, 8:16], dr["lnb"][w].rearrange("(j p) -> p j", p=128),
            reads=[b_lnw], writes=[b_lnw])
        pr.step("vector", [lambda e: e.tensor_copy(sb["gcol_b"][:, 0:8], sb["lnw_col"][:, 0:8]),
                           lambda e: e.tensor_copy(sb["gcol_b"][:, 8:16], sb["lnw_col"][:, 8:16])],
                reads=[b_lnw], writes=[b_misc["gcolb"]])
        for i4 in range(4):
            pr.step("vector", lambda e, i4=i4: e.memset(sb[f"Kt{i4}"][64:66, :], 1.0),
                    writes=[b_Kt[i4], b_yfm[4 + i4]])
        if first:
            for _ in gen_wprep(w, 0):
                pass
            for _ in gen_proj(w, 0, "all"):
                pass
        else:
            for _ in gen_proj(w, 0, "qk"):
                pass
        p2g = None
        prepg = gen_wprep(w, 1)
        for h in range(16):
            g = h // 2
            slot(w, h, p2g, prepg, terngen)
            p2g = gen_p2pv(w, g, h % 2)
            if h % 2 == 0:
                prepg = gen_proj(w, g + 1) if g + 1 < 8 else None
            else:
                prepg = gen_wprep(w, g + 2) if g + 2 < 8 else None
        wpf = (nextfill or [None])[0]
        while p2g is not None:
            try:
                next(p2g)
            except StopIteration:
                p2g = None
            if wpf is not None:
                try:
                    next(wpf)
                except StopIteration:
                    wpf = None
        if terngen is not None:
            for _ in terngen:
                pass
        import collections as _c
        fillers = _c.deque(g for g in (nextfill or []) if g is not None)
        for _ in gen_window_end(w):
            if fillers:
                try:
                    next(fillers[0])
                except StopIteration:
                    fillers.popleft()
        while fillers:
            g0 = fillers.popleft()
            for _ in g0:
                pass

    if not SIM_FP64:
        import itertools
        startfill = gen_passA("wg")
        tern_w = [
            itertools.chain(gen_passA("wu")),
            itertools.chain(gen_passA("wd"), gen_tern_gu("wg", 0, wgT_hbm, b_misc["wgT_hbm"])),
            itertools.chain(gen_tern_gu("wu", 1, wuT_hbm, b_misc["wuT_hbm"]), gen_tern_d()),
        ]
    else:
        startfill = None
        tern_w = [None, None, None]
    emit_ln1(startfill)
    for w in range(NW):
        nextfill = ([gen_wprep(w + 1, 0), gen_proj(w + 1, 0, "v", vstage="ptb1")]
                    if w + 1 < NW else [])
        window(w, tern_w[w], nextfill, first=(w == 0))


    def fm_stats(srcs, bsrcs):
        for tc in range(4):
            for oc in range(ND):
                pr.step("tensor", lambda e, oc=oc, tc=tc: e.matmul(
                            pA1[0:1, 0:512], sb["ones_col_b"][:],
                            srcs[oc][:, 512 * tc:512 * (tc + 1)],
                            start=(oc == 0), stop=(oc == ND - 1)),
                        reads=[bsrcs[oc], b_const], writes=[b_pA[0]])
            for oc in range(ND):
                sc = oc % 2
                pr.step("scalar", lambda e, oc=oc, tc=tc, sc=sc: e.activation(
                            sb["scrap_b"][:, 512 * sc:512 * (sc + 1)],
                            srcs[oc][:, 512 * tc:512 * (tc + 1)], AF.Square),
                        reads=[bsrcs[oc]], writes=[b_scrapb])
                pr.step("tensor", lambda e, oc=oc, sc=sc: e.matmul(
                            pA1[0:1, 512:1024], sb["ones_col_b"][:],
                            sb["scrap_b"][:, 512 * sc:512 * (sc + 1)],
                            start=(oc == 0), stop=(oc == ND - 1)),
                        reads=[b_scrapb, b_const], writes=[b_pA[0]])
            pr.step("scalar", [lambda e, tc=tc: e.copy(rowslot_f(0)[:, 512 * tc:512 * (tc + 1)], pA1[0:1, 0:512]),
                               lambda e, tc=tc: e.copy(rowslot_f(1)[:, 512 * tc:512 * (tc + 1)], pA1[0:1, 512:1024])],
                    reads=[b_pA[0]], writes=[b_rowf[0], b_rowf[1]])
        DMA(sb["r16a"][:], rowslot_f(0), reads=[b_rowf[0]], writes=[b_r16["a"]])
        DMA(sb["r16b"][:], rowslot_f(1), reads=[b_rowf[1]], writes=[b_r16["b"]])
        pr.step("vector", [
            lambda e: e.tensor_scalar(sb["r16a"][:], sb["r16a"][:], 1.0 / D, None, OP.mult),
            lambda e: e.tensor_scalar(sb["r16b"][:], sb["r16b"][:], 1.0 / D, None, OP.mult),
            lambda e: e.tensor_mul(sb["r16c"][:], sb["r16a"][:], sb["r16a"][:]),
            lambda e: e.tensor_sub(sb["r16b"][:], sb["r16b"][:], sb["r16c"][:]),
        ], reads=[b_r16["a"], b_r16["b"]], writes=[b_r16["a"], b_r16["b"], b_r16["c"]])
        pr.step("scalar", lambda e: e.activation(sb["r16b"][:], sb["r16b"][:], AF.Sqrt,
                                                 bias=sb["eps_col"][0:16, :]),
                reads=[b_r16["b"], b_const], writes=[b_r16["b"]])
        pr.step("vector", [
            lambda e: e.reciprocal(sb["r16b"][:], sb["r16b"][:]),
            lambda e: e.tensor_mul(sb["r16c"][:], sb["r16a"][:], sb["r16b"][:]),
            lambda e: e.tensor_scalar(sb["r16c"][:], sb["r16c"][:], -1.0, None, OP.mult),
            lambda e: e.tensor_copy(sb["r16r"][:], sb["r16b"][:]),
            lambda e: e.tensor_copy(sb["r16r2"][:], sb["r16c"][:]),
        ], reads=[b_r16["b"], b_r16["c"]], writes=[b_r16["b"], b_r16["c"], b_r16r[0], b_r16r[1]])
        DMA(sb["Br_r"][0:1, :], sb["r16r"][:], reads=[b_r16r[0]], writes=[b_Br])
        DMA(sb["Bmu_r"][0:1, :], sb["r16r2"][:], reads=[b_r16r[1]], writes=[b_Bmu])

    def col8_to_Br(src_ap, rd):
        pr.step("vector", lambda e: e.tensor_copy(sb["aughl_b"][:, 0:8], src_ap),
                reads=rd, writes=[b_misc["aughl"]])
        TRANS(pBb[0:8, 0:128], sb["aughl_b"][:, 0:8], [b_misc["aughl"]], [b_pB])
        pr.step("scalar", lambda e: e.copy(sb["mrow_b"][0:8, :], pBb[0:8, 0:128]),
                reads=[b_pB], writes=[b_misc["mrow"]])
        pr.step("vector", lambda e: e.tensor_copy(sb["r16r"][0:8, :], sb["mrow_b"][0:8, :]),
                reads=[b_misc["mrow"]], writes=[b_r16r[0]])
        DMA(sb["Br_r"][0:1, 0:1024], sb["r16r"][0:8, :], reads=[b_r16r[0]], writes=[b_Br])

    for srccols, tile, btile, alw, plus1 in (
            (sb["mod_col"][:, 32:40], "cb1t", b_misc["cb1t"], b_wt["k"], True),
            (sb["mod_col"][:, 24:32], "sbt", b_misc["sbt"], b_wt["v"], False),
            (sb["modx_col"][:, 24:32], "gbt", b_misc["gbt"], b_wt["q"], False)):
        col8_to_Br(srccols, [b_modcol, b_modx])
        for hf in range(2):
            pr.step("tensor", lambda e, hf=hf: e.matmul(
                        pA1[:, 512 * hf:512 * (hf + 1)], sb["Abc_r"][:],
                        sb["Br_r"][:, 512 * hf:512 * (hf + 1)], start=True, stop=True),
                    reads=[b_Br, b_Abc], writes=[b_pA[0]])
        if plus1:
            pr.step("vector", lambda e, tile=tile: e.tensor_scalar(sb[tile][:], pA1[:, 0:1024],
                                                                   1.0, None, OP.add),
                    reads=[b_pA[0]], writes=[btile] + list(alw))
        else:
            pr.step("vector", lambda e, tile=tile: e.tensor_copy(sb[tile][:], pA1[:, 0:1024]),
                    reads=[b_pA[0]], writes=[btile] + list(alw))

    for oc in range(ND):
        for hf in range(2):
            s2_ = hf % 2
            DMA(sb[f"stg_f{s2_}"][:], z_hbm[128 * oc:128 * (oc + 1), 1024 * hf:1024 * (hf + 1)],
                reads=[b_stg[s2_], b_zhbm[oc]], writes=[b_stg[s2_]])
            pr.step("vector", lambda e, oc=oc, hf=hf, s2_=s2_: e.tensor_copy(
                        sb[f"yfm{oc}"][:, 1024 * hf:1024 * (hf + 1)], sb[f"stg_f{s2_}"][:]),
                    reads=[b_stg[s2_]], writes=yfm_writes(oc))

    fm_stats([sb[f"yfm{j}"] for j in range(ND)], b_yfm)
    for gi, srccols in enumerate((sb["modx_col"][:, 32:40], sb["modx_col"][:, 40:48])):
        pr.step("vector", lambda e, srccols=srccols: e.tensor_copy(sb["aughl_b"][:, 0:8], srccols),
                reads=[b_modx], writes=[b_misc["aughl"]])
        TRANS(pBb[0:8, 0:128], sb["aughl_b"][:, 0:8], [b_misc["aughl"]], [b_pB])
        pr.step("scalar", lambda e: e.copy(sb["mrow_b"][0:8, :], pBb[0:8, 0:128]),
                reads=[b_pB], writes=[b_misc["mrow"]])
        pr.step("vector", lambda e: e.tensor_copy(sb["r16r"][0:8, :], sb["mrow_b"][0:8, :]),
                reads=[b_misc["mrow"]], writes=[b_r16r[0]])
        DMA(sb["Awgwb_r"][gi:gi + 1, 0:1024], sb["r16r"][0:8, :], reads=[b_r16r[0]], writes=[b_Awgwb])

    for tc in range(4):
        pr.step("tensor", lambda e, tc=tc: e.matmul(
                    pA2[:, 0:512], sb["Abc_r"][:], sb["Br_r"][:, 512 * tc:512 * (tc + 1)],
                    start=True, stop=True),
                reads=[b_Br, b_Abc], writes=[b_pA[1]])
        for oc in range(ND):
            pr.step("tensor", lambda e, oc=oc, tc=tc: e.matmul(
                        pA1[:, 0:512], sb["Awgwb_r"][:, 128 * oc:128 * (oc + 1)],
                        sb["Bmu_r"][:, 512 * tc:512 * (tc + 1)], start=True, stop=True),
                    reads=[b_Awgwb, b_Bmu], writes=[b_pA[0]])
            pr.step("vector", lambda e, oc=oc, tc=tc: e.scalar_tensor_tensor(
                        sb["stats_scrap"][:, 0:512], sb[f"yfm{oc}"][:, 512 * tc:512 * (tc + 1)],
                        sb["modx_col"][:, 32 + oc:32 + oc + 1], pA2[:, 0:512], OP.mult, OP.mult),
                    reads=[b_yfm[oc], b_modx, b_pA[1]], writes=[b_scrap])
            pr.step("vector", lambda e, oc=oc, tc=tc: e.tensor_add(
                        sb[f"yfm{oc}"][:, 512 * tc:512 * (tc + 1)],
                        sb["stats_scrap"][:, 0:512], pA1[:, 0:512]),
                    reads=[b_scrap, b_pA[0]], writes=yfm_writes(oc))

    for i in range(NT):
        s = i % 2
        bs = 32 + 20 * (i % 2)
        bl = b_ln1r[i % 2]
        DMA(sb[f"stg_f{s}"][:], dr["x"][128 * i:128 * (i + 1), :], reads=[b_stg[s]], writes=[b_stg[s]])
        for j in range(ND):
            TRANS(pBb[:, 128 * j:128 * (j + 1)], sb[f"yfm{j}"][:, 128 * i:128 * (i + 1)],
                  [b_yfm[j]], [b_pB])
        pr.step("vector", lambda e, s=s: e.tensor_add(sb["x1t"][:], sb[f"stg_f{s}"][:], pBb[:, 0:1024]),
                reads=[b_stg[s], b_pB], writes=[b_misc["x1t"]] + VP_AL)
        pr.step("vector", lambda e, bs=bs: e.reduce_sum(sb["sm"][:, bs:bs + 1], sb["x1t"][:], axis=AX),
                reads=[b_misc["x1t"]], writes=[bl])
        pr.step("scalar", lambda e, bs=bs: e.activation(sb["stats_scrap"][:], sb["x1t"][:], AF.Square,
                                                 accum_out=sb["sm"][:, bs + 1:bs + 2]),
                reads=[b_misc["x1t"]], writes=[b_scrap, bl])
        pr.step("vector", [
            lambda e, bs=bs: e.tensor_scalar(sb["sm"][:, bs + 2:bs + 3], sb["sm"][:, bs:bs + 1], 1.0 / D, None, OP.mult),
            lambda e, bs=bs: e.tensor_scalar(sb["sm"][:, bs + 3:bs + 4], sb["sm"][:, bs + 1:bs + 2], 1.0 / D, None, OP.mult),
            lambda e, bs=bs: e.tensor_mul(sb["sm"][:, bs + 4:bs + 5], sb["sm"][:, bs + 2:bs + 3], sb["sm"][:, bs + 2:bs + 3]),
            lambda e, bs=bs: e.tensor_sub(sb["sm"][:, bs + 3:bs + 4], sb["sm"][:, bs + 3:bs + 4], sb["sm"][:, bs + 4:bs + 5]),
        ], reads=[bl], writes=[bl])
        pr.step("scalar", lambda e, bs=bs: e.activation(sb["sm"][:, bs + 5:bs + 6], sb["sm"][:, bs + 3:bs + 4], AF.Sqrt,
                                                 bias=sb["eps_col"][:]),
                reads=[bl, b_const], writes=[bl])
        pr.step("vector", [
            lambda e, bs=bs: e.reciprocal(sb["sm"][:, bs + 6:bs + 7], sb["sm"][:, bs + 5:bs + 6]),
            lambda e, bs=bs: e.tensor_mul(sb["sm"][:, bs + 7:bs + 8], sb["sm"][:, bs + 2:bs + 3], sb["sm"][:, bs + 6:bs + 7]),
            lambda e, bs=bs: e.tensor_scalar(sb["sm"][:, bs + 7:bs + 8], sb["sm"][:, bs + 7:bs + 8], -1.0, None, OP.mult),
        ], reads=[bl], writes=[bl])
        pr.step("scalar", lambda e, bs=bs: e.activation(sb["stats_scrap"][:], sb["x1t"][:], AF.Identity,
                                                 scale=sb["sm"][:, bs + 6:bs + 7], bias=sb["sm"][:, bs + 7:bs + 8]),
                reads=[b_misc["x1t"], bl], writes=[b_scrap])
        DMA(yout[128 * i:128 * (i + 1), :], sb["x1t"][:], reads=[b_misc["x1t"]], writes=[b_yout[i]])
        pr.step("vector", [
            lambda e: e.tensor_mul(sb["x2t"][:], sb["stats_scrap"][:], sb["cb1t"][:]),
            lambda e: e.tensor_add(sb["x2t"][:], sb["x2t"][:], sb["sbt"][:]),
        ], reads=[b_scrap, b_misc["cb1t"], b_misc["sbt"]], writes=[b_misc["x2t"]] + VP_AL)
        pr.step("vector", [
            lambda e, bs=bs: e.reduce_max(sb["sm"][:, bs + 8:bs + 9], sb["x2t"][:], axis=AX, apply_absolute_value=True),
            lambda e, bs=bs: e.tensor_scalar(sb["sm"][:, bs + 8:bs + 9], sb["sm"][:, bs + 8:bs + 9], 1e-5, None, OP.max),
            lambda e, bs=bs: e.reciprocal(sb["sm"][:, bs + 9:bs + 10], sb["sm"][:, bs + 8:bs + 9]),
            lambda e, bs=bs: e.tensor_scalar(sb["sm"][:, bs + 9:bs + 10], sb["sm"][:, bs + 9:bs + 10], 127.0, None, OP.mult),
        ], reads=[b_misc["x2t"]], writes=[bl])
        pr.step("vector", lambda e, i=i, bs=bs: e.tensor_scalar(sb["qscl"][:, i:i + 1], sb["sm"][:, bs + 8:bs + 9],
                                                         1.0 / 127.0, None, OP.mult),
                reads=[bl], writes=[b_qscl])
        round_to_int("vector", sb["scrap_b"][:, 0:1024], sb["x2t"][:], sb["sm"][:, bs + 9:bs + 10],
                     [b_misc["x2t"], bl], [b_scrapb], sb["stats_scrap"][:], sb[f"stg_f{1 - s}"][:],
                     tmp_bufs=[b_scrap], tmp2_bufs=[b_stg[1 - s]])
        for j in range(ND):
            TRANS(pBb[:, 128 * j:128 * (j + 1)], sb["scrap_b"][:, 128 * j:128 * (j + 1)],
                  [b_scrapb], [b_pB])
        for j in range(ND):
            pr.step("scalar", lambda e, i=i, j=j: e.copy(
                        sb[f"h2qfm{j}"][:, 128 * i:128 * (i + 1)], pBb[:, 128 * j:128 * (j + 1)]),
                    reads=[b_pB], writes=h2qfm_writes(j))

    for m, nm, nchunk in (((0, "wg", 32), (1, "wu", 32), (2, "wd", 8)) if SIM_FP64 else ()):
        pr.step("vector", lambda e: e.memset(sb["sm"][:, 45:46], 0.0), writes=[B("runsum")])
        for ch in range(nchunk):
            s = ch % 2
            if nm == "wd":
                for piece in range(4):
                    DMA(sb[f"stg_f{s}"][:], dr["wd"][128 * ch:128 * (ch + 1), 1024 * piece:1024 * (piece + 1)],
                        reads=[b_stg[s]], writes=[b_stg[s]])
                    pr.step("vector", [
                        lambda e, s=s: e.reduce_sum(sb["sm"][:, 46:47], sb[f"stg_f{s}"][:], axis=AX,
                                                    apply_absolute_value=True),
                        lambda e: e.tensor_add(sb["sm"][:, 45:46], sb["sm"][:, 45:46], sb["sm"][:, 46:47]),
                    ], reads=[b_stg[s], B("runsum")], writes=[B("runsum")])
                    s = (s + 1) % 2
            else:
                DMA(sb[f"stg_f{s}"][:], dr[nm][128 * ch:128 * (ch + 1), :],
                    reads=[b_stg[s]], writes=[b_stg[s]])
                pr.step("vector", [
                    lambda e, s=s: e.reduce_sum(sb["sm"][:, 46:47], sb[f"stg_f{s}"][:], axis=AX,
                                                apply_absolute_value=True),
                    lambda e: e.tensor_add(sb["sm"][:, 45:46], sb["sm"][:, 45:46], sb["sm"][:, 46:47]),
                ], reads=[b_stg[s], B("runsum")], writes=[B("runsum")])
        pr.step("tensor", lambda e: e.matmul(pA1[0:1, 0:1], sb["ones_col_f"][:], sb["sm"][:, 45:46],
                                             start=True, stop=True),
                reads=[B("runsum"), b_const], writes=[b_pA[0]])
        pr.step("scalar", lambda e: e.copy(sb["sm"][0:1, 47:48], pA1[0:1, 0:1]),
                reads=[b_pA[0]], writes=[B("meansc")])
        pr.step("vector", [
            lambda e: e.tensor_scalar(sb["sm"][0:1, 47:48], sb["sm"][0:1, 47:48], 1.0 / (2 ** 22), None, OP.mult),
            lambda e: e.tensor_scalar(sb["sm"][0:1, 47:48], sb["sm"][0:1, 47:48], 1e-5, None, OP.max),
            lambda e: e.reciprocal(sb["sm"][0:1, 48:49], sb["sm"][0:1, 47:48]),
            lambda e: e.tensor_copy(sb["swr"][:, 0:1], sb["sm"][0:1, 48:49]),
            lambda e: e.tensor_copy(sb["swr"][:, 1:2], sb["sm"][0:1, 47:48]),
        ], reads=[B("meansc")], writes=[b_swr, B("meansc")])
        pr.step("tensor", lambda e: e.matmul(pA1[:, 0:2], sb["ones_row_r"][:], sb["swr"][:, 0:2],
                                             start=True, stop=True),
                reads=[b_swr, b_const], writes=[b_pA[0]])
        pr.step("vector", lambda e, m=m: e.tensor_copy(sb["swcols"][:, 2 * m:2 * m + 2], pA1[:, 0:2]),
                reads=[b_pA[0]], writes=[b_swcols])

    def tern_stage(st, scol, width, rd):
        round_to_int("vector", sb[f"stg_b{st}"][:, 0:width], sb[f"stg_f{st}"][:, 0:width],
                     scol, rd, [b_stgb[st]], sb["stats_scrap"][:, 0:width], sb["x2t"][:, 0:width],
                     tmp_bufs=[b_scrap], tmp2_bufs=[b_misc["x2t"]] + VP_AL)
        pr.step("vector", lambda e, st=st, width=width: e.tensor_scalar(
                    sb[f"stg_b{st}"][:, 0:width], sb[f"stg_b{st}"][:, 0:width],
                    1.0, -1.0, OP.min, OP.max),
                reads=[b_stgb[st]], writes=[b_stgb[st]])

    for k in (range(NI) if SIM_FP64 else ()):
        st = k % 2
        for r in range(ND):
            DMA(sb[f"stg_f{st}"][:, 128 * r:128 * (r + 1)],
                dr["wd"][128 * r:128 * (r + 1), 128 * k:128 * (k + 1)],
                reads=[b_stg[st]], writes=[b_stg[st]])
        tern_stage(st, sb["swcols"][:, 4:5], 1024, [b_stg[st], b_swcols])
        for r in range(ND):
            TRANS(pBb[:, 128 * r:128 * (r + 1)], sb[f"stg_b{st}"][:, 128 * r:128 * (r + 1)],
                  [b_stgb[st]], [b_pB])
        slot = k % 16
        pr.step("scalar", lambda e, slot=slot: e.copy(sb[f"wdT{slot}"][:], pBb[:, 0:1024]),
                reads=[b_pB], writes=wdT_writes(slot))
        DMA(wdT_hbm[128 * k:128 * (k + 1), :], sb[f"wdT{slot}"][:],
            reads=[b_wdT[slot]], writes=[b_misc["wdT_hbm"]])

    for Q in range(4):
        qts = list(range(4 * Q, 4 * Q + 4))
        def gu_reload(ob):
            for wi, (wts, bwts, hbm, bh) in enumerate(
                    (("wgTs", b_wgTs, wgT_hbm, b_misc["wgT_hbm"]),
                     ("wuTs", b_wuTs, wuT_hbm, b_misc["wuT_hbm"]))):
                for j in range(ND):
                    wal = b_wot[j // 2] if wts == "wgTs" else b_wot[4 + j // 2]
                    DMA(sb[f"{wts}{j}"][:], hbm[128 * j:128 * (j + 1), 512 * ob:512 * (ob + 1)],
                        reads=[bwts[j], bh], writes=[bwts[j], wal],
                        chain=f"wgu{2 * wi + j % 2}")

        def gu_reload_j(ob, j):
            for wi, (wts, bwts, hbm, bh) in enumerate(
                    (("wgTs", b_wgTs, wgT_hbm, b_misc["wgT_hbm"]),
                     ("wuTs", b_wuTs, wuT_hbm, b_misc["wuT_hbm"]))):
                wal = b_wot[j // 2] if wts == "wgTs" else b_wot[4 + j // 2]
                DMA(sb[f"{wts}{j}"][:], hbm[128 * j:128 * (j + 1), 512 * ob:512 * (ob + 1)],
                    reads=[bwts[j], bh], writes=[bwts[j], wal],
                    chain=f"wgu{2 * wi + j % 2}")

        if Q == 0 and not SIM_FP64:
            gu_reload(0)
        for ob in range(8):
            if Q == 0 and SIM_FP64:
                for m, nm, wts, bwts, hbm, bh in ((0, "wg", "wgTs", b_wgTs, wgT_hbm, b_misc["wgT_hbm"]),
                                                  (1, "wu", "wuTs", b_wuTs, wuT_hbm, b_misc["wuT_hbm"])):
                    for rc in range(4):
                        st = rc % 2
                        DMA(sb[f"stg_f{st}"][:], dr[nm][512 * ob + 128 * rc:512 * ob + 128 * (rc + 1), :],
                            reads=[b_stg[st]], writes=[b_stg[st]])
                        tern_stage(st, sb["swcols"][:, 2 * m:2 * m + 1], 1024, [b_stg[st], b_swcols])
                        for j in range(ND):
                            TRANS(pBb[:, 128 * j:128 * (j + 1)], sb[f"stg_b{st}"][:, 128 * j:128 * (j + 1)],
                                  [b_stgb[st]], [b_pB])
                        for j in range(ND):
                            wal = b_wot[j // 2] if wts == "wgTs" else b_wot[4 + j // 2]
                            pr.step("scalar", lambda e, wts=wts, j=j, rc=rc: e.copy(
                                        sb[f"{wts}{j}"][:, 128 * rc:128 * (rc + 1)],
                                        pBb[:, 128 * j:128 * (j + 1)]),
                                    reads=[b_pB], writes=[bwts[j], wal])
                    for j in range(ND):
                        DMA(hbm[128 * j:128 * (j + 1), 512 * ob:512 * (ob + 1)], sb[f"{wts}{j}"][:],
                            reads=[bwts[j]], writes=[bh])
            elif SIM_FP64:
                gu_reload(ob)
            for qt in qts:
                qq = qt % 4
                gu_p, gu_bL, gu_bR = ((pB, b_pBL, b_pBR) if qt % 2 == 0
                                      else (pA2, b_pA2L, b_pC))
                a_c, s_c = 49 + 6 * (qt % 2), 50 + 6 * (qt % 2)
                for j in range(ND):
                    pr.step("tensor", [
                        lambda e, j=j, qt=qt, gu_p=gu_p: e.matmul(
                            gu_p[:, 0:512], sb[f"h2qfm{j}"][:, 128 * qt:128 * (qt + 1)],
                            sb[f"wgTs{j}"][:], start=(j == 0), stop=(j == ND - 1)),
                        lambda e, j=j, qt=qt, gu_p=gu_p: e.matmul(
                            gu_p[:, 512:1024], sb[f"h2qfm{j}"][:, 128 * qt:128 * (qt + 1)],
                            sb[f"wuTs{j}"][:], start=(j == 0), stop=(j == ND - 1)),
                    ], reads=[b_h2qfm[j], b_wgTs[j], b_wuTs[j]], writes=[gu_bL, gu_bR])
                    if qt == qts[-1] and not SIM_FP64 and (Q, ob) != (3, 7):
                        gu_reload_j((ob + 1) % 8, j)
                pr.step("vector", [
                    lambda e, qt=qt, a_c=a_c: e.tensor_mul(sb["sm"][:, a_c:a_c + 1],
                                                  sb["qscl"][:, qt:qt + 1], sb["swcols"][:, 1:2]),
                    lambda e, qt=qt, s_c=s_c: e.tensor_mul(sb["sm"][:, s_c:s_c + 1],
                                                  sb["qscl"][:, qt:qt + 1], sb["swcols"][:, 3:4]),
                ], reads=[b_qscl, b_swcols], writes=[B(f"alph{qt % 2}")])
                ev, bev = (("stats_scrap", [b_scrap]) if qt % 2 == 0
                           else ("x2t", [b_misc["x2t"]] + VP_AL))
                pr.step("scalar", lambda e, gu_p=gu_p, a_c=a_c, ev=ev: e.activation(
                            sb[ev][:, 0:512], gu_p[:, 0:512],
                            AF.Silu, scale=sb["sm"][:, a_c:a_c + 1]),
                        reads=[gu_bL, B(f"alph{qt % 2}")], writes=bev)
                pr.step("scalar", lambda e, gu_p=gu_p, s_c=s_c, ev=ev: e.activation(
                            sb[ev][:, 512:1024], gu_p[:, 512:1024],
                            AF.Identity, scale=sb["sm"][:, s_c:s_c + 1]),
                        reads=[gu_bR, B(f"alph{qt % 2}")], writes=bev)
                pr.step("vector", lambda e, qq=qq, ob=ob, ev=ev: e.tensor_mul(
                            sb[f"prod{qq}"][:, 512 * ob:512 * (ob + 1)],
                            sb[ev][:, 0:512], sb[ev][:, 512:1024]),
                        reads=bev, writes=prod_writes(qq))
                pr.step("vector", [
                    lambda e, qq=qq, ob=ob: e.reduce_max(sb["sm"][:, 51:52],
                                                         sb[f"prod{qq}"][:, 512 * ob:512 * (ob + 1)],
                                                         axis=AX, apply_absolute_value=True),
                    (lambda e, qq=qq: e.tensor_copy(sb["sm"][:, 52 + qq:53 + qq], sb["sm"][:, 51:52]))
                    if ob == 0 else
                    (lambda e, qq=qq: e.tensor_max(sb["sm"][:, 52 + qq:53 + qq],
                                                   sb["sm"][:, 52 + qq:53 + qq], sb["sm"][:, 51:52])),
                ], reads=[b_prod[qq]], writes=[B("pmax")])
        def wdma2(kloc, kh):
            kg = 16 * kh + kloc
            DMA(sb[f"wdT{kloc}"][:], wdT_hbm[128 * kg:128 * (kg + 1), :],
                reads=[b_wdT[kloc], b_misc["wdT_hbm"]], writes=wdT_writes(kloc),
                chain=f"wdh{kloc % 2}")
        for ti, qt in enumerate(qts):
            qq = qt % 4
            pr.step("vector", [
                lambda e, qq=qq: e.tensor_scalar(sb["sm"][:, 56:57], sb["sm"][:, 52 + qq:53 + qq],
                                                 1e-5, None, OP.max),
                lambda e: e.reciprocal(sb["sm"][:, 57:58], sb["sm"][:, 56:57]),
                lambda e: e.tensor_scalar(sb["sm"][:, 57:58], sb["sm"][:, 57:58], 127.0, None, OP.mult),
                lambda e: e.tensor_scalar(sb["sm"][:, 58:59], sb["sm"][:, 56:57], 1.0 / 127.0, None, OP.mult),
                lambda e, qq=qq: e.tensor_mul(sb["sm"][:, 60 + qq:61 + qq], sb["sm"][:, 58:59],
                                              sb["swcols"][:, 5:6]),
            ], reads=[B("pmax"), b_swcols], writes=[B("alphd"), B("s3")])
            for p4 in range(4):
                round_to_int("vector", sb["scrap_b"][:, 0:1024],
                             sb[f"prod{qq}"][:, 1024 * p4:1024 * (p4 + 1)], sb["sm"][:, 57:58],
                             [b_prod[qq], B("s3")], [b_scrapb],
                             sb["stats_scrap"][:], sb["x2t"][:],
                             tmp_bufs=[b_scrap], tmp2_bufs=[b_misc["x2t"]] + VP_AL)
                for kk in range(8):
                    TRANS(pBb[:, 128 * kk:128 * (kk + 1)], sb["scrap_b"][:, 128 * kk:128 * (kk + 1)],
                          [b_scrapb], [b_pB])
                pr.step("scalar", lambda e, qq=qq, p4=p4: e.copy(
                            sb[f"prod{qq}"][:, 1024 * p4:1024 * (p4 + 1)], pBb[:, 0:1024]),
                        reads=[b_pB], writes=prod_writes(qq))
                wdma2(4 * ti + p4, 0)
        for kh in range(2):
            for ti, qt in enumerate(qts):
                qq = qt % 4
                xd = "x2t" if (4 * kh + ti) % 2 == 0 else "x1t"
                bxd = [b_misc["x2t"]] if xd == "x2t" else [b_misc["x1t"]]
                for kloc in range(16):
                    kg = 16 * kh + kloc
                    for hf in range(2):
                        pr.step("tensor", lambda e, qq=qq, kg=kg, kloc=kloc, hf=hf: e.matmul(
                                    pA1[:, 512 * hf:512 * (hf + 1)],
                                    sb[f"prod{qq}"][:, 128 * kg:128 * (kg + 1)],
                                    sb[f"wdT{kloc}"][:, 512 * hf:512 * (hf + 1)],
                                    start=(kloc == 0), stop=(kloc == 15)),
                                reads=[b_prod[qq], b_wdT[kloc]], writes=[b_pA[0]])
                    if kh == 0 and ti == 3:
                        wdma2(kloc, 1)
                pr.step("vector", lambda e, qq=qq, xd=xd: e.scalar_tensor_tensor(
                            sb[xd][:], pA1[:, 0:1024], sb["sm"][:, 60 + qq:61 + qq], sb["gbt"][:],
                            OP.mult, OP.mult),
                        reads=[b_pA[0], B("alphd"), b_misc["gbt"]], writes=bxd + VP_AL)
                GDMA(yout[128 * qt:128 * (qt + 1), :], sb[xd][:], f"gyout{(4 * kh + ti) % 2}",
                     reads=bxd + [b_yout[qt]], writes=[b_yout[qt]])

    global _LASTPR
    _LASTPR = pr
    import contextlib
    with contextlib.ExitStack() as stack:
        sems = {}
        for e2 in ("tensor", "vector", "scalar", "gpsimd"):
            sems[e2] = stack.enter_context(nc.semaphore("sem_" + e2))
        for c2 in pr.chains:
            sems["c:" + c2] = stack.enter_context(nc.semaphore("semc_" + c2))
        block = stack.enter_context(nc.Block())

        def replay(e, engname, entries=None):
            if entries is None:
                entries = [(engname, ent) for ent in pr.steps[engname]]
            for qname, entry in entries:
                waits, fns = entry[0], entry[1]
                for k, v in waits:
                    e.wait_ge(sems[k], v)
                last = None
                for fn in fns:
                    last = fn(e)
                if qname in ("qsync", "qgp"):
                    last.then_inc(sems["c:" + entry[2]], 16)
                else:
                    last.then_inc(sems[qname], 1)

        qsync_chains = sorted({e3[2] for e3 in pr.steps["qsync"]})
        qgp_chains = sorted({e3[2] for e3 in pr.steps["qgp"]})

        def sync_sec(e):
            replay(e, "qsync")
            for c2 in qsync_chains:
                e.wait_ge(sems["c:" + c2], 16 * pr.chains[c2]["total"])

        def gp_sec(e):
            merged = sorted(
                [("gpsimd", ent) for ent in pr.steps["gpsimd"]]
                + [("qgp", ent) for ent in pr.steps["qgp"]],
                key=lambda it: it[1][3])
            replay(e, "gpsimd", merged)
            for c2 in qgp_chains:
                e.wait_ge(sems["c:" + c2], 16 * pr.chains[c2]["total"])

        block.sync(sync_sec)
        block.tensor(lambda e: replay(e, "tensor"))
        block.vector(lambda e: replay(e, "vector"))
        block.scalar(lambda e: replay(e, "scalar"))
        block.gpsimd(gp_sec)
    return nc


_LASTPR = None
_CACHE = {}


def kernel(**inputs):
    import ml_dtypes
    if "nc" not in _CACHE:
        _CACHE["nc"] = build_nc()
    nc = _CACHE["nc"]
    bf = ml_dtypes.bfloat16
    ident_b = np.eye(128, dtype=bf)
    lmask = (np.tril(np.ones((128, 128), np.float32), -1) * NEGBIG).astype(bf)
    umask = (np.triu(np.ones((128, 128), np.float32), 1) * NEGBIG).astype(bf)
    base = {
        "Wq": inputs["L0_attn_Wq"], "bq": inputs["L0_attn_bq"],
        "Wk": inputs["L0_attn_Wk"], "bk": inputs["L0_attn_bk"],
        "Wv": inputs["L0_attn_Wv"], "bv": inputs["L0_attn_bv"],
        "Wo": inputs["L0_attn_Wo"], "bo": inputs["L0_attn_bo"],
        "lng": inputs["L0_attn_ln_g"], "lnb": inputs["L0_attn_ln_b"],
        "Wout": inputs["L0_attn_Wout"], "bout": inputs["L0_attn_bout"].reshape(1, -1),
        "ln2g": inputs["L0_attn_ln2_g"].reshape(1, -1), "ln2b": inputs["L0_attn_ln2_b"].reshape(1, -1),
        "wg": inputs["L0_ffn_w_gate"], "wu": inputs["L0_ffn_w_up"], "wd": inputs["L0_ffn_w_down"],
        "wmod": inputs["L0_wmod"],
        "d_ident_b": ident_b, "d_lmask": lmask, "d_umask": umask,
    }
    base = {k: np.ascontiguousarray(v) for k, v in base.items()}
    in_maps = []
    for b in range(8):
        m = dict(base)
        m["x"] = np.ascontiguousarray(inputs["L0_attn_x"][b])
        m["c"] = np.ascontiguousarray(inputs["L0_c"][b:b + 1])
        in_maps.append(m)
    res = run_bass_kernel_spmd(nc, in_maps, core_ids=list(range(8)))
    return np.stack([r["y"] for r in res.results], axis=0)


# version 22 (final)
# speedup vs baseline: 1.0248x; 1.0248x over previous
import os
import numpy as np
import concourse.bass as bass
import concourse.mybir as mybir
from concourse.bass_utils import run_bass_kernel_spmd

F32 = mybir.dt.float32
F32R = mybir.dt.float32r
BF16 = mybir.dt.bfloat16
OP = mybir.AluOpType
AF = mybir.ActivationFunctionType
AX = mybir.AxisListType.X

SIM_FP64 = os.environ.get("BASS_SIM_FP64") == "1"
MAGIC = 12582912.0

L, D, H, NW, INNER = 2048, 1024, 16, 3, 4096
NT, ND, NI = L // 128, D // 128, INNER // 128
EPS = 1e-5
NEGBIG = -1.0e30

TAP = "final"


class Buf:
    __slots__ = ("name", "writer", "readers", "is_psum")
    def __init__(self, name):
        self.name, self.writer, self.readers = name, None, []
        self.is_psum = name in ("pA1L", "pA1R", "pA2L", "pC", "pBL", "pBR", "pPVL", "pPVR")


class MultiBuf:
    __slots__ = ("parts",)
    def __init__(self, parts):
        self.parts = list(parts)


def _expand(bs):
    out = []
    for b in bs:
        if isinstance(b, MultiBuf):
            out.extend(b.parts)
        else:
            out.append(b)
    return out


class Prog:
    ENGINES = ("tensor", "vector", "scalar", "gpsimd", "qsync", "qgp")

    def __init__(self):
        self.steps = {e: [] for e in self.ENGINES}
        self.count = {e: 0 for e in self.ENGINES}
        self.waited = {}
        self.chains = {}
        self.seq = 0

    def _chain(self, c):
        if c not in self.chains:
            self.chains[c] = {"total": 0, "open": False, "last_closed": 0}
        return self.chains[c]

    def _dep_value(self, dep):
        kind, key, v = dep
        if kind == "e":
            return (key, v)
        ch = self._chain(key)
        if v > ch["last_closed"]:
            ch["last_closed"] = ch["total"]
            ch["open"] = False
        return ("c:" + key, 16 * ch["last_closed"])

    def step(self, eng, fns, reads=(), writes=()):
        if not isinstance(fns, (list, tuple)):
            fns = [fns]
        for fn in fns:
            self._step1(eng, fn, reads, writes)

    def _step1(self, eng, fn, reads, writes):
        reads, writes = _expand(reads), _expand(writes)
        deps = {}
        def add(dep, waw_psum=False):
            if waw_psum and dep[0] == "e" and dep[1] == eng:
                return
            k, v = self._dep_value(dep)
            if v > deps.get(k, 0):
                deps[k] = v
        for b in reads:
            if b.writer:
                add(b.writer)
        for b in writes:
            if b.writer:
                add(b.writer, waw_psum=b.is_psum)
            for r in b.readers:
                add(r)
        waits = []
        for k, v in sorted(deps.items()):
            prev = self.waited.get((eng, k), 0)
            if v > prev:
                waits.append((k, v))
                self.waited[(eng, k)] = v
        self.count[eng] += 1
        after = ("e", eng, self.count[eng])
        for b in reads:
            b.readers.append(after)
        for b in writes:
            b.writer = after
            b.readers = []
        self.seq += 1
        self.steps[eng].append((waits, [fn], None, self.seq))

    def dma(self, queue, chain, fn, reads=(), writes=()):
        reads, writes = _expand(reads), _expand(writes)
        ch = self._chain(chain)
        deps = {}
        def add(dep):
            k, v = self._dep_value(dep)
            if v > deps.get(k, 0):
                deps[k] = v
        for b in reads:
            if b.writer:
                add(b.writer)
        for b in writes:
            if b.writer:
                add(b.writer)
            for r in b.readers:
                add(r)
        if not ch["open"]:
            if ch["last_closed"] > 0:
                k = "c:" + chain
                if ch["last_closed"] * 16 > deps.get(k, 0):
                    deps[k] = ch["last_closed"] * 16
            ch["open"] = True
        waits = []
        for k, v in sorted(deps.items()):
            prev = self.waited.get((queue, k), 0)
            if v > prev:
                waits.append((k, v))
                self.waited[(queue, k)] = v
        ch["total"] += 1
        self.count[queue] += 1
        after = ("d", chain, ch["total"])
        for b in reads:
            b.readers.append(after)
        for b in writes:
            b.writer = after
            b.readers = []
        self.seq += 1
        self.steps[queue].append((waits, [fn], chain, self.seq))


def build_nc():
    nc = bass.Bass("TRN2", target_bir_lowering=False)
    pr = Prog()
    bufs = {}
    def B(name):
        if name not in bufs:
            bufs[name] = Buf(name)
        return bufs[name]

    dr = {}
    def din(name, shape, dtype=F32):
        dr[name] = nc.dram_tensor(name, list(shape), dtype, kind="ExternalInput").ap()
    din("x", (L, D)); din("c", (1, D))
    din("Wq", (NW, D, D)); din("bq", (NW, D)); din("Wk", (NW, D, D)); din("bk", (NW, D))
    din("Wv", (NW, D, D)); din("bv", (NW, D)); din("Wo", (NW, D, D)); din("bo", (NW, D))
    din("lng", (NW, D)); din("lnb", (NW, D))
    din("Wout", (D, NW * D)); din("bout", (1, D)); din("ln2g", (1, D)); din("ln2b", (1, D))
    din("wg", (INNER, D)); din("wu", (INNER, D)); din("wd", (D, INNER))
    din("wmod", (D, 6 * D))
    din("d_ident_b", (128, 128), BF16); din("d_lmask", (128, 128), BF16); din("d_umask", (128, 128), BF16)
    yout = nc.dram_tensor("y", [L, D], F32, kind="ExternalOutput").ap()
    z_hbm = nc.dram_tensor("z_scr", [D, L], F32, kind="Internal").ap()
    wgT_hbm = nc.dram_tensor("wgT_scr", [D, INNER], BF16, kind="Internal").ap()
    wuT_hbm = nc.dram_tensor("wuT_scr", [D, INNER], BF16, kind="Internal").ap()
    wdT_hbm = nc.dram_tensor("wdT_scr", [INNER, D], BF16, kind="Internal").ap()

    sb = {}
    def S(name, shape, dtype=F32, offset=None):
        if offset is None:
            t = nc.alloc_sbuf_tensor(name, list(shape), dtype)
        else:
            t = nc.alloc_sbuf_tensor_at(name, list(shape), dtype, offset=offset)
        sb[name] = t
        return t

    S("ident_b", (128, 128), BF16); S("lmask", (128, 128), BF16); S("umask", (128, 128), BF16)
    S("ones_col_b", (128, 1), BF16)
    S("ones_row_r", (1, 128), F32R)
    S("onesrow_f", (1, 128), F32)
    S("eps_col", (128, 1), F32); S("eps6_col", (128, 1), F32)
    S("c_col", (128, 8), F32); S("csil_f", (128, 8), F32); S("csil_r", (128, 8), F32R)
    S("mod_col", (128, 48), F32)
    S("modx_col", (128, 48), F32)
    S("bias_col", (128, 32), F32)
    S("bk_stage", (128, 8), F32)
    S("bv_col_b", (128, 8), BF16)
    S("lnw_col", (128, 16), F32)
    S("gcol_b", (128, 16), BF16)
    S("ln2_col", (128, 16), F32)
    S("sm", (128, 64), F32)
    S("zp", (128, 64), F32)
    S("mfin", (128, 16), F32)
    S("aug_f", (128, 16), F32)
    S("aughl_b", (128, 32), BF16)
    S("auglo_f", (128, 16), F32)
    S("mrow_b", (32, 128), BF16)
    S("r16a", (16, 128), F32); S("r16b", (16, 128), F32); S("r16c", (16, 128), F32)
    S("r16r", (16, 128), F32R); S("r16r2", (16, 128), F32R)
    S("srow_f", (128, 2048), F32)
    S("srow_b", (128, 2048), BF16)
    S("Abc_r", (2, 128), F32R)
    S("Awgwb_r", (2, 1024), F32R)
    S("Br_r", (2, 2048), F32R)
    S("Bmu_r", (2, 2048), F32R)
    S("S1_b", (16, 2048), BF16)
    S("WoBv_b", (16, 1024), BF16)
    S("stats_scrap", (128, 1024), F32)
    S("scrap_b", (128, 1024), BF16)
    S("qscl", (128, 16), F32)
    S("ones_col_f", (128, 1), F32)
    S("swcols", (128, 8), F32)
    S("swr", (1, 4), F32R)
    S("onesrow_b", (1, 128), BF16)
    S("S1f", (16, 2048), F32)

    BASE = nc.SBUF_PARTITION_SIZE_BYTES - nc.SBUF_PARTITION_SIZE_BYTES
    BASE = ((nc._sbuf_addr_for_side("left") + 63) // 64) * 64 + 64
    off = [BASE]
    def R(name, shape, dtype, n=1, at=None):
        if at is not None:
            off[0] = at
        ts = []
        per = int(np.prod(shape[1:])) * mybir.dt.size(dtype)
        for i in range(n):
            ts.append(S(f"{name}{i}" if n > 1 else name, shape, dtype, offset=off[0]))
            off[0] += (per + 31) // 32 * 32
        return ts

    h1 = R("h1_", (128, L), BF16, ND)
    Ofm = R("Ofm", (128, L), BF16, ND)
    qkv_base = off[0]
    Qt = R("Qt", (128, L), BF16, 4)
    Kt = R("Kt", (128, L), BF16, 4)
    qkv_end = off[0]
    yfm = R("yfm", (128, L), BF16, ND, at=qkv_base)
    off[0] = qkv_end
    vp_base = off[0]
    Vp = R("Vp", (128, NT * 130), BF16, 2)
    stg_f = R("stg_f", (128, 1024), F32, 2)
    stgb_base = off[0]
    stg_b = R("stg_b", (128, 1024), BF16, 2)
    stg_r = R("stg_r", (128, 512), F32R, 2, at=stgb_base)
    off[0] = stgb_base + 4 * 1024
    wt_base = off[0]
    wtq = R("wtq", (128, 1024), BF16, 2)
    wtk = R("wtk", (128, 1024), BF16, 2)
    wtv = R("wtv", (128, 1024), BF16, 2)
    wot_base = off[0]
    wot = R("wot", (128, 1024), BF16, ND)
    ptb = R("ptb", (128, 1024), BF16, 2)
    attn_end = off[0]
    assert attn_end <= 228 * 1024, attn_end

    h2qfm = R("h2qfm", (128, L), BF16, ND, at=BASE)
    prod = R("prod", (128, INNER), BF16, 4, at=BASE + 32 * 1024)
    wdT = R("wdT", (128, 1024), BF16, 16, at=qkv_base)
    wgTs = R("wgTs", (128, 512), BF16, 8, at=wot_base)
    wuTs = R("wuTs", (128, 512), BF16, 8)
    x1t = R("x1t", (128, 1024), F32, 1, at=vp_base)[0]
    x2t = R("x2t", (128, 1024), F32)[0]
    assert off[0] <= vp_base + 8 * 1024 + 512
    gbt = R("gbt", (128, 1024), F32, 1, at=wt_base)[0]
    cb1t = R("cb1t", (128, 1024), F32)[0]
    sbt = R("sbt", (128, 1024), F32)[0]
    assert off[0] <= wot_base

    pA1 = nc.alloc_psum_tensor("pA1", [128, 1024], F32)
    pA2 = nc.alloc_psum_tensor("pA2", [128, 1024], F32)
    pB = nc.alloc_psum_tensor("pB", [128, 1024], F32)
    pPV = nc.alloc_psum_tensor("pPV", [65, 1024], F32)
    pBb = pB[:].bitcast(BF16)
    pA2b = pA2[:].bitcast(BF16)

    b_pA1L, b_pA1R, b_pA2L, b_pC = B("pA1L"), B("pA1R"), B("pA2L"), B("pC")
    b_pBL, b_pBR = B("pBL"), B("pBR")
    b_pPVL, b_pPVR = B("pPVL"), B("pPVR")
    b_pA = [MultiBuf([b_pA1L, b_pA1R]), MultiBuf([b_pA2L, b_pC])]
    b_pB = MultiBuf([b_pBL, b_pBR])
    b_pPV = MultiBuf([b_pPVL, b_pPVR])
    b_const = B("consts")
    b_stg = [B("stg_f0"), B("stg_f1")]
    b_stgb = [B("stg_b0"), B("stg_b1")]
    b_stgr = b_stgb
    b_h1 = [B(f"h1_{j}") for j in range(ND)]
    b_O = [B(f"O_{j}") for j in range(ND)]
    b_Qt = [B(f"Qt{i}") for i in range(4)]
    b_Kt = [B(f"Kt{i}") for i in range(4)]
    b_Vp = [B("Vp0"), B("Vp1")]
    b_yfm = [B(f"yfm{j}") for j in range(ND)]
    b_wt = {m: [B(f"wt{m}0"), B(f"wt{m}1")] for m in ("q", "k", "v")}
    b_wot = [B(f"wot{j}") for j in range(ND)]
    b_ptb = [B("ptb0"), B("ptb1")]
    b_zp, b_mfin = B("zp"), B("mfin")
    b_bias, b_lnw = B("bias_col"), B("lnw")
    b_S1, b_WoBv = B("S1"), B("WoBv")
    b_rowf = [B(f"rowf{i}") for i in range(4)]
    b_rowb = [B(f"rowb{i}") for i in range(4)]
    b_Abc, b_Awgwb, b_Br, b_Bmu = B("Abc"), B("Awgwb"), B("Br"), B("Bmu")
    b_r16 = {k: B("r16" + k) for k in "abc"}
    b_r16r = [B("r16r"), B("r16r2")]
    b_modcol, b_modx = B("mod_col"), B("modx")
    b_zhbm = [B(f"zhbm{j}") for j in range(ND)]
    b_yout = [B(f"yout{i}") for i in range(NT)]
    b_scrap, b_scrapb = B("scrap"), B("scrapb")
    b_ln1 = B("ln1")
    b_ln1r = [B("ln1"), B("ln1B")]
    b_qscl = B("qscl")
    b_h2qfm = [B(f"h2qfm{j}") for j in range(ND)]
    b_prod = [B(f"prod{i}") for i in range(4)]
    b_wdT = [B(f"wdT{k}") for k in range(16)]
    b_wgTs = [B(f"wgTs{i}") for i in range(8)]
    b_wuTs = [B(f"wuTs{i}") for i in range(8)]
    b_x1t = [B("x1t0"), B("x1t1")]
    VP_AL = [b_Vp[0], b_Vp[1]]
    class _Misc:
        def __getitem__(self, k):
            return B("misc_" + k)
    b_misc = _Misc()

    def yfm_writes(j):
        return [b_yfm[j], (b_Qt[j] if j < 4 else b_Kt[j - 4])]
    def h2qfm_writes(j):
        return [b_h2qfm[j], b_h1[j]]
    def prod_writes(i):
        return [b_prod[i], b_O[2 * i], b_O[2 * i + 1]]
    def wdT_writes(k):
        extra = [b_Qt[k // 2] if k < 8 else b_Kt[(k - 8) // 2],
                 b_yfm[k // 2]]
        return [b_wdT[k]] + extra

    def rowslot_f(i):
        return sb["srow_f"][32 * i:32 * i + 1, :]
    def rowslot_b(i):
        return sb["srow_b"][32 * i:32 * i + 1, :]

    CHAIN_MAP = (("stg_f0", "stgf0"), ("stg_f1", "stgf1"), ("stg_b0", "stgb0"),
                 ("stg_b1", "stgb1"), ("Qt", "qtrow"), ("yfm", "yfmz"), ("wdT", "wdt"),
                 ("wgTs", "wgu"), ("wuTs", "wgu"), ("misc_x1t", "x1t"), ("yout", "yout"),
                 ("zhbm", "zq"), ("r16", "r16"), ("Abc", "rows"), ("Awgwb", "rows"),
                 ("Br", "rows"), ("Bmu", "rows"), ("S1", "s1"), ("WoBv", "wobv"),
                 ("misc_wgT_hbm", "whbm"), ("misc_wuT_hbm", "whbm"), ("misc_wdT_hbm", "whbm"))

    def _chain_for(writes):
        nm = writes[0].name if writes else "misc"
        for pref, ch in CHAIN_MAP:
            if nm.startswith(pref):
                return ch
        return "const"

    def DMA(dst, src, reads=(), writes=(), chain=None, queue="qsync"):
        pr.dma(queue, chain or _chain_for(list(writes)),
               lambda e: e.dma_start(out=dst, in_=src, allow_slow_non_contiguous=True),
               reads=reads, writes=writes)

    def GDMA(dst, src, chain, reads=(), writes=()):
        pr.dma("qgp", chain,
               lambda e: e.dma_start(out=dst, in_=src, accum_op=OP.add,
                                     allow_slow_non_contiguous=True),
               reads=reads, writes=writes)

    def CDMA(dst, src, chain, reads=(), writes=()):
        pr.dma("qgp", chain,
               lambda e: e.dma_start(out=dst, in_=src, allow_slow_non_contiguous=True),
               reads=reads, writes=writes)

    def TRANS(out_ap, in_ap, rd, wr, start=True, stop=True):
        pr.step("tensor", lambda e: e.matmul(out_ap, in_ap, sb["ident_b"][:],
                                             is_transpose=True, start=start, stop=stop),
                reads=list(rd) + [b_const], writes=wr)

    def round_to_int(eng, out_ap, in_ap, scale_ap, rd, wr, tmp_ap, tmp2_ap,
                     tmp_bufs=None, tmp2_bufs=None):
        bt = [B("rtmp")] + (tmp_bufs or [])
        bt2 = [B("rtmp2")] + (tmp2_bufs or [])
        if not SIM_FP64:
            pr.step(eng, lambda e: e.tensor_scalar(tmp_ap, in_ap, scale_ap, MAGIC, OP.mult, OP.add),
                    reads=rd, writes=bt)
            pr.step(eng, lambda e: e.tensor_scalar(out_ap, tmp_ap, MAGIC, None, OP.subtract),
                    reads=bt, writes=wr)
        else:
            pr.step(eng, lambda e: e.tensor_scalar(tmp_ap, in_ap, scale_ap, 0.5, OP.mult, OP.add),
                    reads=rd, writes=bt)
            pr.step(eng, lambda e: e.tensor_single_scalar(tmp2_ap, tmp_ap, 1.0, OP.mod),
                    reads=bt, writes=bt2)
            pr.step(eng, lambda e: e.tensor_sub(out_ap, tmp_ap, tmp2_ap),
                    reads=bt + bt2, writes=wr)

    DMA(sb["ident_b"][:], dr["d_ident_b"], writes=[b_const])
    DMA(sb["lmask"][:], dr["d_lmask"], writes=[b_const])
    DMA(sb["umask"][:], dr["d_umask"], writes=[b_const])
    DMA(sb["c_col"][:], dr["c"].rearrange("a (j p) -> (a p) j", p=128), writes=[b_misc["c_col"]])
    pr.step("vector", [lambda e: e.memset(sb["ones_col_b"][:], 1.0),
                       lambda e: e.memset(sb["ones_col_f"][:], 1.0),
                       lambda e: e.memset(sb["eps_col"][:], EPS),
                       lambda e: e.memset(sb["eps6_col"][:], 1e-6),
                       lambda e: e.memset(sb["onesrow_f"][:], 1.0)],
            writes=[b_const, b_misc["onesrow_f"]])
    pr.step("vector", lambda e: e.tensor_copy(sb["ones_row_r"][:], sb["onesrow_f"][:]),
            reads=[b_misc["onesrow_f"]], writes=[b_const])
    pr.step("vector", [lambda e: e.memset(sb["srow_f"][0:2, 0:2048], 0.0),
                       lambda e: e.memset(sb["srow_f"][0:1, 0:2048], 1.0)],
            writes=[b_rowf[0]])
    pr.step("vector", [lambda e: e.tensor_copy(sb["Abc_r"][:], sb["srow_f"][0:2, 0:128]),
                       lambda e: e.tensor_copy(sb["Br_r"][:], sb["srow_f"][0:2, 0:2048])],
            reads=[b_rowf[0]], writes=[b_Abc, b_Br])
    pr.step("vector", lambda e: e.memset(sb["srow_f"][0:2, 0:2048], 1.0),
            reads=[b_Abc, b_Br], writes=[b_rowf[0]])
    pr.step("vector", lambda e: e.tensor_copy(sb["Bmu_r"][:], sb["srow_f"][0:2, 0:2048]),
            reads=[b_rowf[0]], writes=[b_Bmu])
    for i4 in range(4):
        pr.step("vector", lambda e, i4=i4: e.memset(sb[f"Kt{i4}"][64:66, :], 1.0),
                writes=[b_Kt[i4]])
    pr.step("vector", lambda e: e.memset(sb["onesrow_b"][:], 1.0), writes=[b_misc["onesrowb"]])
    selrow = sb["srow_b"][64:80, 0:1024]
    pr.step("vector", lambda e: e.memset(selrow, 0.0), writes=[b_misc["selrow"]])
    for oc in range(ND):
        DMA(sb["srow_b"][64 + 2 * oc:64 + 2 * oc + 1, 128 * oc:128 * oc + 64],
            sb["onesrow_b"][:, 0:64],
            reads=[b_misc["onesrowb"]], writes=[b_misc["selrow"]])
        DMA(sb["srow_b"][64 + 2 * oc + 1:64 + 2 * oc + 2, 128 * oc + 64:128 * oc + 128],
            sb["onesrow_b"][:, 64:128], reads=[b_misc["onesrowb"]], writes=[b_misc["selrow"]])
    b_S1f = B("S1f")

    pr.step("scalar", lambda e: e.activation(sb["csil_f"][:], sb["c_col"][:], AF.Silu),
            reads=[b_misc["c_col"]], writes=[b_misc["csil_f"]])
    pr.step("vector", lambda e: e.tensor_copy(sb["csil_r"][:], sb["csil_f"][:]),
            reads=[b_misc["csil_f"]], writes=[b_misc["csil"]])

    def gen_mod():
        si = 0
        for g in range(6):
            for hf in range(2):
                for j in range(8):
                    s = si % 2; si += 1
                    DMA(sb[f"stg_f{s}"][:, 0:512],
                        dr["wmod"][128 * j:128 * (j + 1), 1024 * g + 512 * hf:1024 * g + 512 * hf + 512],
                        reads=[b_stg[s]], writes=[b_stg[s]],
                        queue=("qgp" if s == 1 else "qsync"))
                    pr.step("vector", lambda e, s=s: e.tensor_copy(sb[f"stg_r{s}"][:], sb[f"stg_f{s}"][:, 0:512]),
                            reads=[b_stg[s]], writes=[b_stgr[s]])
                    pr.step("tensor", lambda e, s=s, j=j: e.matmul(
                                pA1[0:1, 0:512], sb["csil_r"][:, j:j + 1], sb[f"stg_r{s}"][:],
                                start=(j == 0), stop=(j == 7)),
                            reads=[b_stgr[s], b_misc["csil"]], writes=[b_pA[0]])
                    yield
                pr.step("scalar", lambda e, hf=hf: e.copy(rowslot_f(0)[:, 512 * hf:512 * (hf + 1)],
                                                          pA1[0:1, 0:512]),
                        reads=[b_pA[0]], writes=[b_misc["modrow"]])
            for j in range(8):
                DMA(sb["mod_col"][:, 8 * g + j:8 * g + j + 1],
                    sb["srow_f"][0:1, 128 * j:128 * (j + 1)],
                    reads=[b_misc["modrow"]], writes=[b_modcol])
            yield

        pr.step("vector", [
            lambda e: e.tensor_scalar(sb["modx_col"][:, 0:8], sb["mod_col"][:, 8:16], 1.0, None, OP.add),
            lambda e: e.tensor_scalar(sb["modx_col"][:, 8:16], sb["mod_col"][:, 32:40], 1.0, None, OP.add),
        ], reads=[b_modcol], writes=[b_modx])
        for (s0, d0) in ((16, 16), (40, 24)):
            pr.step("vector", [
                lambda e, s0=s0: e.tensor_scalar(sb["sm"][:, 16:24], sb["mod_col"][:, s0:s0 + 8], 0.0, 2.0, OP.is_ge, OP.mult),
                lambda e: e.tensor_scalar(sb["sm"][:, 16:24], sb["sm"][:, 16:24], 1.0, None, OP.subtract),
                lambda e, s0=s0: e.tensor_scalar(sb["sm"][:, 24:32], sb["mod_col"][:, s0:s0 + 8], -1.0, None, OP.mult),
                lambda e, s0=s0: e.tensor_max(sb["sm"][:, 24:32], sb["sm"][:, 24:32], sb["mod_col"][:, s0:s0 + 8]),
                lambda e: e.tensor_scalar(sb["sm"][:, 24:32], sb["sm"][:, 24:32], 0.25, None, OP.max),
                lambda e, d0=d0: e.tensor_mul(sb["modx_col"][:, d0:d0 + 8], sb["sm"][:, 16:24], sb["sm"][:, 24:32]),
            ], reads=[b_modcol, b_misc["sm_mod"]], writes=[b_modx, b_misc["sm_mod"]])
            yield
        DMA(sb["ln2_col"][:, 0:8], dr["ln2g"].rearrange("a (j p) -> (a p) j", p=128), writes=[b_misc["ln2col"]])
        DMA(sb["ln2_col"][:, 8:16], dr["ln2b"].rearrange("a (j p) -> (a p) j", p=128),
            reads=[b_misc["ln2col"]], writes=[b_misc["ln2col"]])
        pr.step("vector", [
            lambda e: e.tensor_mul(sb["modx_col"][:, 32:40], sb["modx_col"][:, 16:24], sb["ln2_col"][:, 0:8]),
            lambda e: e.tensor_mul(sb["modx_col"][:, 40:48], sb["modx_col"][:, 16:24], sb["ln2_col"][:, 8:16]),
        ], reads=[b_misc["ln2col"], b_modx], writes=[b_modx])
        yield

    def emit_ln1(startfill=None):
      xts = (sb["x1t"], sb["x2t"])
      xbufs = ([b_misc["x1t"]], [b_misc["x2t"]])
      for i in range(NT):
        s = i % 2
        bs = 32 + 20 * (i % 2)
        bl = b_ln1r[i % 2]
        if startfill is not None:
            for _ in range(9):
                try:
                    next(startfill)
                except StopIteration:
                    startfill = None
                    break
        DMA(xts[s][:], dr["x"][128 * i:128 * (i + 1), :], reads=xbufs[s],
            writes=xbufs[s] + VP_AL, chain=("x1t" if s == 0 else "x2t"))
        pr.step("vector", lambda e, s=s, bs=bs: e.reduce_sum(sb["sm"][:, bs:bs + 1], xts[s][:], axis=AX),
                reads=xbufs[s], writes=[bl])
        pr.step("scalar", lambda e, s=s, bs=bs, i=i: e.activation(
                    sb["stats_scrap"][:, 512 * (i % 2):512 * (i % 2) + 512],
                    xts[s][:, 0:512], AF.Square, accum_out=sb["sm"][:, bs + 1:bs + 2]),
                reads=xbufs[s], writes=[b_scrap, bl])
        pr.step("scalar", lambda e, s=s, bs=bs, i=i: e.activation(
                    sb["stats_scrap"][:, 512 * (i % 2):512 * (i % 2) + 512],
                    xts[s][:, 512:1024], AF.Square, accum_out=sb["sm"][:, bs + 8:bs + 9]),
                reads=xbufs[s], writes=[b_scrap, bl])
        pr.step("vector", [
            lambda e, bs=bs: e.tensor_add(sb["sm"][:, bs + 1:bs + 2], sb["sm"][:, bs + 1:bs + 2], sb["sm"][:, bs + 8:bs + 9]),
            lambda e, bs=bs: e.tensor_scalar(sb["sm"][:, bs + 2:bs + 3], sb["sm"][:, bs:bs + 1], 1.0 / D, None, OP.mult),
            lambda e, bs=bs: e.tensor_scalar(sb["sm"][:, bs + 3:bs + 4], sb["sm"][:, bs + 1:bs + 2], 1.0 / D, None, OP.mult),
            lambda e, bs=bs: e.tensor_mul(sb["sm"][:, bs + 4:bs + 5], sb["sm"][:, bs + 2:bs + 3], sb["sm"][:, bs + 2:bs + 3]),
            lambda e, bs=bs: e.tensor_sub(sb["sm"][:, bs + 3:bs + 4], sb["sm"][:, bs + 3:bs + 4], sb["sm"][:, bs + 4:bs + 5]),
        ], reads=[bl], writes=[bl])
        pr.step("scalar", lambda e, bs=bs: e.activation(sb["sm"][:, bs + 5:bs + 6], sb["sm"][:, bs + 3:bs + 4], AF.Sqrt,
                                                 bias=sb["eps_col"][:]),
                reads=[bl, b_const], writes=[bl])
        pr.step("vector", [
            lambda e, bs=bs: e.reciprocal(sb["sm"][:, bs + 6:bs + 7], sb["sm"][:, bs + 5:bs + 6]),
            lambda e, bs=bs: e.tensor_mul(sb["sm"][:, bs + 7:bs + 8], sb["sm"][:, bs + 2:bs + 3], sb["sm"][:, bs + 6:bs + 7]),
            lambda e, bs=bs: e.tensor_scalar(sb["sm"][:, bs + 7:bs + 8], sb["sm"][:, bs + 7:bs + 8], -1.0, None, OP.mult),
        ], reads=[bl], writes=[bl])
        pr.step("scalar", lambda e, s=s, bs=bs: e.activation(sb["scrap_b"][:], xts[s][:], AF.Identity,
                                                      scale=sb["sm"][:, bs + 6:bs + 7], bias=sb["sm"][:, bs + 7:bs + 8]),
                reads=xbufs[s] + [bl], writes=[b_scrapb])
        for j in range(ND):
            TRANS(pBb[:, 128 * j:128 * (j + 1)], sb["scrap_b"][:, 128 * j:128 * (j + 1)],
                  [b_scrapb], [b_pB])
        for j in range(ND):
            pr.step("scalar", lambda e, i=i, j=j: e.copy(
                        sb[f"h1_{j}"][:, 128 * i:128 * (i + 1)], pBb[:, 128 * j:128 * (j + 1)]),
                    reads=[b_pB], writes=[b_h1[j]])
      while startfill is not None:
          try:
              next(startfill)
          except StopIteration:
              startfill = None

    P1REG = ((pA1, 0, b_pA1L), (pA1, 512, b_pA1R), (pA2, 0, b_pA2L))

    def gen_wprep(w, g):
        s = g % 2
        for m, wnm in (("q", "Wq"), ("k", "Wk"), ("v", "Wv")):
            CDMA(sb[f"wt{m}{s}"][:], dr[wnm][w, 128 * g:128 * (g + 1), :], "wcast",
                 reads=[b_wt[m][s]], writes=[b_wt[m][s]])
            yield
            for j in range(ND):
                TRANS(pA2b[:, 1024 + 128 * j:1024 + 128 * (j + 1)],
                      sb[f"wt{m}{s}"][:, 128 * j:128 * (j + 1)], [b_wt[m][s]], [b_pC])
            pr.step("scalar", lambda e, m=m, s=s: e.copy(sb[f"wt{m}{s}"][:], pA2b[:, 1024:2048]),
                    reads=[b_pC, b_wt[m][s]], writes=[b_wt[m][s]])
            yield

    def gen_proj(w, g, part="all", vstage="stg"):
        s = g % 2
        for m, bc in ((("q", 0), ("k", 8)) if part in ("all", "qk") else ()):
            tiles = Qt if m == "q" else Kt
            btile = b_Qt if m == "q" else b_Kt
            yoff = 0 if m == "q" else 4
            for tc in range(4):
                for j in range(ND):
                    pr.step("tensor", lambda e, m=m, s=s, j=j, tc=tc: e.matmul(
                                pA2[:, 512:1024], sb[f"wt{m}{s}"][:, 128 * j:128 * (j + 1)],
                                sb[f"h1_{j}"][:, 512 * tc:512 * (tc + 1)],
                                start=(j == 0), stop=(j == ND - 1)),
                            reads=[b_wt[m][s], b_h1[j]], writes=[b_pC])
                for hh in range(2):
                    dst = tiles[2 * s + hh]
                    pr.step("scalar", lambda e, dst=dst, tc=tc, hh=hh, g=g, bc=bc: e.activation(
                                dst[0:64, 512 * tc:512 * (tc + 1)], pA2[64 * hh:64 * (hh + 1), 512:1024],
                                AF.Identity, bias=sb["bias_col"][64 * hh:64 * (hh + 1), bc + g:bc + g + 1]),
                            reads=[b_pC, b_bias],
                            writes=[btile[2 * s + hh], b_yfm[yoff + 2 * s + hh]])
                yield
        if part == "qk":
            return
        if vstage == "stg":
            vst = lambda tc: (sb[f"stg_b{tc % 2}"], b_stgb[tc % 2], 512 * (tc // 2))
        else:
            vst = lambda tc: (sb["ptb1"], b_ptb[1], 512 * (tc % 2))
        for tc in range(4):
            for j in range(ND):
                pr.step("tensor", lambda e, s=s, j=j, tc=tc: e.matmul(
                            pA2[:, 512:1024], sb[f"wtv{s}"][:, 128 * j:128 * (j + 1)],
                            sb[f"h1_{j}"][:, 512 * tc:512 * (tc + 1)],
                            start=(j == 0), stop=(j == ND - 1)),
                        reads=[b_wt["v"][s], b_h1[j]], writes=[b_pC])
            tile, btile, hc = vst(tc)
            pr.step("scalar", lambda e, tile=tile, hc=hc: e.copy(
                        tile[:, hc:hc + 512], pA2[:, 512:1024]),
                    reads=[b_pC], writes=[btile])
            yield
            for t4 in range(4):
                TRANS(pA2b[:, 1024 + 128 * t4:1024 + 128 * (t4 + 1)],
                      tile[:, hc + 128 * t4:hc + 128 * (t4 + 1)],
                      [btile], [b_pC])
            for t4 in range(4):
                i = 4 * tc + t4
                vdst = sb[f"Vp{s}"][:, 130 * i:130 * i + 130].rearrange(
                    "p (h c) -> p h c", c=65)[:, :, 0:64]
                pr.step("scalar", lambda e, vdst=vdst, t4=t4: e.copy(
                            vdst, pA2b[:, 1024 + 128 * t4:1024 + 128 * (t4 + 1)].rearrange(
                                "p (h c) -> p h c", h=2)),
                        reads=[b_pC], writes=[b_Vp[s]])
            yield

    def gen_p2pv(w, g, hh):
        s = g % 2
        h = 2 * g + hh
        q_t, k_t = Qt[2 * s + hh], Kt[2 * s + hh]
        bq_, bk_ = b_Qt[2 * s + hh], b_Kt[2 * s + hh]
        vp, bvp = sb[f"Vp{s}"], b_Vp[s]
        oc, ro = h // 2, 64 * (h % 2)
        bpv = (b_pPVL, b_pPVR)
        prev = None
        pend_evac = None
        pc = 0

        def emit_pv(piece):
            qq_, c_, qs_, wd_, pts_ = piece
            off = 512 * (qq_ % 2) + (qs_ - 512 * qq_)
            pr.step("tensor", lambda e, c_=c_, off=off, wd_=wd_, pts_=pts_, qq_=qq_: e.matmul(
                        pPV[0:65, off:off + wd_],
                        vp[:, 130 * c_ + 65 * hh:130 * c_ + 65 * hh + 65],
                        ptb[pts_][:, 0:wd_],
                        start=(c_ == 0), stop=(c_ == 4 * qq_ + 3)),
                    reads=[b_ptb[pts_], bvp], writes=[bpv[qq_ % 2]])

        def emit_evac(qq_):
            off = 512 * (qq_ % 2)
            pr.step("scalar", lambda e, qq_=qq_, off=off: e.copy(
                        sb[f"Ofm{oc}"][ro:ro + 64, 512 * qq_:512 * (qq_ + 1)], pPV[0:64, off:off + 512]),
                    reads=[bpv[qq_ % 2]], writes=[b_O[oc]])
            rs = h % 4
            pr.step("scalar", lambda e, qq_=qq_, off=off, rs=rs: e.copy(
                        rowslot_f(rs)[:, 512 * qq_:512 * (qq_ + 1)], pPV[64:65, off:off + 512]),
                    reads=[bpv[qq_ % 2]], writes=[b_rowf[rs]])
            DMA(sb["S1f"][h:h + 1, 512 * qq_:512 * (qq_ + 1)], rowslot_f(rs)[:, 512 * qq_:512 * (qq_ + 1)],
                reads=[b_rowf[rs]], writes=[b_S1f])

        for qq in range(4):
            for c in range(4 * qq + 4):
                qs = max(512 * qq, 128 * c)
                width = 512 * (qq + 1) - qs
                pbh = pc % 2
                bpb = b_pBL if pbh == 0 else b_pBR
                pts = pc % 2
                diag = c >= 4 * qq
                pr.step("tensor", lambda e, c=c, qs=qs, width=width, pbh=pbh, diag=diag,
                                   q_t=q_t, k_t=k_t: e.matmul(
                            pB[:, 512 * pbh:512 * pbh + width], k_t[0:66, 128 * c:128 * (c + 1)],
                            q_t[0:66, qs:qs + width],
                            start=True, stop=not diag),
                        reads=[bq_, bk_], writes=[bpb])
                if diag:
                    pr.step("tensor", lambda e, pbh=pbh: e.matmul(
                                pB[:, 512 * pbh:512 * pbh + 128], sb["ident_b"][:], sb["lmask"][:],
                                start=False, stop=True),
                            reads=[b_const], writes=[bpb])
                pr.step("scalar", lambda e, pbh=pbh, width=width, pts=pts: e.activation(
                            ptb[pts][:, 0:width], pB[:, 512 * pbh:512 * pbh + width], AF.Exp,
                            scale=8.0),
                        reads=[bpb], writes=[b_ptb[pts]])
                if prev is not None:
                    emit_pv(prev)
                if pend_evac is not None:
                    emit_evac(pend_evac)
                    pend_evac = None
                prev = (qq, c, qs, width, pts)
                pc += 1
                yield
            pend_evac = qq
        emit_pv(prev)
        emit_evac(3)
        yield

    def slot(w, h, p2gen, prepgen, terngen=None):
        s = (h // 2) % 2
        hh = h % 2
        q_t, k_t = Qt[2 * s + hh], Kt[2 * s + hh]
        bq_, bk_ = b_Qt[2 * s + hh], b_Kt[2 * s + hh]
        pr.step("vector", lambda e: e.memset(sb["zp"][:], NEGBIG), writes=[b_zp])
        p2n = 0
        blk = 0
        for q in range(NT):
            kend = 128 * (q + 1)
            for b0 in range(0, kend, 512):
                bw = min(512, kend - b0)
                pa, off, bpa = P1REG[blk % 3]
                blk += 1
                has_diag = (b0 + bw == kend)
                pr.step("tensor", lambda e, pa=pa, off=off, q=q, b0=b0, bw=bw,
                                   hd=has_diag, q_t=q_t, k_t=k_t: e.matmul(
                            pa[:, off:off + bw], q_t[0:64, 128 * q:128 * (q + 1)],
                            k_t[0:64, b0:b0 + bw],
                            start=True, stop=not hd),
                        reads=[bq_, bk_], writes=[bpa])
                if has_diag:
                    pr.step("tensor", lambda e, pa=pa, off=off, bw=bw: e.matmul(
                                pa[:, off + bw - 128:off + bw], sb["ident_b"][:], sb["umask"][:],
                                start=False, stop=True),
                            reads=[b_const], writes=[bpa])
                mc = 16 * (b0 // 512) + q
                pr.step("vector", lambda e, pa=pa, off=off, bw=bw, mc=mc: e.reduce_max(
                            sb["zp"][:, mc:mc + 1], pa[:, off:off + bw], axis=AX),
                        reads=[bpa], writes=[b_zp])
            if p2gen is not None:
                tgt = (41 * max(0, q - 1)) // 14
                while p2n < tgt:
                    try:
                        next(p2gen)
                        p2n += 1
                    except StopIteration:
                        p2gen = None
                        break
            if prepgen is not None:
                try:
                    next(prepgen)
                except StopIteration:
                    prepgen = None
            if terngen is not None:
                try:
                    next(terngen)
                except StopIteration:
                    terngen = None
        while p2gen is not None:
            try:
                next(p2gen)
            except StopIteration:
                p2gen = None
        while prepgen is not None:
            try:
                next(prepgen)
            except StopIteration:
                prepgen = None
        pr.step("vector", [
            lambda e: e.tensor_max(sb["mfin"][:], sb["zp"][:, 0:16], sb["zp"][:, 16:32]),
            lambda e: e.tensor_max(sb["mfin"][:], sb["mfin"][:], sb["zp"][:, 32:48]),
            lambda e: e.tensor_max(sb["mfin"][:], sb["mfin"][:], sb["zp"][:, 48:64]),
        ], reads=[b_zp], writes=[b_mfin])
        pr.step("vector", [
            lambda e: e.tensor_scalar(sb["aug_f"][:], sb["mfin"][:], -1.0, None, OP.mult),
            lambda e: e.tensor_copy(sb["aughl_b"][:, 0:16], sb["aug_f"][:]),
        ], reads=[b_mfin], writes=[b_misc["aug"], b_misc["aughl"]])
        pr.step("vector", [
            lambda e: e.tensor_copy(sb["auglo_f"][:], sb["aughl_b"][:, 0:16]),
            lambda e: e.tensor_sub(sb["auglo_f"][:], sb["aug_f"][:], sb["auglo_f"][:]),
            lambda e: e.tensor_copy(sb["aughl_b"][:, 16:32], sb["auglo_f"][:]),
        ], reads=[b_misc["aug"], b_misc["aughl"]], writes=[b_misc["aughl"]])
        TRANS(pA2b[0:32, 1024:1152], sb["aughl_b"][:], [b_misc["aughl"]], [b_pC])
        pr.step("scalar", lambda e: e.copy(sb["mrow_b"][:], pA2b[0:32, 1024:1152]),
                reads=[b_pC], writes=[b_misc["mrow"]])
        DMA(q_t[64:65, :], sb["mrow_b"][0:16, :], reads=[b_misc["mrow"]],
            writes=[bq_, b_yfm[2 * s + hh]])
        DMA(q_t[65:66, :], sb["mrow_b"][16:32, :], reads=[b_misc["mrow"]],
            writes=[bq_, b_yfm[2 * s + hh]])

    b_swcols = B("swcols")
    b_swr = B("swr")
    b_rsum = {nm: B("rsum_" + nm) for nm in ("wg", "wu", "wd")}
    RSCOL = {"wg": 44, "wu": 45, "wd": 46}

    def gen_passA(nm):
        mcol = RSCOL[nm]
        pr.step("vector", lambda e, mcol=mcol: e.memset(sb["sm"][:, mcol:mcol + 1], 0.0),
                writes=[b_rsum[nm]])
        if nm == "wd":
            pieces = [dr["wd"][128 * ch:128 * (ch + 1), 1024 * p:1024 * (p + 1)]
                      for ch in range(8) for p in range(4)]
        else:
            pieces = [dr[nm][128 * ch:128 * (ch + 1), :] for ch in range(32)]
        for src in pieces:
            DMA(sb["stats_scrap"][:], src, reads=[b_scrap], writes=[b_scrap],
                chain=("pAg" if nm == "wg" else None),
                queue=("qgp" if nm == "wg" else "qsync"))
            pr.step("vector", [
                lambda e: e.reduce_sum(sb["sm"][:, 47:48], sb["stats_scrap"][:], axis=AX,
                                       apply_absolute_value=True),
                lambda e, mcol=mcol: e.tensor_add(sb["sm"][:, mcol:mcol + 1],
                                                  sb["sm"][:, mcol:mcol + 1], sb["sm"][:, 47:48]),
            ], reads=[b_scrap, b_rsum[nm], b_misc["rscr"]], writes=[b_rsum[nm], b_misc["rscr"]])
            yield

    def finalize_passA(nm, m):
        mcol = RSCOL[nm]
        pr.step("tensor", lambda e, mcol=mcol: e.matmul(
                    pA1[0:1, 0:1], sb["ones_col_f"][:], sb["sm"][:, mcol:mcol + 1],
                    start=True, stop=True),
                reads=[b_rsum[nm], b_const], writes=[b_pA1L])
        pr.step("scalar", lambda e: e.copy(sb["sm"][0:1, 47:48], pA1[0:1, 0:1]),
                reads=[b_pA1L], writes=[B("meansc")])
        pr.step("vector", [
            lambda e: e.tensor_scalar(sb["sm"][0:1, 47:48], sb["sm"][0:1, 47:48],
                                      1.0 / (2 ** 22), None, OP.mult),
            lambda e: e.tensor_scalar(sb["sm"][0:1, 47:48], sb["sm"][0:1, 47:48],
                                      1e-5, None, OP.max),
            lambda e: e.reciprocal(sb["sm"][0:1, 48:49], sb["sm"][0:1, 47:48]),
            lambda e: e.tensor_copy(sb["swr"][:, 0:1], sb["sm"][0:1, 48:49]),
            lambda e: e.tensor_copy(sb["swr"][:, 1:2], sb["sm"][0:1, 47:48]),
        ], reads=[B("meansc")], writes=[b_swr, B("meansc")])
        pr.step("tensor", lambda e: e.matmul(pA1[:, 0:2], sb["ones_row_r"][:], sb["swr"][:, 0:2],
                                             start=True, stop=True),
                reads=[b_swr, b_const], writes=[b_pA1L])
        pr.step("vector", lambda e, m=m: e.tensor_copy(sb["swcols"][:, 2 * m:2 * m + 2],
                                                       pA1[:, 0:2]),
                reads=[b_pA1L], writes=[b_swcols])

    def _tern_round(scol_ap, rd):
        pr.step("vector", lambda e, scol_ap=scol_ap: e.tensor_scalar(
                    sb["stats_scrap"][:], sb["stats_scrap"][:], scol_ap, MAGIC, OP.mult, OP.add),
                reads=list(rd) + [b_scrap], writes=[b_scrap])
        pr.step("vector", lambda e: e.tensor_scalar(sb["scrap_b"][:], sb["stats_scrap"][:],
                                                    MAGIC, None, OP.subtract),
                reads=[b_scrap], writes=[b_scrapb])
        pr.step("vector", lambda e: e.tensor_scalar(sb["scrap_b"][:], sb["scrap_b"][:],
                                                    1.0, -1.0, OP.min, OP.max),
                reads=[b_scrapb], writes=[b_scrapb])

    def gen_tern_gu(nm, m, hbm, bh):
        for ob in range(8):
            for rc in range(4):
                DMA(sb["stats_scrap"][:], dr[nm][512 * ob + 128 * rc:512 * ob + 128 * (rc + 1), :],
                    reads=[b_scrap], writes=[b_scrap])
                yield
                _tern_round(sb["swcols"][:, 2 * m:2 * m + 1], [b_swcols])
                for j in range(ND):
                    TRANS(pA2b[:, 1024 + 128 * j:1024 + 128 * (j + 1)],
                          sb["scrap_b"][:, 128 * j:128 * (j + 1)], [b_scrapb], [b_pC])
                pr.step("scalar", lambda e: e.copy(sb["scrap_b"][:], pA2b[:, 1024:2048]),
                        reads=[b_pC], writes=[b_scrapb])
                dst = hbm[:, 512 * ob + 128 * rc:512 * ob + 128 * (rc + 1)].rearrange(
                    "(j p) c -> p j c", p=128)
                DMA(dst, sb["scrap_b"][:].rearrange("p (j c) -> p j c", j=8),
                    reads=[b_scrapb], writes=[bh])
                yield

    def gen_tern_d():
        for k in range(NI):
            for r in range(ND):
                DMA(sb["stats_scrap"][:, 128 * r:128 * (r + 1)],
                    dr["wd"][128 * r:128 * (r + 1), 128 * k:128 * (k + 1)],
                    reads=[b_scrap], writes=[b_scrap])
            yield
            _tern_round(sb["swcols"][:, 4:5], [b_swcols])
            for r in range(ND):
                TRANS(pA2b[:, 1024 + 128 * r:1024 + 128 * (r + 1)],
                      sb["scrap_b"][:, 128 * r:128 * (r + 1)], [b_scrapb], [b_pC])
            pr.step("scalar", lambda e: e.copy(sb["scrap_b"][:], pA2b[:, 1024:2048]),
                    reads=[b_pC], writes=[b_scrapb])
            DMA(wdT_hbm[128 * k:128 * (k + 1), :], sb["scrap_b"][:],
                reads=[b_scrapb], writes=[b_misc["wdT_hbm"]])
            yield

    def gen_window_end(w):
        if not SIM_FP64:
            if w == 0:
                finalize_passA("wg", 0)
                finalize_passA("wu", 1)
            elif w == 1:
                finalize_passA("wd", 2)
        pr.step("vector", [
            lambda e: e.tensor_single_scalar(sb["S1f"][:], sb["S1f"][:], 1e-6, OP.add),
            lambda e: e.reciprocal(sb["S1f"][:], sb["S1f"][:]),
        ], reads=[b_S1f], writes=[b_S1f])
        pr.step("vector", lambda e: e.tensor_scalar(sb["S1_b"][:], sb["S1f"][:], -1e-6, 1.0,
                                                    OP.mult, OP.add),
                reads=[b_S1f], writes=[b_S1])
        pr.step("vector", lambda e: e.tensor_copy(sb["srow_b"][0:16, :], sb["S1f"][:]),
                reads=[b_S1f], writes=[b_rowb[0]])
        DMA(sb["ptb0"][0:16, 0:1024], sb["srow_b"][64:80, 0:1024],
            reads=[b_misc["selrow"]], writes=[b_ptb[0]])
        for oc in range(ND):
            for tc in range(4):
                reg, breg = ((pA2[:, 512:1024], b_pC) if tc % 2 == 0
                             else (pB[:, 0:512], b_pBL))
                pr.step("tensor", lambda e, oc=oc, tc=tc, reg=reg: e.matmul(
                            reg, sb["ptb0"][0:16, 128 * oc:128 * (oc + 1)],
                            sb["srow_b"][0:16, 512 * tc:512 * (tc + 1)],
                            start=True, stop=True),
                        reads=[b_ptb[0], b_rowb[0]], writes=[breg])
                pr.step("vector", lambda e, oc=oc, tc=tc, reg=reg: e.tensor_tensor(
                            sb[f"Ofm{oc}"][:, 512 * tc:512 * (tc + 1)],
                            sb[f"Ofm{oc}"][:, 512 * tc:512 * (tc + 1)], reg, OP.mult),
                        reads=[b_O[oc], breg], writes=[b_O[oc]])
            yield
        for r in range(ND):
            st = r % 2
            DMA(sb[f"stg_f{st}"][:], dr["Wo"][w, 128 * r:128 * (r + 1), :],
                reads=[b_stg[st]], writes=[b_stg[st]])
            pr.step("vector", lambda e, st=st: e.tensor_copy(sb[f"stg_b{st}"][:], sb[f"stg_f{st}"][:]),
                    reads=[b_stg[st]], writes=[b_stgb[st]])
            for j in range(ND):
                TRANS(pBb[:, 128 * j:128 * (j + 1)], sb[f"stg_b{st}"][:, 128 * j:128 * (j + 1)],
                      [b_stgb[st]], [b_pB])
            for j in range(ND):
                pr.step("scalar", lambda e, j=j, r=r: e.copy(
                            sb[f"wot{j}"][:, 128 * r:128 * (r + 1)], pBb[:, 128 * j:128 * (j + 1)]),
                        reads=[b_pB], writes=[b_wot[j]])
            yield
        for h in range(H):
            j, ro = h // 2, 64 * (h % 2)
            for half in range(2):
                pr.step("tensor", lambda e, j=j, ro=ro, half=half: e.matmul(
                            pA1[0:1, 512 * half:512 * (half + 1)],
                            sb["bv_col_b"][ro:ro + 64, j:j + 1],
                            sb[f"wot{j}"][ro:ro + 64, 512 * half:512 * (half + 1)],
                            start=True, stop=True),
                        reads=[b_misc["bvcol"], b_wot[j]], writes=[b_pA[0]])
            pr.step("scalar", lambda e: e.copy(rowslot_b(1)[:, 0:1024], pA1[0:1, 0:1024]),
                    reads=[b_pA[0]], writes=[b_rowb[1]])
            DMA(sb["WoBv_b"][h:h + 1, :], rowslot_b(1)[:, 0:1024], reads=[b_rowb[1]], writes=[b_WoBv])
            if h % 4 == 3:
                yield
        for oc in range(ND):
            for tc in range(4):
                for j in range(ND):
                    pr.step("tensor", lambda e, oc=oc, tc=tc, j=j: e.matmul(
                                pB[:, 0:512], sb[f"wot{j}"][:, 128 * oc:128 * (oc + 1)],
                                sb[f"Ofm{j}"][:, 512 * tc:512 * (tc + 1)],
                                start=(j == 0), stop=False),
                            reads=[b_wot[j], b_O[j]], writes=[b_pB])
                pr.step("tensor", lambda e, oc=oc, tc=tc: e.matmul(
                            pB[:, 0:512], sb["WoBv_b"][:, 128 * oc:128 * (oc + 1)],
                            sb["S1_b"][:, 512 * tc:512 * (tc + 1)],
                            start=False, stop=True),
                        reads=[b_WoBv, b_S1], writes=[b_pB])
                pr.step("scalar", lambda e, oc=oc, tc=tc: e.activation(
                            sb[f"yfm{oc}"][:, 512 * tc:512 * (tc + 1)], pB[:, 0:512], AF.Identity,
                            bias=sb["bias_col"][:, 16 + oc:16 + oc + 1]),
                        reads=[b_pB, b_bias], writes=yfm_writes(oc))
            yield
        for tc in range(4):
            for oc in range(ND):
                pr.step("tensor", lambda e, oc=oc, tc=tc: e.matmul(
                            pA1[0:1, 0:512], sb["ones_col_b"][:],
                            sb[f"yfm{oc}"][:, 512 * tc:512 * (tc + 1)],
                            start=(oc == 0), stop=(oc == ND - 1)),
                        reads=[b_yfm[oc], b_const], writes=[b_pA[0]])
            for oc in range(ND):
                sc = oc % 2
                pr.step("scalar", lambda e, oc=oc, tc=tc, sc=sc: e.activation(
                            sb["scrap_b"][:, 512 * sc:512 * (sc + 1)],
                            sb[f"yfm{oc}"][:, 512 * tc:512 * (tc + 1)], AF.Square),
                        reads=[b_yfm[oc]], writes=[b_scrapb])
                pr.step("tensor", lambda e, oc=oc, sc=sc: e.matmul(
                            pA1[0:1, 512:1024], sb["ones_col_b"][:],
                            sb["scrap_b"][:, 512 * sc:512 * (sc + 1)],
                            start=(oc == 0), stop=(oc == ND - 1)),
                        reads=[b_scrapb, b_const], writes=[b_pA[0]])
            pr.step("scalar", [lambda e, tc=tc: e.copy(rowslot_f(0)[:, 512 * tc:512 * (tc + 1)], pA1[0:1, 0:512]),
                               lambda e, tc=tc: e.copy(rowslot_f(1)[:, 512 * tc:512 * (tc + 1)], pA1[0:1, 512:1024])],
                    reads=[b_pA[0]], writes=[b_rowf[0], b_rowf[1]])
            yield
        DMA(sb["r16a"][:], rowslot_f(0), reads=[b_rowf[0]], writes=[b_r16["a"]])
        DMA(sb["r16b"][:], rowslot_f(1), reads=[b_rowf[1]], writes=[b_r16["b"]])
        pr.step("vector", [
            lambda e: e.tensor_scalar(sb["r16a"][:], sb["r16a"][:], 1.0 / D, None, OP.mult),
            lambda e: e.tensor_scalar(sb["r16b"][:], sb["r16b"][:], 1.0 / D, None, OP.mult),
            lambda e: e.tensor_mul(sb["r16c"][:], sb["r16a"][:], sb["r16a"][:]),
            lambda e: e.tensor_sub(sb["r16b"][:], sb["r16b"][:], sb["r16c"][:]),
        ], reads=[b_r16["a"], b_r16["b"]], writes=[b_r16["a"], b_r16["b"], b_r16["c"]])
        yield
        pr.step("scalar", lambda e: e.activation(sb["r16b"][:], sb["r16b"][:], AF.Sqrt,
                                                 bias=sb["eps_col"][0:16, :]),
                reads=[b_r16["b"], b_const], writes=[b_r16["b"]])
        pr.step("vector", [
            lambda e: e.reciprocal(sb["r16b"][:], sb["r16b"][:]),
            lambda e: e.tensor_mul(sb["r16c"][:], sb["r16a"][:], sb["r16b"][:]),
            lambda e: e.tensor_scalar(sb["r16c"][:], sb["r16c"][:], -1.0, None, OP.mult),
            lambda e: e.tensor_copy(sb["r16r"][:], sb["r16b"][:]),
            lambda e: e.tensor_copy(sb["r16r2"][:], sb["r16c"][:]),
        ], reads=[b_r16["b"], b_r16["c"]], writes=[b_r16["b"], b_r16["c"], b_r16r[0], b_r16r[1]])
        DMA(sb["Br_r"][0:1, :], sb["r16r"][:], reads=[b_r16r[0]], writes=[b_Br])
        DMA(sb["Bmu_r"][0:1, :], sb["r16r2"][:], reads=[b_r16r[1]], writes=[b_Bmu])
        for tc in range(4):
            pr.step("tensor", lambda e, tc=tc: e.matmul(
                        pA2[:, 0:512], sb["Abc_r"][:], sb["Br_r"][:, 512 * tc:512 * (tc + 1)],
                        start=True, stop=True),
                    reads=[b_Br, b_Abc], writes=[b_pA[1]])
            for oc in range(ND):
                pr.step("vector", lambda e, oc=oc, tc=tc: e.scalar_tensor_tensor(
                            sb[f"yfm{oc}"][:, 512 * tc:512 * (tc + 1)],
                            sb[f"yfm{oc}"][:, 512 * tc:512 * (tc + 1)],
                            sb["lnw_col"][:, oc:oc + 1], pA2[:, 0:512], OP.mult, OP.mult),
                        reads=[b_yfm[oc], b_lnw, b_pA[1]], writes=yfm_writes(oc))
            yield
        for k in range(ND):
            st = k % 2
            for r in range(ND):
                DMA(sb[f"stg_f{st}"][:, 128 * r:128 * (r + 1)],
                    dr["Wout"][128 * r:128 * (r + 1), 1024 * w + 128 * k:1024 * w + 128 * (k + 1)],
                    reads=[b_stg[st]], writes=[b_stg[st]])
            pr.step("vector", lambda e, st=st: e.tensor_copy(sb[f"stg_b{st}"][:], sb[f"stg_f{st}"][:]),
                    reads=[b_stg[st]], writes=[b_stgb[st]])
            for r in range(ND):
                TRANS(pBb[:, 128 * r:128 * (r + 1)], sb[f"stg_b{st}"][:, 128 * r:128 * (r + 1)],
                      [b_stgb[st]], [b_pB])
            pr.step("scalar", lambda e, k=k: e.copy(sb[f"wot{k}"][:], pBb[:, 0:1024]),
                    reads=[b_pB], writes=[b_wot[k]])
            yield
        for (base, slot) in ((0, 2), (8, 3)):
            for k in range(ND):
                for half in range(2):
                    pr.step("tensor", lambda e, k=k, half=half, base=base: e.matmul(
                                pA1[0:1, 512 * half:512 * (half + 1)],
                                sb["gcol_b"][:, base + k:base + k + 1],
                                sb[f"wot{k}"][:, 512 * half:512 * (half + 1)],
                                start=(k == 0), stop=(k == ND - 1)),
                            reads=[b_misc["gcolb"], b_wot[k]], writes=[b_pA[0]])
            pr.step("scalar", lambda e: e.copy(rowslot_f(2)[:, 0:1024], pA1[0:1, 0:1024]),
                    reads=[b_pA[0]], writes=[b_rowf[2]])
            if slot == 3 and w == 0:
                DMA(rowslot_f(2)[:, 1024:2048], dr["bout"], reads=[b_rowf[2]], writes=[b_rowf[2]])
                pr.step("vector", lambda e: e.tensor_add(rowslot_f(2)[:, 0:1024],
                                                         rowslot_f(2)[:, 0:1024], rowslot_f(2)[:, 1024:2048]),
                        reads=[b_rowf[2]], writes=[b_rowf[2]])
            DMA(sb["r16c"][0:8, :], rowslot_f(2)[:, 0:1024], reads=[b_rowf[2]], writes=[b_r16["c"]])
            pr.step("vector", lambda e: e.tensor_copy(sb["r16r"][0:8, :], sb["r16c"][0:8, :]),
                    reads=[b_r16["c"]], writes=[b_r16r[0]])
            DMA(sb["Awgwb_r"][slot - 2:slot - 1, :], sb["r16r"][0:8, :],
                reads=[b_r16r[0]], writes=[b_Awgwb])
            yield
        for oc in range(ND):
            for tc in range(4):
                for k in range(ND):
                    pr.step("tensor", lambda e, oc=oc, tc=tc, k=k: e.matmul(
                                pB[:, 0:512], sb[f"wot{k}"][:, 128 * oc:128 * (oc + 1)],
                                sb[f"yfm{k}"][:, 512 * tc:512 * (tc + 1)],
                                start=(k == 0), stop=False),
                            reads=[b_wot[k], b_yfm[k]], writes=[b_pB])
                pr.step("tensor", lambda e, oc=oc, tc=tc: e.matmul(
                            pB[:, 0:512], sb["Awgwb_r"][:, 128 * oc:128 * (oc + 1)],
                            sb["Bmu_r"][:, 512 * tc:512 * (tc + 1)],
                            start=False, stop=True),
                        reads=[b_Awgwb, b_Bmu], writes=[b_pB])
                sc = tc % 2
                pr.step("vector", lambda e, sc=sc: e.tensor_copy(
                            sb["stats_scrap"][:, 512 * sc:512 * (sc + 1)], pB[:, 0:512]),
                        reads=[b_pB], writes=[b_scrap])
                if w == 0:
                    DMA(z_hbm[128 * oc:128 * (oc + 1), 512 * tc:512 * (tc + 1)],
                        sb["stats_scrap"][:, 512 * sc:512 * (sc + 1)],
                        reads=[b_scrap], writes=[b_zhbm[oc]])
                else:
                    GDMA(z_hbm[128 * oc:128 * (oc + 1), 512 * tc:512 * (tc + 1)],
                         sb["stats_scrap"][:, 512 * sc:512 * (sc + 1)], "gz",
                         reads=[b_scrap], writes=[b_zhbm[oc]])
            yield

    def window(w, terngen=None, nextfill=None, first=False):
        DMA(sb["bias_col"][:, 0:8], dr["bq"][w].rearrange("(j p) -> p j", p=128),
            reads=[b_bias], writes=[b_bias])
        DMA(sb["bk_stage"][:], dr["bk"][w].rearrange("(j p) -> p j", p=128),
            reads=[b_misc["bkraw"]], writes=[b_misc["bkraw"]])
        pr.step("vector", lambda e: e.tensor_copy(sb["bias_col"][:, 8:16], sb["bk_stage"][:]),
                reads=[b_misc["bkraw"]], writes=[b_bias])
        DMA(sb["bias_col"][:, 16:24], dr["bo"][w].rearrange("(j p) -> p j", p=128),
            reads=[b_bias], writes=[b_bias])
        DMA(sb["bk_stage"][:], dr["bv"][w].rearrange("(j p) -> p j", p=128),
            reads=[b_misc["bkraw"]], writes=[b_misc["bkraw"]])
        pr.step("vector", lambda e: e.tensor_copy(sb["bv_col_b"][:], sb["bk_stage"][:]),
                reads=[b_misc["bkraw"]], writes=[b_misc["bvcol"]])
        DMA(sb["lnw_col"][:, 0:8], dr["lng"][w].rearrange("(j p) -> p j", p=128),
            reads=[b_lnw], writes=[b_lnw])
        DMA(sb["lnw_col"][:, 8:16], dr["lnb"][w].rearrange("(j p) -> p j", p=128),
            reads=[b_lnw], writes=[b_lnw])
        pr.step("vector", [lambda e: e.tensor_copy(sb["gcol_b"][:, 0:8], sb["lnw_col"][:, 0:8]),
                           lambda e: e.tensor_copy(sb["gcol_b"][:, 8:16], sb["lnw_col"][:, 8:16])],
                reads=[b_lnw], writes=[b_misc["gcolb"]])
        for i4 in range(4):
            pr.step("vector", lambda e, i4=i4: e.memset(sb[f"Kt{i4}"][64:66, :], 1.0),
                    writes=[b_Kt[i4], b_yfm[4 + i4]])
        if first:
            for _ in gen_wprep(w, 0):
                pass
            for _ in gen_proj(w, 0, "all"):
                pass
        else:
            for _ in gen_proj(w, 0, "qk"):
                pass
        p2g = None
        prepg = gen_wprep(w, 1)
        for h in range(16):
            g = h // 2
            slot(w, h, p2g, prepg, terngen)
            p2g = gen_p2pv(w, g, h % 2)
            if h % 2 == 0:
                prepg = gen_proj(w, g + 1) if g + 1 < 8 else None
            else:
                prepg = gen_wprep(w, g + 2) if g + 2 < 8 else None
        wpf = (nextfill or [None])[0]
        while p2g is not None:
            try:
                next(p2g)
            except StopIteration:
                p2g = None
            if wpf is not None:
                try:
                    next(wpf)
                except StopIteration:
                    wpf = None
        if terngen is not None:
            for _ in terngen:
                pass
        import collections as _c
        fillers = _c.deque(g for g in (nextfill or []) if g is not None)
        for _ in gen_window_end(w):
            if fillers:
                try:
                    next(fillers[0])
                except StopIteration:
                    fillers.popleft()
        while fillers:
            g0 = fillers.popleft()
            for _ in g0:
                pass

    import itertools
    if not SIM_FP64:
        startfill = itertools.chain(gen_mod(), gen_passA("wg"))
        tern_w = [
            itertools.chain(gen_passA("wu")),
            itertools.chain(gen_passA("wd"), gen_tern_gu("wg", 0, wgT_hbm, b_misc["wgT_hbm"])),
            itertools.chain(gen_tern_gu("wu", 1, wuT_hbm, b_misc["wuT_hbm"]), gen_tern_d()),
        ]
    else:
        startfill = gen_mod()
        tern_w = [None, None, None]
    emit_ln1(startfill)
    for s in range(2):
        pr.step("vector", [
            lambda e, s=s: e.memset(sb[f"Vp{s}"][:].rearrange("p (t u) -> p t u", u=130)[:, :, 64:65], 1.0),
            lambda e, s=s: e.memset(sb[f"Vp{s}"][:].rearrange("p (t u) -> p t u", u=130)[:, :, 129:130], 1.0),
        ], writes=[b_Vp[s], b_misc["x1t"], b_misc["x2t"]])
    for j in range(ND):
        pr.step("scalar", lambda e, j=j: e.activation(
                    sb[f"h1_{j}"][:], sb[f"h1_{j}"][:], AF.Identity,
                    scale=sb["modx_col"][:, j:j + 1], bias=sb["mod_col"][:, j:j + 1]),
                reads=[b_h1[j], b_modx, b_modcol], writes=[b_h1[j]])
    for w in range(NW):
        nextfill = ([gen_wprep(w + 1, 0), gen_proj(w + 1, 0, "v", vstage="ptb1")]
                    if w + 1 < NW else [])
        window(w, tern_w[w], nextfill, first=(w == 0))


    def fm_stats(srcs, bsrcs):
        for tc in range(4):
            for oc in range(ND):
                pr.step("tensor", lambda e, oc=oc, tc=tc: e.matmul(
                            pA1[0:1, 0:512], sb["ones_col_b"][:],
                            srcs[oc][:, 512 * tc:512 * (tc + 1)],
                            start=(oc == 0), stop=(oc == ND - 1)),
                        reads=[bsrcs[oc], b_const], writes=[b_pA[0]])
            for oc in range(ND):
                sc = oc % 2
                pr.step("scalar", lambda e, oc=oc, tc=tc, sc=sc: e.activation(
                            sb["scrap_b"][:, 512 * sc:512 * (sc + 1)],
                            srcs[oc][:, 512 * tc:512 * (tc + 1)], AF.Square),
                        reads=[bsrcs[oc]], writes=[b_scrapb])
                pr.step("tensor", lambda e, oc=oc, sc=sc: e.matmul(
                            pA1[0:1, 512:1024], sb["ones_col_b"][:],
                            sb["scrap_b"][:, 512 * sc:512 * (sc + 1)],
                            start=(oc == 0), stop=(oc == ND - 1)),
                        reads=[b_scrapb, b_const], writes=[b_pA[0]])
            pr.step("scalar", [lambda e, tc=tc: e.copy(rowslot_f(0)[:, 512 * tc:512 * (tc + 1)], pA1[0:1, 0:512]),
                               lambda e, tc=tc: e.copy(rowslot_f(1)[:, 512 * tc:512 * (tc + 1)], pA1[0:1, 512:1024])],
                    reads=[b_pA[0]], writes=[b_rowf[0], b_rowf[1]])
        DMA(sb["r16a"][:], rowslot_f(0), reads=[b_rowf[0]], writes=[b_r16["a"]])
        DMA(sb["r16b"][:], rowslot_f(1), reads=[b_rowf[1]], writes=[b_r16["b"]])
        pr.step("vector", [
            lambda e: e.tensor_scalar(sb["r16a"][:], sb["r16a"][:], 1.0 / D, None, OP.mult),
            lambda e: e.tensor_scalar(sb["r16b"][:], sb["r16b"][:], 1.0 / D, None, OP.mult),
            lambda e: e.tensor_mul(sb["r16c"][:], sb["r16a"][:], sb["r16a"][:]),
            lambda e: e.tensor_sub(sb["r16b"][:], sb["r16b"][:], sb["r16c"][:]),
        ], reads=[b_r16["a"], b_r16["b"]], writes=[b_r16["a"], b_r16["b"], b_r16["c"]])
        pr.step("scalar", lambda e: e.activation(sb["r16b"][:], sb["r16b"][:], AF.Sqrt,
                                                 bias=sb["eps_col"][0:16, :]),
                reads=[b_r16["b"], b_const], writes=[b_r16["b"]])
        pr.step("vector", [
            lambda e: e.reciprocal(sb["r16b"][:], sb["r16b"][:]),
            lambda e: e.tensor_mul(sb["r16c"][:], sb["r16a"][:], sb["r16b"][:]),
            lambda e: e.tensor_scalar(sb["r16c"][:], sb["r16c"][:], -1.0, None, OP.mult),
            lambda e: e.tensor_copy(sb["r16r"][:], sb["r16b"][:]),
            lambda e: e.tensor_copy(sb["r16r2"][:], sb["r16c"][:]),
        ], reads=[b_r16["b"], b_r16["c"]], writes=[b_r16["b"], b_r16["c"], b_r16r[0], b_r16r[1]])
        DMA(sb["Br_r"][0:1, :], sb["r16r"][:], reads=[b_r16r[0]], writes=[b_Br])
        DMA(sb["Bmu_r"][0:1, :], sb["r16r2"][:], reads=[b_r16r[1]], writes=[b_Bmu])

    def col8_to_Br(src_ap, rd):
        pr.step("vector", lambda e: e.tensor_copy(sb["aughl_b"][:, 0:8], src_ap),
                reads=rd, writes=[b_misc["aughl"]])
        TRANS(pBb[0:8, 0:128], sb["aughl_b"][:, 0:8], [b_misc["aughl"]], [b_pB])
        pr.step("scalar", lambda e: e.copy(sb["mrow_b"][0:8, :], pBb[0:8, 0:128]),
                reads=[b_pB], writes=[b_misc["mrow"]])
        pr.step("vector", lambda e: e.tensor_copy(sb["r16r"][0:8, :], sb["mrow_b"][0:8, :]),
                reads=[b_misc["mrow"]], writes=[b_r16r[0]])
        DMA(sb["Br_r"][0:1, 0:1024], sb["r16r"][0:8, :], reads=[b_r16r[0]], writes=[b_Br])

    for srccols, tile, btile, alw, plus1 in (
            (sb["mod_col"][:, 32:40], "cb1t", b_misc["cb1t"], b_wt["k"], True),
            (sb["mod_col"][:, 24:32], "sbt", b_misc["sbt"], b_wt["v"], False),
            (sb["modx_col"][:, 24:32], "gbt", b_misc["gbt"], b_wt["q"], False)):
        col8_to_Br(srccols, [b_modcol, b_modx])
        for hf in range(2):
            pr.step("tensor", lambda e, hf=hf: e.matmul(
                        pA1[:, 512 * hf:512 * (hf + 1)], sb["Abc_r"][:],
                        sb["Br_r"][:, 512 * hf:512 * (hf + 1)], start=True, stop=True),
                    reads=[b_Br, b_Abc], writes=[b_pA[0]])
        if plus1:
            pr.step("vector", lambda e, tile=tile: e.tensor_scalar(sb[tile][:], pA1[:, 0:1024],
                                                                   1.0, None, OP.add),
                    reads=[b_pA[0]], writes=[btile] + list(alw))
        else:
            pr.step("vector", lambda e, tile=tile: e.tensor_copy(sb[tile][:], pA1[:, 0:1024]),
                    reads=[b_pA[0]], writes=[btile] + list(alw))

    for oc in range(ND):
        for hf in range(2):
            s2_ = hf % 2
            DMA(sb[f"stg_f{s2_}"][:], z_hbm[128 * oc:128 * (oc + 1), 1024 * hf:1024 * (hf + 1)],
                reads=[b_stg[s2_], b_zhbm[oc]], writes=[b_stg[s2_]])
            pr.step("vector", lambda e, oc=oc, hf=hf, s2_=s2_: e.tensor_copy(
                        sb[f"yfm{oc}"][:, 1024 * hf:1024 * (hf + 1)], sb[f"stg_f{s2_}"][:]),
                    reads=[b_stg[s2_]], writes=yfm_writes(oc))

    fm_stats([sb[f"yfm{j}"] for j in range(ND)], b_yfm)
    for gi, srccols in enumerate((sb["modx_col"][:, 32:40], sb["modx_col"][:, 40:48])):
        pr.step("vector", lambda e, srccols=srccols: e.tensor_copy(sb["aughl_b"][:, 0:8], srccols),
                reads=[b_modx], writes=[b_misc["aughl"]])
        TRANS(pBb[0:8, 0:128], sb["aughl_b"][:, 0:8], [b_misc["aughl"]], [b_pB])
        pr.step("scalar", lambda e: e.copy(sb["mrow_b"][0:8, :], pBb[0:8, 0:128]),
                reads=[b_pB], writes=[b_misc["mrow"]])
        pr.step("vector", lambda e: e.tensor_copy(sb["r16r"][0:8, :], sb["mrow_b"][0:8, :]),
                reads=[b_misc["mrow"]], writes=[b_r16r[0]])
        DMA(sb["Awgwb_r"][gi:gi + 1, 0:1024], sb["r16r"][0:8, :], reads=[b_r16r[0]], writes=[b_Awgwb])

    for tc in range(4):
        pr.step("tensor", lambda e, tc=tc: e.matmul(
                    pA2[:, 0:512], sb["Abc_r"][:], sb["Br_r"][:, 512 * tc:512 * (tc + 1)],
                    start=True, stop=True),
                reads=[b_Br, b_Abc], writes=[b_pA[1]])
        for oc in range(ND):
            pr.step("tensor", lambda e, oc=oc, tc=tc: e.matmul(
                        pA1[:, 0:512], sb["Awgwb_r"][:, 128 * oc:128 * (oc + 1)],
                        sb["Bmu_r"][:, 512 * tc:512 * (tc + 1)], start=True, stop=True),
                    reads=[b_Awgwb, b_Bmu], writes=[b_pA[0]])
            pr.step("vector", lambda e, oc=oc, tc=tc: e.scalar_tensor_tensor(
                        sb["stats_scrap"][:, 0:512], sb[f"yfm{oc}"][:, 512 * tc:512 * (tc + 1)],
                        sb["modx_col"][:, 32 + oc:32 + oc + 1], pA2[:, 0:512], OP.mult, OP.mult),
                    reads=[b_yfm[oc], b_modx, b_pA[1]], writes=[b_scrap])
            pr.step("vector", lambda e, oc=oc, tc=tc: e.tensor_add(
                        sb[f"yfm{oc}"][:, 512 * tc:512 * (tc + 1)],
                        sb["stats_scrap"][:, 0:512], pA1[:, 0:512]),
                    reads=[b_scrap, b_pA[0]], writes=yfm_writes(oc))

    for i in range(NT):
        s = i % 2
        bs = 32 + 20 * (i % 2)
        bl = b_ln1r[i % 2]
        DMA(sb[f"stg_f{s}"][:], dr["x"][128 * i:128 * (i + 1), :], reads=[b_stg[s]], writes=[b_stg[s]])
        for j in range(ND):
            TRANS(pBb[:, 128 * j:128 * (j + 1)], sb[f"yfm{j}"][:, 128 * i:128 * (i + 1)],
                  [b_yfm[j]], [b_pB])
        pr.step("vector", lambda e, s=s: e.tensor_add(sb["x1t"][:], sb[f"stg_f{s}"][:], pBb[:, 0:1024]),
                reads=[b_stg[s], b_pB], writes=[b_misc["x1t"]] + VP_AL)
        pr.step("vector", lambda e, bs=bs: e.reduce_sum(sb["sm"][:, bs:bs + 1], sb["x1t"][:], axis=AX),
                reads=[b_misc["x1t"]], writes=[bl])
        pr.step("scalar", lambda e, bs=bs: e.activation(sb["stats_scrap"][:], sb["x1t"][:], AF.Square,
                                                 accum_out=sb["sm"][:, bs + 1:bs + 2]),
                reads=[b_misc["x1t"]], writes=[b_scrap, bl])
        pr.step("vector", [
            lambda e, bs=bs: e.tensor_scalar(sb["sm"][:, bs + 2:bs + 3], sb["sm"][:, bs:bs + 1], 1.0 / D, None, OP.mult),
            lambda e, bs=bs: e.tensor_scalar(sb["sm"][:, bs + 3:bs + 4], sb["sm"][:, bs + 1:bs + 2], 1.0 / D, None, OP.mult),
            lambda e, bs=bs: e.tensor_mul(sb["sm"][:, bs + 4:bs + 5], sb["sm"][:, bs + 2:bs + 3], sb["sm"][:, bs + 2:bs + 3]),
            lambda e, bs=bs: e.tensor_sub(sb["sm"][:, bs + 3:bs + 4], sb["sm"][:, bs + 3:bs + 4], sb["sm"][:, bs + 4:bs + 5]),
        ], reads=[bl], writes=[bl])
        pr.step("scalar", lambda e, bs=bs: e.activation(sb["sm"][:, bs + 5:bs + 6], sb["sm"][:, bs + 3:bs + 4], AF.Sqrt,
                                                 bias=sb["eps_col"][:]),
                reads=[bl, b_const], writes=[bl])
        pr.step("vector", [
            lambda e, bs=bs: e.reciprocal(sb["sm"][:, bs + 6:bs + 7], sb["sm"][:, bs + 5:bs + 6]),
            lambda e, bs=bs: e.tensor_mul(sb["sm"][:, bs + 7:bs + 8], sb["sm"][:, bs + 2:bs + 3], sb["sm"][:, bs + 6:bs + 7]),
            lambda e, bs=bs: e.tensor_scalar(sb["sm"][:, bs + 7:bs + 8], sb["sm"][:, bs + 7:bs + 8], -1.0, None, OP.mult),
        ], reads=[bl], writes=[bl])
        pr.step("scalar", lambda e, bs=bs: e.activation(sb["stats_scrap"][:], sb["x1t"][:], AF.Identity,
                                                 scale=sb["sm"][:, bs + 6:bs + 7], bias=sb["sm"][:, bs + 7:bs + 8]),
                reads=[b_misc["x1t"], bl], writes=[b_scrap])
        DMA(yout[128 * i:128 * (i + 1), :], sb["x1t"][:], reads=[b_misc["x1t"]], writes=[b_yout[i]])
        pr.step("vector", [
            lambda e: e.tensor_mul(sb["x2t"][:], sb["stats_scrap"][:], sb["cb1t"][:]),
            lambda e: e.tensor_add(sb["x2t"][:], sb["x2t"][:], sb["sbt"][:]),
        ], reads=[b_scrap, b_misc["cb1t"], b_misc["sbt"]], writes=[b_misc["x2t"]] + VP_AL)
        pr.step("vector", [
            lambda e, bs=bs: e.reduce_max(sb["sm"][:, bs + 8:bs + 9], sb["x2t"][:], axis=AX, apply_absolute_value=True),
            lambda e, bs=bs: e.tensor_scalar(sb["sm"][:, bs + 8:bs + 9], sb["sm"][:, bs + 8:bs + 9], 1e-5, None, OP.max),
            lambda e, bs=bs: e.reciprocal(sb["sm"][:, bs + 9:bs + 10], sb["sm"][:, bs + 8:bs + 9]),
            lambda e, bs=bs: e.tensor_scalar(sb["sm"][:, bs + 9:bs + 10], sb["sm"][:, bs + 9:bs + 10], 127.0, None, OP.mult),
        ], reads=[b_misc["x2t"]], writes=[bl])
        pr.step("vector", lambda e, i=i, bs=bs: e.tensor_scalar(sb["qscl"][:, i:i + 1], sb["sm"][:, bs + 8:bs + 9],
                                                         1.0 / 127.0, None, OP.mult),
                reads=[bl], writes=[b_qscl])
        round_to_int("vector", sb["scrap_b"][:, 0:1024], sb["x2t"][:], sb["sm"][:, bs + 9:bs + 10],
                     [b_misc["x2t"], bl], [b_scrapb], sb["stats_scrap"][:], sb[f"stg_f{1 - s}"][:],
                     tmp_bufs=[b_scrap], tmp2_bufs=[b_stg[1 - s]])
        for j in range(ND):
            TRANS(pBb[:, 128 * j:128 * (j + 1)], sb["scrap_b"][:, 128 * j:128 * (j + 1)],
                  [b_scrapb], [b_pB])
        for j in range(ND):
            pr.step("scalar", lambda e, i=i, j=j: e.copy(
                        sb[f"h2qfm{j}"][:, 128 * i:128 * (i + 1)], pBb[:, 128 * j:128 * (j + 1)]),
                    reads=[b_pB], writes=h2qfm_writes(j))

    for m, nm, nchunk in (((0, "wg", 32), (1, "wu", 32), (2, "wd", 8)) if SIM_FP64 else ()):
        pr.step("vector", lambda e: e.memset(sb["sm"][:, 45:46], 0.0), writes=[B("runsum")])
        for ch in range(nchunk):
            s = ch % 2
            if nm == "wd":
                for piece in range(4):
                    DMA(sb[f"stg_f{s}"][:], dr["wd"][128 * ch:128 * (ch + 1), 1024 * piece:1024 * (piece + 1)],
                        reads=[b_stg[s]], writes=[b_stg[s]])
                    pr.step("vector", [
                        lambda e, s=s: e.reduce_sum(sb["sm"][:, 46:47], sb[f"stg_f{s}"][:], axis=AX,
                                                    apply_absolute_value=True),
                        lambda e: e.tensor_add(sb["sm"][:, 45:46], sb["sm"][:, 45:46], sb["sm"][:, 46:47]),
                    ], reads=[b_stg[s], B("runsum")], writes=[B("runsum")])
                    s = (s + 1) % 2
            else:
                DMA(sb[f"stg_f{s}"][:], dr[nm][128 * ch:128 * (ch + 1), :],
                    reads=[b_stg[s]], writes=[b_stg[s]])
                pr.step("vector", [
                    lambda e, s=s: e.reduce_sum(sb["sm"][:, 46:47], sb[f"stg_f{s}"][:], axis=AX,
                                                apply_absolute_value=True),
                    lambda e: e.tensor_add(sb["sm"][:, 45:46], sb["sm"][:, 45:46], sb["sm"][:, 46:47]),
                ], reads=[b_stg[s], B("runsum")], writes=[B("runsum")])
        pr.step("tensor", lambda e: e.matmul(pA1[0:1, 0:1], sb["ones_col_f"][:], sb["sm"][:, 45:46],
                                             start=True, stop=True),
                reads=[B("runsum"), b_const], writes=[b_pA[0]])
        pr.step("scalar", lambda e: e.copy(sb["sm"][0:1, 47:48], pA1[0:1, 0:1]),
                reads=[b_pA[0]], writes=[B("meansc")])
        pr.step("vector", [
            lambda e: e.tensor_scalar(sb["sm"][0:1, 47:48], sb["sm"][0:1, 47:48], 1.0 / (2 ** 22), None, OP.mult),
            lambda e: e.tensor_scalar(sb["sm"][0:1, 47:48], sb["sm"][0:1, 47:48], 1e-5, None, OP.max),
            lambda e: e.reciprocal(sb["sm"][0:1, 48:49], sb["sm"][0:1, 47:48]),
            lambda e: e.tensor_copy(sb["swr"][:, 0:1], sb["sm"][0:1, 48:49]),
            lambda e: e.tensor_copy(sb["swr"][:, 1:2], sb["sm"][0:1, 47:48]),
        ], reads=[B("meansc")], writes=[b_swr, B("meansc")])
        pr.step("tensor", lambda e: e.matmul(pA1[:, 0:2], sb["ones_row_r"][:], sb["swr"][:, 0:2],
                                             start=True, stop=True),
                reads=[b_swr, b_const], writes=[b_pA[0]])
        pr.step("vector", lambda e, m=m: e.tensor_copy(sb["swcols"][:, 2 * m:2 * m + 2], pA1[:, 0:2]),
                reads=[b_pA[0]], writes=[b_swcols])

    def tern_stage(st, scol, width, rd):
        round_to_int("vector", sb[f"stg_b{st}"][:, 0:width], sb[f"stg_f{st}"][:, 0:width],
                     scol, rd, [b_stgb[st]], sb["stats_scrap"][:, 0:width], sb["x2t"][:, 0:width],
                     tmp_bufs=[b_scrap], tmp2_bufs=[b_misc["x2t"]] + VP_AL)
        pr.step("vector", lambda e, st=st, width=width: e.tensor_scalar(
                    sb[f"stg_b{st}"][:, 0:width], sb[f"stg_b{st}"][:, 0:width],
                    1.0, -1.0, OP.min, OP.max),
                reads=[b_stgb[st]], writes=[b_stgb[st]])

    for k in (range(NI) if SIM_FP64 else ()):
        st = k % 2
        for r in range(ND):
            DMA(sb[f"stg_f{st}"][:, 128 * r:128 * (r + 1)],
                dr["wd"][128 * r:128 * (r + 1), 128 * k:128 * (k + 1)],
                reads=[b_stg[st]], writes=[b_stg[st]])
        tern_stage(st, sb["swcols"][:, 4:5], 1024, [b_stg[st], b_swcols])
        for r in range(ND):
            TRANS(pBb[:, 128 * r:128 * (r + 1)], sb[f"stg_b{st}"][:, 128 * r:128 * (r + 1)],
                  [b_stgb[st]], [b_pB])
        slot = k % 16
        pr.step("scalar", lambda e, slot=slot: e.copy(sb[f"wdT{slot}"][:], pBb[:, 0:1024]),
                reads=[b_pB], writes=wdT_writes(slot))
        DMA(wdT_hbm[128 * k:128 * (k + 1), :], sb[f"wdT{slot}"][:],
            reads=[b_wdT[slot]], writes=[b_misc["wdT_hbm"]])

    for Q in range(4):
        qts = list(range(4 * Q, 4 * Q + 4))
        def gu_reload(ob):
            for wi, (wts, bwts, hbm, bh) in enumerate(
                    (("wgTs", b_wgTs, wgT_hbm, b_misc["wgT_hbm"]),
                     ("wuTs", b_wuTs, wuT_hbm, b_misc["wuT_hbm"]))):
                for j in range(ND):
                    wal = b_wot[j // 2] if wts == "wgTs" else b_wot[4 + j // 2]
                    DMA(sb[f"{wts}{j}"][:], hbm[128 * j:128 * (j + 1), 512 * ob:512 * (ob + 1)],
                        reads=[bwts[j], bh], writes=[bwts[j], wal],
                        chain=f"wgu{2 * wi + j % 2}")

        def gu_reload_j(ob, j):
            for wi, (wts, bwts, hbm, bh) in enumerate(
                    (("wgTs", b_wgTs, wgT_hbm, b_misc["wgT_hbm"]),
                     ("wuTs", b_wuTs, wuT_hbm, b_misc["wuT_hbm"]))):
                wal = b_wot[j // 2] if wts == "wgTs" else b_wot[4 + j // 2]
                DMA(sb[f"{wts}{j}"][:], hbm[128 * j:128 * (j + 1), 512 * ob:512 * (ob + 1)],
                    reads=[bwts[j], bh], writes=[bwts[j], wal],
                    chain=f"wgu{2 * wi + j % 2}")

        if Q == 0 and not SIM_FP64:
            gu_reload(0)
        for ob in range(8):
            if Q == 0 and SIM_FP64:
                for m, nm, wts, bwts, hbm, bh in ((0, "wg", "wgTs", b_wgTs, wgT_hbm, b_misc["wgT_hbm"]),
                                                  (1, "wu", "wuTs", b_wuTs, wuT_hbm, b_misc["wuT_hbm"])):
                    for rc in range(4):
                        st = rc % 2
                        DMA(sb[f"stg_f{st}"][:], dr[nm][512 * ob + 128 * rc:512 * ob + 128 * (rc + 1), :],
                            reads=[b_stg[st]], writes=[b_stg[st]])
                        tern_stage(st, sb["swcols"][:, 2 * m:2 * m + 1], 1024, [b_stg[st], b_swcols])
                        for j in range(ND):
                            TRANS(pBb[:, 128 * j:128 * (j + 1)], sb[f"stg_b{st}"][:, 128 * j:128 * (j + 1)],
                                  [b_stgb[st]], [b_pB])
                        for j in range(ND):
                            wal = b_wot[j // 2] if wts == "wgTs" else b_wot[4 + j // 2]
                            pr.step("scalar", lambda e, wts=wts, j=j, rc=rc: e.copy(
                                        sb[f"{wts}{j}"][:, 128 * rc:128 * (rc + 1)],
                                        pBb[:, 128 * j:128 * (j + 1)]),
                                    reads=[b_pB], writes=[bwts[j], wal])
                    for j in range(ND):
                        DMA(hbm[128 * j:128 * (j + 1), 512 * ob:512 * (ob + 1)], sb[f"{wts}{j}"][:],
                            reads=[bwts[j]], writes=[bh])
            elif SIM_FP64:
                gu_reload(ob)
            for qt in qts:
                qq = qt % 4
                gu_p, gu_bL, gu_bR = ((pB, b_pBL, b_pBR) if qt % 2 == 0
                                      else (pA2, b_pA2L, b_pC))
                a_c, s_c = 49 + 6 * (qt % 2), 50 + 6 * (qt % 2)
                for j in range(ND):
                    pr.step("tensor", [
                        lambda e, j=j, qt=qt, gu_p=gu_p: e.matmul(
                            gu_p[:, 0:512], sb[f"h2qfm{j}"][:, 128 * qt:128 * (qt + 1)],
                            sb[f"wgTs{j}"][:], start=(j == 0), stop=(j == ND - 1)),
                        lambda e, j=j, qt=qt, gu_p=gu_p: e.matmul(
                            gu_p[:, 512:1024], sb[f"h2qfm{j}"][:, 128 * qt:128 * (qt + 1)],
                            sb[f"wuTs{j}"][:], start=(j == 0), stop=(j == ND - 1)),
                    ], reads=[b_h2qfm[j], b_wgTs[j], b_wuTs[j]], writes=[gu_bL, gu_bR])
                    if qt == qts[-1] and not SIM_FP64 and (Q, ob) != (3, 7):
                        gu_reload_j((ob + 1) % 8, j)
                pr.step("vector", [
                    lambda e, qt=qt, a_c=a_c: e.tensor_mul(sb["sm"][:, a_c:a_c + 1],
                                                  sb["qscl"][:, qt:qt + 1], sb["swcols"][:, 1:2]),
                    lambda e, qt=qt, s_c=s_c: e.tensor_mul(sb["sm"][:, s_c:s_c + 1],
                                                  sb["qscl"][:, qt:qt + 1], sb["swcols"][:, 3:4]),
                ], reads=[b_qscl, b_swcols], writes=[B(f"alph{qt % 2}")])
                ev, bev = (("stats_scrap", [b_scrap]) if qt % 2 == 0
                           else ("x2t", [b_misc["x2t"]] + VP_AL))
                pr.step("scalar", lambda e, gu_p=gu_p, a_c=a_c, ev=ev: e.activation(
                            sb[ev][:, 0:512], gu_p[:, 0:512],
                            AF.Silu, scale=sb["sm"][:, a_c:a_c + 1]),
                        reads=[gu_bL, B(f"alph{qt % 2}")], writes=bev)
                pr.step("scalar", lambda e, gu_p=gu_p, s_c=s_c, ev=ev: e.activation(
                            sb[ev][:, 512:1024], gu_p[:, 512:1024],
                            AF.Identity, scale=sb["sm"][:, s_c:s_c + 1]),
                        reads=[gu_bR, B(f"alph{qt % 2}")], writes=bev)
                pr.step("vector", lambda e, qq=qq, ob=ob, ev=ev: e.tensor_mul(
                            sb[f"prod{qq}"][:, 512 * ob:512 * (ob + 1)],
                            sb[ev][:, 0:512], sb[ev][:, 512:1024]),
                        reads=bev, writes=prod_writes(qq))
                pr.step("vector", [
                    lambda e, qq=qq, ob=ob: e.reduce_max(sb["sm"][:, 51:52],
                                                         sb[f"prod{qq}"][:, 512 * ob:512 * (ob + 1)],
                                                         axis=AX, apply_absolute_value=True),
                    (lambda e, qq=qq: e.tensor_copy(sb["sm"][:, 52 + qq:53 + qq], sb["sm"][:, 51:52]))
                    if ob == 0 else
                    (lambda e, qq=qq: e.tensor_max(sb["sm"][:, 52 + qq:53 + qq],
                                                   sb["sm"][:, 52 + qq:53 + qq], sb["sm"][:, 51:52])),
                ], reads=[b_prod[qq]], writes=[B("pmax")])
        def wdma2(kloc, kh):
            kg = 16 * kh + kloc
            DMA(sb[f"wdT{kloc}"][:], wdT_hbm[128 * kg:128 * (kg + 1), :],
                reads=[b_wdT[kloc], b_misc["wdT_hbm"]], writes=wdT_writes(kloc),
                chain=f"wdh{kloc % 2}")
        for ti, qt in enumerate(qts):
            qq = qt % 4
            pr.step("vector", [
                lambda e, qq=qq: e.tensor_scalar(sb["sm"][:, 56:57], sb["sm"][:, 52 + qq:53 + qq],
                                                 1e-5, None, OP.max),
                lambda e: e.reciprocal(sb["sm"][:, 57:58], sb["sm"][:, 56:57]),
                lambda e: e.tensor_scalar(sb["sm"][:, 57:58], sb["sm"][:, 57:58], 127.0, None, OP.mult),
                lambda e: e.tensor_scalar(sb["sm"][:, 58:59], sb["sm"][:, 56:57], 1.0 / 127.0, None, OP.mult),
                lambda e, qq=qq: e.tensor_mul(sb["sm"][:, 60 + qq:61 + qq], sb["sm"][:, 58:59],
                                              sb["swcols"][:, 5:6]),
            ], reads=[B("pmax"), b_swcols], writes=[B("alphd"), B("s3")])
            for p4 in range(4):
                round_to_int("vector", sb["scrap_b"][:, 0:1024],
                             sb[f"prod{qq}"][:, 1024 * p4:1024 * (p4 + 1)], sb["sm"][:, 57:58],
                             [b_prod[qq], B("s3")], [b_scrapb],
                             sb["stats_scrap"][:], sb["x2t"][:],
                             tmp_bufs=[b_scrap], tmp2_bufs=[b_misc["x2t"]] + VP_AL)
                for kk in range(8):
                    TRANS(pBb[:, 128 * kk:128 * (kk + 1)], sb["scrap_b"][:, 128 * kk:128 * (kk + 1)],
                          [b_scrapb], [b_pB])
                pr.step("scalar", lambda e, qq=qq, p4=p4: e.copy(
                            sb[f"prod{qq}"][:, 1024 * p4:1024 * (p4 + 1)], pBb[:, 0:1024]),
                        reads=[b_pB], writes=prod_writes(qq))
                wdma2(4 * ti + p4, 0)
        for kh in range(2):
            for ti, qt in enumerate(qts):
                qq = qt % 4
                xd = "x2t" if (4 * kh + ti) % 2 == 0 else "x1t"
                bxd = [b_misc["x2t"]] if xd == "x2t" else [b_misc["x1t"]]
                for kloc in range(16):
                    kg = 16 * kh + kloc
                    for hf in range(2):
                        pr.step("tensor", lambda e, qq=qq, kg=kg, kloc=kloc, hf=hf: e.matmul(
                                    pA1[:, 512 * hf:512 * (hf + 1)],
                                    sb[f"prod{qq}"][:, 128 * kg:128 * (kg + 1)],
                                    sb[f"wdT{kloc}"][:, 512 * hf:512 * (hf + 1)],
                                    start=(kloc == 0), stop=(kloc == 15)),
                                reads=[b_prod[qq], b_wdT[kloc]], writes=[b_pA[0]])
                    if kh == 0 and ti == 3:
                        wdma2(kloc, 1)
                pr.step("vector", lambda e, qq=qq, xd=xd: e.scalar_tensor_tensor(
                            sb[xd][:], pA1[:, 0:1024], sb["sm"][:, 60 + qq:61 + qq], sb["gbt"][:],
                            OP.mult, OP.mult),
                        reads=[b_pA[0], B("alphd"), b_misc["gbt"]], writes=bxd + VP_AL)
                GDMA(yout[128 * qt:128 * (qt + 1), :], sb[xd][:], f"gyout{(4 * kh + ti) % 2}",
                     reads=bxd + [b_yout[qt]], writes=[b_yout[qt]])

    global _LASTPR
    _LASTPR = pr
    import contextlib
    with contextlib.ExitStack() as stack:
        sems = {}
        for e2 in ("tensor", "vector", "scalar", "gpsimd"):
            sems[e2] = stack.enter_context(nc.semaphore("sem_" + e2))
        for c2 in pr.chains:
            sems["c:" + c2] = stack.enter_context(nc.semaphore("semc_" + c2))
        block = stack.enter_context(nc.Block())

        def replay(e, engname, entries=None):
            if entries is None:
                entries = [(engname, ent) for ent in pr.steps[engname]]
            for qname, entry in entries:
                waits, fns = entry[0], entry[1]
                for k, v in waits:
                    e.wait_ge(sems[k], v)
                last = None
                for fn in fns:
                    last = fn(e)
                if qname in ("qsync", "qgp"):
                    last.then_inc(sems["c:" + entry[2]], 16)
                else:
                    last.then_inc(sems[qname], 1)

        qsync_chains = sorted({e3[2] for e3 in pr.steps["qsync"]})
        qgp_chains = sorted({e3[2] for e3 in pr.steps["qgp"]})

        def sync_sec(e):
            replay(e, "qsync")
            for c2 in qsync_chains:
                e.wait_ge(sems["c:" + c2], 16 * pr.chains[c2]["total"])

        def gp_sec(e):
            merged = sorted(
                [("gpsimd", ent) for ent in pr.steps["gpsimd"]]
                + [("qgp", ent) for ent in pr.steps["qgp"]],
                key=lambda it: it[1][3])
            replay(e, "gpsimd", merged)
            for c2 in qgp_chains:
                e.wait_ge(sems["c:" + c2], 16 * pr.chains[c2]["total"])

        block.sync(sync_sec)
        block.tensor(lambda e: replay(e, "tensor"))
        block.vector(lambda e: replay(e, "vector"))
        block.scalar(lambda e: replay(e, "scalar"))
        block.gpsimd(gp_sec)
    return nc


_LASTPR = None
_CACHE = {}


def kernel(**inputs):
    import ml_dtypes
    if "nc" not in _CACHE:
        _CACHE["nc"] = build_nc()
    nc = _CACHE["nc"]
    bf = ml_dtypes.bfloat16
    ident_b = np.eye(128, dtype=bf)
    lmask = (np.tril(np.ones((128, 128), np.float32), -1) * NEGBIG).astype(bf)
    umask = (np.triu(np.ones((128, 128), np.float32), 1) * NEGBIG).astype(bf)
    base = {
        "Wq": inputs["L0_attn_Wq"], "bq": inputs["L0_attn_bq"],
        "Wk": inputs["L0_attn_Wk"], "bk": inputs["L0_attn_bk"],
        "Wv": inputs["L0_attn_Wv"], "bv": inputs["L0_attn_bv"],
        "Wo": inputs["L0_attn_Wo"], "bo": inputs["L0_attn_bo"],
        "lng": inputs["L0_attn_ln_g"], "lnb": inputs["L0_attn_ln_b"],
        "Wout": inputs["L0_attn_Wout"], "bout": inputs["L0_attn_bout"].reshape(1, -1),
        "ln2g": inputs["L0_attn_ln2_g"].reshape(1, -1), "ln2b": inputs["L0_attn_ln2_b"].reshape(1, -1),
        "wg": inputs["L0_ffn_w_gate"], "wu": inputs["L0_ffn_w_up"], "wd": inputs["L0_ffn_w_down"],
        "wmod": inputs["L0_wmod"],
        "d_ident_b": ident_b, "d_lmask": lmask, "d_umask": umask,
    }
    base = {k: np.ascontiguousarray(v) for k, v in base.items()}
    in_maps = []
    for b in range(8):
        m = dict(base)
        m["x"] = np.ascontiguousarray(inputs["L0_attn_x"][b])
        m["c"] = np.ascontiguousarray(inputs["L0_c"][b:b + 1])
        in_maps.append(m)
    res = run_bass_kernel_spmd(nc, in_maps, core_ids=list(range(8)))
    return np.stack([r["y"] for r in res.results], axis=0)
